# version 12
# baseline (speedup 1.0000x reference)
import numpy as np
import ml_dtypes

import concourse.bass as bass
import concourse.mybir as mybir
from concourse.bass_utils import run_bass_kernel_spmd

F32 = mybir.dt.float32
BF16 = mybir.dt.bfloat16
AF = mybir.ActivationFunctionType
ALU = mybir.AluOpType
AXL = mybir.AxisListType
BF = ml_dtypes.bfloat16

L = 2048
D = 1024
NH = 16
E = 64
I_FFN = 2816
C = 128
NT = L // C
ND = D // 128
NPAIR = NH // 2
NTILE_W = [512, 512, 512, 512, 512, 256]

_CACHE = {}


def _gammas():
    return np.array([1.0 - 2.0 ** (-5 - (h // 2)) for h in range(NH)], dtype=np.float64)


def host_consts():
    g = _gammas()
    t_ = np.arange(C)
    u_ = np.arange(C)
    tri8 = (np.where(t_[None, :] - u_[:, None] >= 0, 1.0, 0.0) / 8.0).astype(np.float32)
    gam16 = g.astype(np.float32).reshape(16, 1)
    gam8 = np.zeros((128, NPAIR), dtype=np.float32)
    for hp in range(NPAIR):
        gam8[0:64, hp] = np.float32(g[2 * hp] / 8.0)
        gam8[64:128, hp] = np.float32(g[2 * hp + 1] / 8.0)
    selg = np.zeros((16, NPAIR * 128), dtype=np.float32)
    for hp in range(NPAIR):
        selg[2 * hp, hp * 128: hp * 128 + 64] = 1.0
        selg[2 * hp + 1, hp * 128 + 64: (hp + 1) * 128] = 1.0
    sel2 = np.zeros((16, 128), dtype=np.float32)
    for k in range(16):
        sel2[k, :] = ((k % 2) == (np.arange(128) >= 64)).astype(np.float32)
    m2 = np.zeros((16, NPAIR), dtype=np.float32)
    for k in range(16):
        m2[k, k // 2] = 1.0
    HH = np.zeros((128, 128), dtype=np.float32)
    HH[0:64, 0:64] = 1.0
    HH[64:128, 64:128] = 1.0
    trow = np.tile(np.arange(128, dtype=np.float32)[None, :], (16, 1))
    qthr = np.zeros((128, 8), dtype=np.float32)
    qthr[:, 0:4] = -np.array([0.25, 0.75, 1.25, 1.75], dtype=np.float32)
    return dict(ident=np.eye(128, dtype=np.float32),
                ident_bf=np.eye(128, dtype=BF),
                tri8=tri8, gam16=gam16, gam8=gam8, selg=selg, sel2=sel2,
                m2=m2, HH=HH, trow=trow, tneg=-trow,
                t127=(127.0 - trow).astype(np.float32), qthr=qthr)


ENGINES = ("tensor", "vector", "scalar", "gpsimd", "sync")


class Buf:
    __slots__ = ("name", "writer", "readers", "psum")

    def __init__(self, name, psum=False):
        self.name = name
        self.writer = None
        self.readers = []
        self.psum = psum


class Op:
    __slots__ = ("oid", "engine", "emit", "deps", "is_dma", "dma_idx", "pos",
                 "hazard", "tag", "wfree")

    def __init__(self, oid, engine, emit, deps, is_dma, hazard, tag, wfree):
        self.oid = oid
        self.engine = engine
        self.emit = emit
        self.deps = deps
        self.is_dma = is_dma
        self.dma_idx = None
        self.pos = None
        self.hazard = hazard
        self.tag = tag
        self.wfree = wfree


class Prog:
    def __init__(self):
        self.ops = []
        self.streams = {e: [] for e in ENGINES}
        self.dma_counts = {e: 0 for e in ENGINES}
        self.last_dma = {}
        self.last_barrier = {}

    def add(self, engine, emit, reads=(), writes=(), dma=False, hazard=False,
            extra_deps=(), tag="", wfree=1, _is_barrier=False):
        deps = set(extra_deps)
        if not _is_barrier and engine in self.last_barrier:
            deps.add(self.last_barrier[engine])
        reads = list(reads)
        writes = list(writes)
        reads2 = []
        for b in reads:
            (writes if b.psum else reads2).append(b)
        reads = reads2
        for b in reads:
            if b.writer is not None:
                deps.add(b.writer)
        for b in writes:
            if b.writer is not None:
                deps.add(b.writer)
            deps.update(b.readers)
        oid = len(self.ops)
        op = Op(oid, engine, emit, deps, dma, hazard, tag, wfree)
        self.ops.append(op)
        op.pos = len(self.streams[engine])
        self.streams[engine].append(op)
        if dma:
            op.dma_idx = self.dma_counts[engine]
            self.dma_counts[engine] += 1
            self.last_dma[engine] = oid
        for b in reads:
            b.readers.append(oid)
        for b in writes:
            b.writer = oid
            b.readers = []
        return oid

    def barrier(self):
        deps = set()
        for e in ENGINES:
            if self.streams[e]:
                deps.add(self.streams[e][-1].oid)
        deps.update(self.last_dma.values())
        new_bar = {}
        for e in ENGINES:
            new_bar[e] = self.add(e, None, extra_deps=deps, tag="barrier",
                                  _is_barrier=True)
        self.last_barrier = new_bar

    def lower(self, nc, block):
        dma_dep = set()
        for o in self.ops:
            if o.is_dma:
                dma_dep.update(o.deps)
        wait_targets = {e: set() for e in ENGINES}
        for o in self.ops:
            for did in o.deps:
                d = self.ops[did]
                if d.is_dma:
                    continue
                if d.engine != o.engine:
                    wait_targets[d.engine].add(d.pos)
        inc_rank = {}
        for e in ENGINES:
            r = {}
            for i, pp in enumerate(sorted(wait_targets[e])):
                r[pp] = i + 1
            inc_rank[e] = r

        eng_sems = {}
        dma_sems = {}
        ctx = []
        for e in ENGINES:
            s = nc.semaphore(f"es_{e}")
            eng_sems[e] = s.__enter__()
            ctx.append(s)
            s2 = nc.semaphore(f"ds_{e}")
            dma_sems[e] = s2.__enter__()
            ctx.append(s2)

        def run_engine(e, eng):
            prev = None
            for o in self.streams[e]:
                if (prev is not None and not prev.is_dma and not o.is_dma
                        and prev.wfree < 1000000 and not prev.hazard
                        and prev.oid in o.deps and o.emit is not None
                        and prev.emit is not None):
                    if e in ("vector", "gpsimd"):
                        eng.engine_nop()
                    else:
                        eng.nop()
                prev = o
                waits = {}
                for did in sorted(o.deps):
                    d = self.ops[did]
                    if d.is_dma:
                        key = ("dma", d.engine)
                        val = 16 * (d.dma_idx + 1)
                    elif d.engine == e:
                        continue
                    else:
                        key = ("eng", d.engine)
                        val = inc_rank[d.engine][d.pos]
                    if waits.get(key, 0) < val:
                        waits[key] = val
                for (kind, se), val in sorted(waits.items()):
                    sem = dma_sems[se] if kind == "dma" else eng_sems[se]
                    eng.wait_ge(sem, val)
                if o.emit is None:
                    inst = eng.drain()
                    if o.pos in inc_rank[e]:
                        inst.then_inc(eng_sems[e], 1)
                    continue
                inst = o.emit(eng)
                if o.is_dma:
                    inst.then_inc(dma_sems[e], 16)
                elif o.hazard or (o.oid in dma_dep) or (o.pos in inc_rank[e]):
                    dr = eng.drain()
                    if o.pos in inc_rank[e]:
                        dr.then_inc(eng_sems[e], 1)

        for e in ENGINES:
            getattr(block, e)(lambda eng, e=e: run_engine(e, eng))
        return ctx


def emit_quant_multi(p, items, thr, tag=""):
    def blk(a):
        return a.rearrange("p (b k) -> p b k", k=16)

    for it in items:
        nb = it["w"] // 16
        p.add("vector", lambda v, it=it, nb=nb: v.tensor_reduce(
            out=it["scr"]["bm"][1][:, 0:nb], in_=blk(it["src_ap"]()),
            axis=AXL.X, op=ALU.max, apply_absolute_value=True),
            reads=[it["src_buf"]], writes=[it["scr"]["bm"][0]],
            hazard=True, tag=tag + "bm", wfree=64)
    for it in items:
        nb = it["w"] // 16
        p.add("vector", lambda v, it=it, nb=nb: v.tensor_scalar(
            out=it["scr"]["s6"][1][:, 0:nb], in0=it["scr"]["bm"][1][:, 0:nb],
            scalar1=float(1.0 / 6.0), scalar2=1e-12, op0=ALU.mult, op1=ALU.max),
            reads=[it["scr"]["bm"][0]], writes=[it["scr"]["s6"][0]],
            tag=tag + "s6", wfree=64)
    for it in items:
        p.add("scalar", lambda s, it=it: s.activation(
            out=it["scr"]["ab"][1][:, 0:it["w"]], in_=it["src_ap"](),
            func=AF.Abs), reads=[it["src_buf"]], writes=[it["scr"]["ab"][0]],
            tag=tag + "ab")
    for it in items:
        nb = it["w"] // 16
        p.add("vector", lambda v, it=it, nb=nb: v.reciprocal(
            out=it["scr"]["si"][1][:, 0:nb], in_=it["scr"]["s6"][1][:, 0:nb]),
            reads=[it["scr"]["s6"][0]], writes=[it["scr"]["si"][0]],
            hazard=True, tag=tag + "si")
    for it in items:
        p.add("scalar", lambda s, it=it: s.activation(
            out=it["scr"]["sg"][1][:, 0:it["w"]], in_=it["src_ap"](),
            func=AF.Sign), reads=[it["src_buf"]], writes=[it["scr"]["sg"][0]],
            tag=tag + "sg")
    for it in items:
        nb = it["w"] // 16
        p.add("vector", lambda v, it=it, nb=nb: v.tensor_tensor(
            out=blk(it["scr"]["z"][1].ap()[:, 0:it["w"]]),
            in0=blk(it["scr"]["ab"][1].ap()[:, 0:it["w"]]),
            in1=it["scr"]["si"][1].ap()[:, 0:nb].unsqueeze(2).broadcast_to([128, nb, 16]),
            op=ALU.mult),
            reads=[it["scr"]["ab"][0], it["scr"]["si"][0]],
            writes=[it["scr"]["z"][0]], tag=tag + "z", wfree=it["w"])
    for k in range(4):
        for it in items:
            p.add("scalar", lambda s, it=it, k=k: s.activation(
                out=it["scr"][f"s{k + 1}"][1][:, 0:it["w"]],
                in_=it["scr"]["z"][1][:, 0:it["w"]], func=AF.Sign,
                bias=thr[1][:, k:k + 1], scale=1.0),
                reads=[it["scr"]["z"][0], thr[0]],
                writes=[it["scr"][f"s{k + 1}"][0]], tag=tag + f"lo{k}")
    for nm, mid in (("iA", 2.5), ("iB", 3.5), ("iC", 5.0)):
        for it in items:
            p.add("vector", lambda v, it=it, nm=nm, mid=mid: v.tensor_scalar(
                out=it["scr"][nm][1][:, 0:it["w"]],
                in0=it["scr"]["z"][1][:, 0:it["w"]], scalar1=mid,
                scalar2=None, op0=ALU.is_gt),
                reads=[it["scr"]["z"][0]], writes=[it["scr"][nm][0]],
                tag=tag + nm, wfree=it["w"])
    for other in ("s2", "s3", "s4"):
        for it in items:
            p.add("vector", lambda v, it=it, other=other: v.tensor_tensor(
                out=it["scr"]["s1"][1][:, 0:it["w"]],
                in0=it["scr"]["s1"][1][:, 0:it["w"]],
                in1=it["scr"][other][1][:, 0:it["w"]], op=ALU.add),
                reads=[it["scr"]["s1"][0], it["scr"][other][0]],
                writes=[it["scr"]["s1"][0]], tag=tag + "A" + other,
                wfree=it["w"])
    for it in items:
        p.add("vector", lambda v, it=it: v.tensor_tensor(
            out=it["scr"]["iA"][1][:, 0:it["w"]],
            in0=it["scr"]["iA"][1][:, 0:it["w"]],
            in1=it["scr"]["iB"][1][:, 0:it["w"]], op=ALU.add),
            reads=[it["scr"]["iA"][0], it["scr"]["iB"][0]],
            writes=[it["scr"]["iA"][0]], tag=tag + "dAB", wfree=it["w"])
    for it in items:
        p.add("vector", lambda v, it=it: v.scalar_tensor_tensor(
            out=it["scr"]["iA"][1][:, 0:it["w"]],
            in0=it["scr"]["iC"][1][:, 0:it["w"]], scalar=2.0,
            in1=it["scr"]["iA"][1][:, 0:it["w"]], op0=ALU.mult, op1=ALU.add),
            reads=[it["scr"]["iC"][0], it["scr"]["iA"][0]],
            writes=[it["scr"]["iA"][0]], tag=tag + "dC", wfree=it["w"])
    for it in items:
        p.add("vector", lambda v, it=it: v.scalar_tensor_tensor(
            out=it["scr"]["s2"][1][:, 0:it["w"]],
            in0=it["scr"]["s1"][1][:, 0:it["w"]], scalar=0.25,
            in1=it["scr"]["iA"][1][:, 0:it["w"]], op0=ALU.mult, op1=ALU.add),
            reads=[it["scr"]["s1"][0], it["scr"]["iA"][0]],
            writes=[it["scr"]["s2"][0]], tag=tag + "u", wfree=it["w"])
    for it in items:
        p.add("vector", lambda v, it=it: v.scalar_tensor_tensor(
            out=it["scr"]["s3"][1][:, 0:it["w"]],
            in0=it["scr"]["s2"][1][:, 0:it["w"]], scalar=1.0,
            in1=it["scr"]["sg"][1][:, 0:it["w"]], op0=ALU.add, op1=ALU.mult),
            reads=[it["scr"]["s2"][0], it["scr"]["sg"][0]],
            writes=[it["scr"]["s3"][0]], tag=tag + "tn", wfree=it["w"])
    for it in items:
        nb = it["w"] // 16
        p.add("vector", lambda v, it=it, nb=nb: v.tensor_tensor(
            out=blk(it["dst_ap"]()),
            in0=blk(it["scr"]["s3"][1].ap()[:, 0:it["w"]]),
            in1=it["scr"]["s6"][1].ap()[:, 0:nb].unsqueeze(2).broadcast_to([128, nb, 16]),
            op=ALU.mult),
            reads=[it["scr"]["s3"][0], it["scr"]["s6"][0]],
            writes=[it["dst_buf"]], tag=tag + "y", wfree=it["w"])


def emit_quant(p, src_buf, src_ap, w, scr, dst_buf, dst_ap, tag="", thr=None):
    emit_quant_multi(p, [dict(src_buf=src_buf, src_ap=src_ap, w=w, scr=scr,
                              dst_buf=dst_buf, dst_ap=dst_ap)], thr, tag=tag)


def quant_scratch(alloc, prefix, w):
    scr = {}
    scr["ab"] = alloc(prefix + "ab", [128, w], F32)
    scr["z"] = alloc(prefix + "z", [128, w], F32)
    for nm in ("sg", "s1", "s2", "s3", "s4", "iA", "iB", "iC"):
        scr[nm] = alloc(prefix + nm, [128, w], BF16)
    for nm in ("bm", "s6", "si"):
        scr[nm] = alloc(prefix + nm, [128, w // 16], F32)
    return scr


def emit_rstd_newton(p, st, src_var, dst_rstd, t1, t2):
    p.add("scalar", lambda s: s.activation(
        out=t1[1][:, :], in_=src_var[1][:, :], func=AF.Sqrt),
        reads=[src_var[0]], writes=[t1[0]])
    p.add("vector", lambda v: v.reciprocal(
        out=dst_rstd[1][:, :], in_=t1[1][:, :]),
        reads=[t1[0]], writes=[dst_rstd[0]], hazard=True)
    p.add("vector", lambda v: v.tensor_tensor(
        out=t1[1][:, :], in0=dst_rstd[1][:, :], in1=dst_rstd[1][:, :],
        op=ALU.mult), reads=[dst_rstd[0]], writes=[t1[0]])
    p.add("vector", lambda v: v.tensor_tensor(
        out=t2[1][:, :], in0=t1[1][:, :], in1=src_var[1][:, :], op=ALU.mult),
        reads=[t1[0], src_var[0]], writes=[t2[0]])
    p.add("vector", lambda v: v.tensor_scalar(
        out=t2[1][:, :], in0=t2[1][:, :], scalar1=-0.5, scalar2=1.5,
        op0=ALU.mult, op1=ALU.add), reads=[t2[0]], writes=[t2[0]])
    p.add("vector", lambda v: v.tensor_tensor(
        out=dst_rstd[1][:, :], in0=dst_rstd[1][:, :], in1=t2[1][:, :],
        op=ALU.mult), reads=[dst_rstd[0], t2[0]], writes=[dst_rstd[0]])


def build_nc(phases=(1, 2, 3, 4), debug=(), p3_ngroups=4, p3_nchunks=4, p3_qkv=True, p3_norm=True):
    nc = bass.Bass("TRN2", target_bir_lowering=False, detect_race_conditions=False)

    dx = nc.dram_tensor("x", [L, D], F32, kind="ExternalInput").ap()
    dwq = nc.dram_tensor("wq_r", [128, ND * D], F32, kind="ExternalInput").ap()
    dwk = nc.dram_tensor("wk_r", [128, ND * D], F32, kind="ExternalInput").ap()
    dwv = nc.dram_tensor("wv_r", [128, ND * D], F32, kind="ExternalInput").ap()
    dwo = nc.dram_tensor("wout_r", [128, ND * D], F32, kind="ExternalInput").ap()
    dwg = nc.dram_tensor("w_gate", [I_FFN, D], F32, kind="ExternalInput").ap()
    dwu = nc.dram_tensor("w_up", [I_FFN, D], F32, kind="ExternalInput").ap()
    dwd = nc.dram_tensor("w_down", [D, I_FFN], F32, kind="ExternalInput").ap()
    dgnw = nc.dram_tensor("gnw_r", [128, ND], F32, kind="ExternalInput").ap()
    dgnb = nc.dram_tensor("gnb_r", [128, ND], F32, kind="ExternalInput").ap()
    dident = nc.dram_tensor("ident", [128, 128], F32, kind="ExternalInput").ap()
    didentb = nc.dram_tensor("ident_bf", [128, 128], BF16, kind="ExternalInput").ap()
    dtri8 = nc.dram_tensor("tri8", [128, 128], F32, kind="ExternalInput").ap()
    dgam16 = nc.dram_tensor("gam16", [16, 1], F32, kind="ExternalInput").ap()
    dgam8 = nc.dram_tensor("gam8", [128, NPAIR], F32, kind="ExternalInput").ap()
    dselg = nc.dram_tensor("selg", [16, NPAIR * 128], F32, kind="ExternalInput").ap()
    dsel2 = nc.dram_tensor("sel2", [16, 128], F32, kind="ExternalInput").ap()
    dm2 = nc.dram_tensor("m2", [16, NPAIR], F32, kind="ExternalInput").ap()
    dtrow = nc.dram_tensor("trow", [16, 128], F32, kind="ExternalInput").ap()
    dtneg = nc.dram_tensor("tneg", [16, 128], F32, kind="ExternalInput").ap()
    dt127 = nc.dram_tensor("t127", [16, 128], F32, kind="ExternalInput").ap()
    dhh = nc.dram_tensor("HH", [128, 128], F32, kind="ExternalInput").ap()
    dqthr = nc.dram_tensor("qthr", [128, 8], F32, kind="ExternalInput").ap()
    dy = nc.dram_tensor("y", [L, D], F32, kind="ExternalOutput").ap()
    dxnT = nc.dram_tensor("xnT_sp", [128, ND * L], BF16, kind="Internal").ap()
    dgres = nc.dram_tensor("gres_sp", [128, NT * D], BF16, kind="Internal").ap()
    ddbg = {}
    if "xqT" in debug:
        ddbg["xqT"] = nc.dram_tensor("dbg_xqT", [128, ND * L], BF16, kind="ExternalOutput").ap()
    if "xnT" in debug:
        ddbg["xnT"] = nc.dram_tensor("dbg_xnT", [128, ND * L], BF16, kind="ExternalOutput").ap()
    if "acc" in debug:
        ddbg["acc"] = nc.dram_tensor("dbg_acc", [128, NT * D], F32, kind="ExternalOutput").ap()
    if "aout" in debug:
        ddbg["aout"] = nc.dram_tensor("dbg_aout", [128, ND * L], BF16, kind="ExternalOutput").ap()

    p = Prog()
    perm_ctx = []
    late_ctx = []
    p12_ctx = []
    phase_ctx = []

    def alloc(name, shape, dtype, perm=False, late=False, p12=False):
        g = nc.sbuf_tensor(name, shape, dtype)
        t = g.__enter__()
        (perm_ctx if perm else (late_ctx if late else
         (p12_ctx if p12 else phase_ctx))).append(g)
        return (Buf(name), t)

    def palloc(name, shape, dtype):
        g = nc.psum_tensor(name, shape, dtype)
        t = g.__enter__()
        perm_ctx.append(g)
        return (Buf(name, psum=True), t)

    def free_phase():
        nonlocal phase_ctx
        for g in reversed(phase_ctx):
            g.__exit__(None, None, None)
        phase_ctx = []

    def dma(engine, out_ap, in_ap, reads=(), writes=(), tag=""):
        def f(eng, out_ap=out_ap, in_ap=in_ap):
            return eng.dma_start(out=out_ap, in_=in_ap)
        return p.add(engine, f, reads=reads, writes=writes, dma=True, tag=tag)

    ident = alloc("identS", [128, 128], F32, perm=True)
    identb = alloc("identbS", [128, 128], BF16, perm=True)
    gC = alloc("gCS", [128, NPAIR], F32, perm=True)
    HHs = alloc("HHS", [128, 128], F32, perm=True)
    gnw = alloc("gnwS", [128, ND], F32, perm=True)
    gnb = alloc("gnbS", [128, ND], F32, perm=True)
    qthr = alloc("qthrS", [128, 8], F32, perm=True)
    xqT = alloc("xqT", [128, ND * L], BF16, p12=True)

    pbank = [palloc(f"pb{i}", [128, 512], F32) for i in range(6)]
    pbbf = [palloc(f"pbb{i}", [128, 1024], BF16) for i in range(2)]

    dma("sync", ident[1][:, :], dident, writes=[ident[0]], tag="c0")
    dma("sync", identb[1][:, :], didentb, writes=[identb[0]], tag="c1")
    dma("sync", HHs[1][:, :], dhh, writes=[HHs[0]], tag="c6")
    dma("sync", gnw[1][:, :], dgnw, writes=[gnw[0]], tag="c7")
    dma("sync", gnb[1][:, :], dgnb, writes=[gnb[0]], tag="c8")
    dma("sync", qthr[1][:, :], dqthr, writes=[qthr[0]], tag="c9")

    xfull = alloc("xfull", [128, NT * D], F32)
    sums16 = alloc("sums16", [128, NT], F32)
    sqs16 = alloc("sqs16", [128, NT], F32)
    mean16 = alloc("mean16", [128, NT], F32)
    msq16 = alloc("msq16", [128, NT], F32)
    var16 = alloc("var16", [128, NT], F32)
    rstd16 = alloc("rstd16", [128, NT], F32)
    negm16 = alloc("negm16", [128, NT], F32)
    nt1 = alloc("nt1", [128, NT], F32)
    nt2 = alloc("nt2", [128, NT], F32)
    sqscrA = alloc("sqscrA", [128, D], F32)
    ph1 = []
    for par in range(2):
        b = {}
        b["xn"] = alloc(f"xn{par}", [128, D], F32)
        b["xnb"] = alloc(f"xnb{par}", [128, D], BF16)
        b["xq"] = alloc(f"p1xq{par}", [128, D], BF16)
        b["xntS"] = alloc(f"p1xnt{par}", [128, D], BF16)
        b.update(quant_scratch(alloc, f"p1s{par}", D))
        ph1.append(b)

    dma("sync", xfull[1].ap().rearrange("p (i d) -> p i d", d=D),
        dx.rearrange("(i p) d -> p i d", p=128), writes=[xfull[0]], tag="p1x")
    for i in range(NT):
        p.add("vector", lambda v, i=i: v.tensor_reduce(
            out=sums16[1][:, i:i + 1], in_=xfull[1][:, i * D:(i + 1) * D],
            axis=AXL.X, op=ALU.add),
            reads=[xfull[0]], writes=[sums16[0]], hazard=(i == NT - 1))
        p.add("scalar", lambda s, i=i: s.activation(
            out=sqscrA[1][:, :], in_=xfull[1][:, i * D:(i + 1) * D],
            func=AF.Square, accum_out=sqs16[1][:, i:i + 1]),
            reads=[xfull[0]], writes=[sqscrA[0], sqs16[0]],
            hazard=(i == NT - 1))
    p.add("vector", lambda v: v.tensor_scalar(
        out=mean16[1][:, :], in0=sums16[1][:, :], scalar1=float(1.0 / D),
        scalar2=None, op0=ALU.mult), reads=[sums16[0]], writes=[mean16[0]])
    p.add("vector", lambda v: v.tensor_tensor(
        out=msq16[1][:, :], in0=mean16[1][:, :], in1=mean16[1][:, :],
        op=ALU.mult), reads=[mean16[0]], writes=[msq16[0]])
    p.add("vector", lambda v: v.scalar_tensor_tensor(
        out=var16[1][:, :], in0=sqs16[1][:, :], scalar=float(1.0 / D),
        in1=msq16[1][:, :], op0=ALU.mult, op1=ALU.subtract),
        reads=[sqs16[0], msq16[0]], writes=[var16[0]])
    p.add("vector", lambda v: v.tensor_scalar(
        out=var16[1][:, :], in0=var16[1][:, :], scalar1=1e-5,
        scalar2=None, op0=ALU.add), reads=[var16[0]], writes=[var16[0]])
    emit_rstd_newton(p, None, var16, rstd16, nt1, nt2)
    p.add("vector", lambda v: v.tensor_tensor(
        out=negm16[1][:, :], in0=mean16[1][:, :], in1=rstd16[1][:, :],
        op=ALU.mult), reads=[mean16[0], rstd16[0]], writes=[negm16[0]])
    p.add("vector", lambda v: v.tensor_scalar(
        out=negm16[1][:, :], in0=negm16[1][:, :], scalar1=-1.0,
        scalar2=None, op0=ALU.mult), reads=[negm16[0]], writes=[negm16[0]])

    for i0 in range(0, NT, 2):
        items = []
        for i in (i0, i0 + 1):
            b = ph1[i % 2]
            p.add("scalar", lambda s, b=b, i=i: s.activation(
                out=b["xn"][1][:, :], in_=xfull[1][:, i * D:(i + 1) * D],
                func=AF.Identity, scale=rstd16[1][:, i:i + 1],
                bias=negm16[1][:, i:i + 1]),
                reads=[xfull[0], rstd16[0], negm16[0]], writes=[b["xn"][0]])
            items.append(dict(
                src_buf=b["xn"][0], src_ap=(lambda b=b: b["xn"][1].ap()[:, :]),
                w=D, scr=b, dst_buf=b["xq"][0],
                dst_ap=(lambda b=b: b["xq"][1].ap()[:, :])))
        emit_quant_multi(p, items, qthr, tag=f"p1q{i0}_")
        for i in (i0, i0 + 1):
            b = ph1[i % 2]
            p.add("vector", lambda v, b=b: v.tensor_copy(
                out=b["xnb"][1][:, :], in_=b["xn"][1][:, :]),
                reads=[b["xn"][0]], writes=[b["xnb"][0]], wfree=D)
            pbq = pbbf[i % 2]
            for dc in range(ND):
                p.add("tensor", lambda t, b=b, dc=dc, pbq=pbq: t.transpose(
                    pbq[1][:, dc * 128:(dc + 1) * 128],
                    b["xq"][1][:, dc * 128:(dc + 1) * 128], identb[1][:, :]),
                    reads=[b["xq"][0], identb[0]], writes=[pbq[0]])
            p.add("scalar", lambda s, i=i, pbq=pbq: s.activation(
                out=xqT[1].ap().rearrange("p (dc t) -> p dc t", t=L)[:, :, i * 128:(i + 1) * 128],
                in_=pbq[1].ap().rearrange("p (dc t) -> p dc t", t=128),
                func=AF.Copy), reads=[pbq[0]], writes=[xqT[0]])
            for dc in range(ND):
                p.add("tensor", lambda t, b=b, dc=dc, pbq=pbq: t.transpose(
                    pbq[1][:, dc * 128:(dc + 1) * 128],
                    b["xnb"][1][:, dc * 128:(dc + 1) * 128], identb[1][:, :]),
                    reads=[b["xnb"][0], identb[0]], writes=[pbq[0]])
            p.add("scalar", lambda s, b=b, pbq=pbq: s.activation(
                out=b["xntS"][1][:, :], in_=pbq[1][:, :], func=AF.Copy),
                reads=[pbq[0]], writes=[b["xntS"][0]])
            dma("gpsimd",
                dxnT.rearrange("p (dc i t) -> p dc i t", dc=ND, i=NT)[:, :, i, :],
                b["xntS"][1].ap().rearrange("p (dc t) -> p dc t", t=128),
                reads=[b["xntS"][0]], tag=f"p1sp{i}")

    p.barrier()
    if "xqT" in ddbg:
        dma("sync", ddbg["xqT"], xqT[1][:, :], reads=[xqT[0]], tag="dbg_xqT")
    if "xnT" in ddbg:
        dma("sync", ddbg["xnT"], dxnT, tag="dbg_xnT")
    free_phase()

    run2 = 2 in phases
    acc = alloc("accF", [128, NT * D], F32)
    wstg2 = [alloc(f"wstg{k}", [128, D], F32) for k in range(2)]
    wq_nat = alloc("wq_nat", [128, 12 * D], BF16)
    wgqT = alloc("wgqT", [128, ND * 512], BF16)
    wuqT = alloc("wuqT", [128, ND * 512], BF16)
    wdqT = alloc("wdqT", [128, 4 * D], BF16)
    f2 = [quant_scratch(alloc, f"f2{k}", 256) for k in range(2)]
    g2 = [quant_scratch(alloc, f"g2{k}", 512) for k in range(2)]
    hb = []
    for par in range(2):
        hh = {}
        hh["hs"] = alloc(f"hs{par}", [128, 512], F32)
        hh["h"] = alloc(f"h{par}", [128, 512], F32)
        hh["hq"] = alloc(f"hq{par}", [128, 512], BF16)
        hh["hqT"] = alloc(f"hqT{par}", [128, 512], BF16)
        hb.append(hh)
    gstg = [alloc(f"gstg{par}", [128, D], BF16) for par in range(2)]

    NT_OFF = [0]
    for W in NTILE_W:
        NT_OFF.append(NT_OFF[-1] + W)

    def prep_tile(j):
        W = NTILE_W[j]
        CW = W // 128
        n0 = NT_OFF[j]
        slot = 0
        for kind in range(3):
            for sl in range(CW):
                stg = wstg2[slot % 2]
                if kind < 2:
                    dsrc = dwg if kind == 0 else dwu
                    dma("gpsimd", stg[1][:, :],
                        dsrc.rearrange("(a p) k -> p a k", p=128)[:, (n0 // 128) + sl:(n0 // 128) + sl + 1, :].rearrange("p a k -> p (a k)"),
                        writes=[stg[0]], tag=f"w{kind}_{j}_{sl}")
                else:
                    ndc = D // W
                    dma("gpsimd",
                        stg[1].ap().rearrange("p (dc n) -> p dc n", n=W),
                        dwd.rearrange("(dc p) n -> p dc n", p=128)[:, sl * ndc:(sl + 1) * ndc, n0:n0 + W],
                        writes=[stg[0]], tag=f"wd_{j}_{sl}")
                so = slot
                items = []
                for hf in range(2):
                    items.append(dict(
                        src_buf=stg[0],
                        src_ap=(lambda stg=stg, hf=hf: stg[1].ap()[:, hf * 512:(hf + 1) * 512]),
                        w=512, scr=g2[hf], dst_buf=wq_nat[0],
                        dst_ap=(lambda so=so, hf=hf: wq_nat[1].ap()[:, so * D + hf * 512: so * D + (hf + 1) * 512])))
                emit_quant_multi(p, items, qthr, tag=f"wq{j}_{slot}_")
                slot += 1

    def trans_tile(j):
        W = NTILE_W[j]
        CW = W // 128
        for kind, dstT in ((0, wgqT), (1, wuqT)):
            for nc4 in range(CW):
                for kc in range(ND):
                    p.add("tensor", lambda t, kind=kind, nc4=nc4, kc=kc, CW=CW: t.transpose(
                        pbbf[0][1][:, kc * 128:(kc + 1) * 128],
                        wq_nat[1][:, (kind * CW + nc4) * D + kc * 128: (kind * CW + nc4) * D + (kc + 1) * 128],
                        identb[1][:, :]), reads=[wq_nat[0], identb[0]],
                        writes=[pbbf[0][0]])
                p.add("scalar", lambda s, dstT=dstT, nc4=nc4: s.activation(
                    out=dstT[1].ap().rearrange("p (kc n) -> p kc n", n=512)[:, :, nc4 * 128:(nc4 + 1) * 128],
                    in_=pbbf[0][1].ap().rearrange("p (kc t) -> p kc t", t=128),
                    func=AF.Copy), reads=[pbbf[0][0]], writes=[dstT[0]])
        dbase = 2 * CW * D
        for nc4 in range(CW):
            for dc in range(ND):
                p.add("tensor", lambda t, nc4=nc4, dc=dc, W=W, dbase=dbase: t.transpose(
                    pbbf[0][1][:, dc * 128:(dc + 1) * 128],
                    wq_nat[1][:, dbase + dc * W + nc4 * 128: dbase + dc * W + (nc4 + 1) * 128],
                    identb[1][:, :]), reads=[wq_nat[0], identb[0]],
                    writes=[pbbf[0][0]])
            p.add("scalar", lambda s, nc4=nc4: s.activation(
                out=wdqT[1][:, nc4 * D:(nc4 + 1) * D], in_=pbbf[0][1][:, :],
                func=AF.Copy), reads=[pbbf[0][0]], writes=[wdqT[0]])

    def t_loop(j):
        W = NTILE_W[j]
        CW = W // 128

        def emit_gateup(i):
            bg, bu = pbank[(i % 2) * 2], pbank[(i % 2) * 2 + 1]
            for kc in range(ND):
                p.add("tensor", lambda t, kc=kc, i=i, W=W, bg=bg: t.matmul(
                    bg[1][:, 0:W],
                    xqT[1][:, kc * L + i * 128: kc * L + (i + 1) * 128],
                    wgqT[1][:, kc * 512: kc * 512 + W],
                    start=(kc == 0), stop=(kc == ND - 1)),
                    reads=[xqT[0], wgqT[0]], writes=[bg[0]])
            for kc in range(ND):
                p.add("tensor", lambda t, kc=kc, i=i, W=W, bu=bu: t.matmul(
                    bu[1][:, 0:W],
                    xqT[1][:, kc * L + i * 128: kc * L + (i + 1) * 128],
                    wuqT[1][:, kc * 512: kc * 512 + W],
                    start=(kc == 0), stop=(kc == ND - 1)),
                    reads=[xqT[0], wuqT[0]], writes=[bu[0]])

        emit_gateup(0)
        for i in range(NT):
            hh = hb[i % 2]
            bg, bu = pbank[(i % 2) * 2], pbank[(i % 2) * 2 + 1]
            bd1, bd2 = pbank[4], pbank[5]
            p.add("scalar", lambda s, hh=hh, W=W, bg=bg: s.activation(
                out=hh["hs"][1][:, 0:W], in_=bg[1][:, 0:W], func=AF.Silu),
                reads=[bg[0]], writes=[hh["hs"][0]])
            p.add("vector", lambda v, hh=hh, W=W, bu=bu: v.tensor_tensor(
                out=hh["h"][1][:, 0:W], in0=hh["hs"][1][:, 0:W],
                in1=bu[1][:, 0:W], op=ALU.mult),
                reads=[hh["hs"][0], bu[0]], writes=[hh["h"][0]], wfree=W)
            if i + 1 < NT:
                emit_gateup(i + 1)
            Wh = W // 2
            items = []
            for hf in range(2):
                items.append(dict(
                    src_buf=hh["h"][0],
                    src_ap=(lambda hh=hh, hf=hf, Wh=Wh: hh["h"][1].ap()[:, hf * Wh:(hf + 1) * Wh]),
                    w=Wh, scr=f2[hf], dst_buf=hh["hq"][0],
                    dst_ap=(lambda hh=hh, hf=hf, Wh=Wh: hh["hq"][1].ap()[:, hf * Wh:(hf + 1) * Wh])))
            emit_quant_multi(p, items, qthr, tag=f"hq{j}_{i}_")
            pbh = pbbf[1]
            for nc4 in range(CW):
                p.add("tensor", lambda t, hh=hh, nc4=nc4, pbh=pbh: t.transpose(
                    pbh[1][:, nc4 * 128:(nc4 + 1) * 128],
                    hh["hq"][1][:, nc4 * 128:(nc4 + 1) * 128],
                    identb[1][:, :]), reads=[hh["hq"][0], identb[0]],
                    writes=[pbh[0]])
            p.add("scalar", lambda s, hh=hh, CW=CW, pbh=pbh: s.activation(
                out=hh["hqT"][1][:, 0:CW * 128], in_=pbh[1][:, 0:CW * 128],
                func=AF.Copy), reads=[pbh[0]], writes=[hh["hqT"][0]])
            for half, bd in ((0, bd1), (1, bd2)):
                for nc4 in range(CW):
                    p.add("tensor", lambda t, hh=hh, nc4=nc4, half=half, bd=bd, CW=CW: t.matmul(
                        bd[1][:, 0:512],
                        hh["hqT"][1][:, nc4 * 128:(nc4 + 1) * 128],
                        wdqT[1][:, nc4 * D + half * 512: nc4 * D + (half + 1) * 512],
                        start=(nc4 == 0), stop=(nc4 == CW - 1)),
                        reads=[hh["hqT"][0], wdqT[0]], writes=[bd[0]])
            for half, bd in ((0, bd1), (1, bd2)):
                if j == 0:
                    p.add("scalar", lambda s, i=i, half=half, bd=bd: s.activation(
                        out=acc[1][:, i * D + half * 512: i * D + (half + 1) * 512],
                        in_=bd[1][:, 0:512], func=AF.Copy),
                        reads=[bd[0]], writes=[acc[0]])
                else:
                    p.add("vector", lambda v, i=i, half=half, bd=bd: v.tensor_tensor(
                        out=acc[1][:, i * D + half * 512: i * D + (half + 1) * 512],
                        in0=acc[1][:, i * D + half * 512: i * D + (half + 1) * 512],
                        in1=bd[1][:, 0:512], op=ALU.add),
                        reads=[acc[0], bd[0]], writes=[acc[0]], wfree=512)

    if run2:
        prep_tile(0)
        trans_tile(0)
        for j in range(len(NTILE_W)):
            if j + 1 < len(NTILE_W):
                prep_tile(j + 1)
            t_loop(j)
            if j + 1 < len(NTILE_W):
                trans_tile(j + 1)

    for i in range(NT if run2 else 0):
        gs = gstg[i % 2]
        p.add("scalar", lambda s, i=i, gs=gs: s.activation(
            out=gs[1][:, :], in_=acc[1][:, i * D:(i + 1) * D], func=AF.Silu),
            reads=[acc[0]], writes=[gs[0]])
        dma("gpsimd", dgres[:, i * D:(i + 1) * D], gs[1][:, :],
            reads=[gs[0]], tag=f"gsp{i}")

    p.barrier()
    if "acc" in ddbg and run2:
        dma("sync", ddbg["acc"], acc[1][:, :], reads=[acc[0]], tag="dbg_acc")
        p.barrier()
    free_phase()
    for g in reversed(p12_ctx):
        g.__exit__(None, None, None)
    p12_ctx = []

    run3 = 3 in phases
    aout = alloc("aout", [128, ND * L], BF16, late=True)
    xnT = alloc("xnT", [128, ND * L], BF16)
    maskT = alloc("maskTS", [128, NH * C], F32)
    Gam = alloc("GamS", [128, NPAIR * C], F32)
    tri8 = alloc("tri8S", [128, 128], F32)
    gam16 = alloc("gam16S", [16, 1], F32)
    gam8 = alloc("gam8S", [128, NPAIR], F32)
    selg = alloc("selgS", [16, NPAIR * 128], F32)
    sel2 = alloc("sel2S", [16, 128], F32)
    m2t = alloc("m2S", [16, NPAIR], F32)
    r16 = alloc("r16S", [16, 128], F32)
    ri16 = alloc("ri16S", [16, 128], F32)
    ri2 = alloc("ri2S", [16, 128], F32)
    ri2T = alloc("ri2TS", [128, 16], F32)
    sck = alloc("sckS", [16, 1], F32)
    sck2 = alloc("sck2S", [16, 1], F32)
    tb = alloc("tbS", [16, 16], F32)
    lnG = alloc("lnGS", [16, 1], F32)
    trow = alloc("trowS", [16, 128], F32)
    tneg = alloc("tnegS", [16, 128], F32)
    t127 = alloc("t127S", [16, 128], F32)
    dma("sync", tri8[1][:, :], dtri8, writes=[tri8[0]], tag="c2")
    dma("sync", gam16[1][:, :], dgam16, writes=[gam16[0]], tag="c3")
    dma("sync", gam8[1][:, :], dgam8, writes=[gam8[0]], tag="c4")
    dma("sync", selg[1][:, :], dselg, writes=[selg[0]], tag="c4b")
    dma("sync", sel2[1][:, :], dsel2, writes=[sel2[0]], tag="c4c")
    dma("sync", m2t[1][:, :], dm2, writes=[m2t[0]], tag="c4d")
    dma("sync", trow[1][:, :], dtrow, writes=[trow[0]], tag="c4e")
    dma("sync", tneg[1][:, :], dtneg, writes=[tneg[0]], tag="c4f")
    dma("sync", t127[1][:, :], dt127, writes=[t127[0]], tag="c4g")

    p.add("scalar", lambda s: s.activation(
        out=lnG[1][:, :], in_=gam16[1][:, :], func=AF.Ln),
        reads=[gam16[0]], writes=[lnG[0]])
    p.add("vector", lambda v: v.tensor_tensor(
        out=r16[1][:, :], in0=trow[1][:, :],
        in1=lnG[1].ap()[:, 0:1].broadcast_to([16, 128]), op=ALU.mult),
        reads=[trow[0], lnG[0]], writes=[r16[0]])
    p.add("scalar", lambda s: s.activation(
        out=r16[1][:, :], in_=r16[1][:, :], func=AF.Exp),
        reads=[r16[0]], writes=[r16[0]])
    p.add("vector", lambda v: v.tensor_tensor(
        out=ri16[1][:, :], in0=tneg[1][:, :],
        in1=lnG[1].ap()[:, 0:1].broadcast_to([16, 128]), op=ALU.mult),
        reads=[tneg[0], lnG[0]], writes=[ri16[0]])
    p.add("scalar", lambda s: s.activation(
        out=ri16[1][:, :], in_=ri16[1][:, :], func=AF.Exp),
        reads=[ri16[0]], writes=[ri16[0]])
    p.add("vector", lambda v: v.tensor_tensor(
        out=ri2[1][:, :], in0=t127[1][:, :],
        in1=lnG[1].ap()[:, 0:1].broadcast_to([16, 128]), op=ALU.mult),
        reads=[t127[0], lnG[0]], writes=[ri2[0]])
    p.add("scalar", lambda s: s.activation(
        out=ri2[1][:, :], in_=ri2[1][:, :], func=AF.Exp),
        reads=[ri2[0]], writes=[ri2[0]])
    p.add("tensor", lambda t: t.transpose(
        pbank[0][1][:, 0:16], ri2[1][:, :], ident[1][0:16, 0:16]),
        reads=[ri2[0], ident[0]], writes=[pbank[0][0]])
    p.add("vector", lambda v: v.tensor_copy(out=ri2T[1][:, :], in_=pbank[0][1][:, 0:16]),
          reads=[pbank[0][0]], writes=[ri2T[0]])
    rflat = alloc("rflatS", [1, NH * 128], F32)
    riflat = alloc("riflatS", [1, NH * 128], F32)
    dma("sync", rflat[1].ap().rearrange("p (k t) -> p k t", k=16),
        r16[1].ap()[:, :], reads=[r16[0]], writes=[rflat[0]], tag="rf")
    dma("sync", riflat[1].ap().rearrange("p (k t) -> p k t", k=16),
        ri16[1].ap()[:, :], reads=[ri16[0]], writes=[riflat[0]], tag="rif")
    for h in range(NH):
        p.add("tensor", lambda t, h=h: t.matmul(
            pbank[0][1][:, 0:128], riflat[1][0:1, h * 128:(h + 1) * 128],
            rflat[1][0:1, h * 128:(h + 1) * 128], start=True, stop=True),
            reads=[riflat[0], rflat[0]], writes=[pbank[0][0]])
        p.add("vector", lambda v, h=h: v.tensor_tensor(
            out=maskT[1][:, h * 128:(h + 1) * 128], in0=pbank[0][1][:, 0:128],
            in1=tri8[1][:, :], op=ALU.mult),
            reads=[pbank[0][0], tri8[0]], writes=[maskT[0]])
    for hp in range(NPAIR):
        p.add("tensor", lambda t, hp=hp: t.matmul(
            pbank[1][1][:, 0:128], selg[1][:, hp * 128:(hp + 1) * 128],
            r16[1][:, :], start=True, stop=True),
            reads=[selg[0], r16[0]], writes=[pbank[1][0]])
        p.add("vector", lambda v, hp=hp: v.tensor_scalar(
            out=Gam[1][:, hp * 128:(hp + 1) * 128], in0=pbank[1][1][:, 0:128],
            scalar1=gam8[1][:, hp:hp + 1], scalar2=None, op0=ALU.mult),
            reads=[pbank[1][0], gam8[0]], writes=[Gam[0]])
    p.add("vector", lambda v: v.tensor_scalar(
        out=sck[1][:, :], in0=lnG[1][:, :], scalar1=128.0, scalar2=None,
        op0=ALU.mult), reads=[lnG[0]], writes=[sck[0]])
    p.add("scalar", lambda s: s.activation(
        out=sck2[1][:, :], in_=sck[1][:, :], func=AF.Exp),
        reads=[sck[0]], writes=[sck2[0]])
    p.add("vector", lambda v: v.tensor_tensor(
        out=tb[1][:, 0:NPAIR], in0=m2t[1][:, :],
        in1=sck2[1].ap()[:, 0:1].broadcast_to([16, NPAIR]), op=ALU.mult),
        reads=[m2t[0], sck2[0]], writes=[tb[0]])
    p.add("tensor", lambda t: t.matmul(
        pbank[0][1][:, 0:NPAIR], sel2[1][:, :], tb[1][:, 0:NPAIR],
        start=True, stop=True), reads=[sel2[0], tb[0]], writes=[pbank[0][0]])
    p.add("vector", lambda v: v.tensor_copy(out=gC[1][:, :], in_=pbank[0][1][:, 0:NPAIR]),
          reads=[pbank[0][0]], writes=[gC[0]])

    wqS = alloc("wqS", [128, ND * D], BF16)
    wkS = alloc("wkS", [128, ND * D], BF16)
    wvS = alloc("wvS", [128, ND * D], BF16)
    dma("sync", xnT[1][:, :], dxnT, writes=[xnT[0]], tag="xnT_rl")
    dma("gpsimd", wqS[1][:, :], dwq, writes=[wqS[0]], tag="wqS")
    dma("gpsimd", wkS[1][:, :], dwk, writes=[wkS[0]], tag="wkS")
    dma("gpsimd", wvS[1][:, :], dwv, writes=[wvS[0]], tag="wvS")

    QT = alloc("QT", [128, NPAIR * 512], BF16)
    Qg = alloc("Qg", [128, NPAIR * 512], BF16)
    KT = alloc("KT", [128, NPAIR * 512], BF16)
    Kd = alloc("Kd", [128, NPAIR * 512], BF16)
    Vs = alloc("Vs", [128, 4 * D], BF16)
    SA = [alloc("SA0", [128, 256], BF16), alloc("SA1", [128, 256], BF16)]
    Sst = [alloc(f"Sst{hp}", [128, 128], F32) for hp in range(NPAIR)]
    Sbf = [alloc(f"Sbf{hp}", [128, 128], BF16) for hp in range(NPAIR)]
    csb = alloc("csb", [128, 128], F32)
    sums = alloc("sums", [128, 2 * ND], F32)
    s1t = alloc("s1t", [128, 1], F32)
    s2t = alloc("s2t", [128, 1], F32)
    sqscr = alloc("sqscr", [128, 128], F32)
    reps = alloc("reps", [128, 16], F32)
    gnst = {k: alloc("gn" + k, [128, ND], F32)
            for k in ("mean", "ex2", "var", "rstd", "t1", "t2", "scalev", "biasv")}

    p.add("vector", lambda v: v.memset(sums[1][:, :], 0.0), writes=[sums[0]])
    for hp in range(NPAIR):
        p.add("vector", lambda v, hp=hp: v.memset(Sst[hp][1][:, :], 0.0),
              writes=[Sst[hp][0]])
        p.add("vector", lambda v, hp=hp: v.memset(Sbf[hp][1][:, :], 0.0),
              writes=[Sbf[hp][0]])

    qkvlvl = 5 if p3_qkv is True else int(p3_qkv)
    for g4 in range(p3_ngroups if run3 else 0):
        for hp in range(NPAIR if qkvlvl >= 1 else 0):
            bq = pbank[0]
            for kc in range(ND):
                p.add("tensor", lambda t, hp=hp, kc=kc, g4=g4: t.matmul(
                    bq[1][:, 0:512],
                    wqS[1][:, kc * D + hp * 128: kc * D + (hp + 1) * 128],
                    xnT[1][:, kc * L + g4 * 512: kc * L + (g4 + 1) * 512],
                    start=(kc == 0), stop=(kc == ND - 1)),
                    reads=[wqS[0], xnT[0]], writes=[bq[0]])
            p.add("scalar", lambda s, hp=hp: s.activation(
                out=QT[1][:, hp * 512:(hp + 1) * 512], in_=bq[1][:, 0:512],
                func=AF.Copy), reads=[bq[0]], writes=[QT[0]])
            if qkvlvl < 2:
                continue
            p.add("vector", lambda v, hp=hp: v.tensor_tensor(
                out=Qg[1].ap().rearrange("p (a t) -> p a t", t=128)[:, 4 * hp:4 * hp + 4, :],
                in0=bq[1].ap().rearrange("p (c t) -> p c t", t=128),
                in1=Gam[1].ap()[:, hp * 128:(hp + 1) * 128].unsqueeze(1).broadcast_to([128, 4, 128]),
                op=ALU.mult), reads=[bq[0], Gam[0]], writes=[Qg[0]])
            if qkvlvl < 3:
                continue
            bk = pbank[1]
            for kc in range(ND):
                p.add("tensor", lambda t, hp=hp, kc=kc, g4=g4: t.matmul(
                    bk[1][:, 0:512],
                    wkS[1][:, kc * D + hp * 128: kc * D + (hp + 1) * 128],
                    xnT[1][:, kc * L + g4 * 512: kc * L + (g4 + 1) * 512],
                    start=(kc == 0), stop=(kc == ND - 1)),
                    reads=[wkS[0], xnT[0]], writes=[bk[0]])
            p.add("scalar", lambda s, hp=hp: s.activation(
                out=KT[1][:, hp * 512:(hp + 1) * 512], in_=bk[1][:, 0:512],
                func=AF.Copy), reads=[bk[0]], writes=[KT[0]])
            if qkvlvl < 4:
                continue
            for c4 in range(4):
                p.add("tensor", lambda t, hp=hp, c4=c4: t.transpose(
                    pbbf[0][1][:, c4 * 128:(c4 + 1) * 128],
                    KT[1][:, hp * 512 + c4 * 128: hp * 512 + (c4 + 1) * 128],
                    identb[1][:, :]), reads=[KT[0], identb[0]], writes=[pbbf[0][0]])
            p.add("vector", lambda v, hp=hp: v.tensor_tensor(
                out=Kd[1].ap().rearrange("p (a h e) -> p a h e", h=2, e=64)[:, 4 * hp:4 * hp + 4, :, :],
                in0=pbbf[0][1].ap().rearrange("p (c h e) -> p c h e", h=2, e=64)[:, 0:4, :, :],
                in1=ri2T[1].ap()[:, 2 * hp:2 * hp + 2].unsqueeze(1).unsqueeze(3).broadcast_to([128, 4, 2, 64]),
                op=ALU.mult), reads=[pbbf[0][0], ri2T[0]], writes=[Kd[0]])
        for c4 in range(4 if qkvlvl >= 5 else 0):
            for half, bv in ((0, pbank[2]), (1, pbank[3])):
                for kc in range(ND):
                    p.add("tensor", lambda t, c4=c4, kc=kc, g4=g4, half=half, bv=bv: t.matmul(
                        bv[1][:, 0:512],
                        xnT[1][:, kc * L + g4 * 512 + c4 * 128: kc * L + g4 * 512 + (c4 + 1) * 128],
                        wvS[1][:, kc * D + half * 512: kc * D + (half + 1) * 512],
                        start=(kc == 0), stop=(kc == ND - 1)),
                        reads=[xnT[0], wvS[0]], writes=[bv[0]])
            p.add("scalar", lambda s, c4=c4: s.activation(
                out=Vs[1][:, c4 * D: c4 * D + 512], in_=pbank[2][1][:, 0:512],
                func=AF.Copy), reads=[pbank[2][0]], writes=[Vs[0]])
            p.add("scalar", lambda s, c4=c4: s.activation(
                out=Vs[1][:, c4 * D + 512: (c4 + 1) * D], in_=pbank[3][1][:, 0:512],
                func=AF.Copy), reads=[pbank[3][0]], writes=[Vs[0]])

        def emit_scores(hp, c4):
            bs = pbank[4 + (hp % 2)]
            p.add("tensor", lambda t, hp=hp, c4=c4, bs=bs: t.matmul(
                bs[1][:, 0:128],
                KT[1][0:64, hp * 512 + c4 * 128: hp * 512 + (c4 + 1) * 128],
                QT[1][0:64, hp * 512 + c4 * 128: hp * 512 + (c4 + 1) * 128],
                start=True, stop=True, tile_position=(0, 0)),
                reads=[KT[0], QT[0]], writes=[bs[0]])
            p.add("tensor", lambda t, hp=hp, c4=c4, bs=bs: t.matmul(
                bs[1][:, 128:256],
                KT[1][64:128, hp * 512 + c4 * 128: hp * 512 + (c4 + 1) * 128],
                QT[1][64:128, hp * 512 + c4 * 128: hp * 512 + (c4 + 1) * 128],
                start=True, stop=True, tile_position=(64, 0)),
                reads=[KT[0], QT[0]], writes=[bs[0]])

        for c4 in range(p3_nchunks):
            c = g4 * 4 + c4
            emit_scores(0, c4)
            for hp in range(NPAIR):
                bs = pbank[4 + (hp % 2)]
                sa = SA[hp % 2]
                p.add("vector", lambda v, hp=hp, bs=bs, sa=sa: v.tensor_tensor(
                    out=sa[1][:, 0:256], in0=bs[1][:, 0:256],
                    in1=maskT[1][:, (2 * hp) * 128: (2 * hp + 2) * 128],
                    op=ALU.mult), reads=[bs[0], maskT[0]], writes=[sa[0]],
                    wfree=256)
                if hp + 1 < NPAIR:
                    emit_scores(hp + 1, c4)
                bo = pbank[0]
                p.add("tensor", lambda t, hp=hp, c4=c4, bo=bo: t.matmul(
                    bo[1][:, 0:128], Sbf[hp][1][:, :],
                    Qg[1][:, hp * 512 + c4 * 128: hp * 512 + (c4 + 1) * 128],
                    start=True, stop=True),
                    reads=[Sbf[hp][0], Qg[0]], writes=[bo[0]])
                bi = pbank[1]
                p.add("tensor", lambda t, hp=hp, c4=c4, bi=bi, sa=sa: t.matmul(
                    bi[1][0:64, 0:128],
                    Vs[1][:, c4 * D + hp * 128: c4 * D + hp * 128 + 64],
                    sa[1][:, 0:128],
                    start=True, stop=True, tile_position=(0, 0)),
                    reads=[Vs[0], sa[0]], writes=[bi[0]])
                p.add("tensor", lambda t, hp=hp, c4=c4, bi=bi, sa=sa: t.matmul(
                    bi[1][64:128, 0:128],
                    Vs[1][:, c4 * D + hp * 128 + 64: c4 * D + (hp + 1) * 128],
                    sa[1][:, 128:256],
                    start=True, stop=True, tile_position=(0, 64)),
                    reads=[Vs[0], sa[0]], writes=[bi[0]])
                p.add("scalar", lambda s, bo=bo: s.activation(
                    out=csb[1][:, :], in_=bo[1][:, 0:128], func=AF.Copy),
                    reads=[bo[0]], writes=[csb[0]])
                p.add("vector", lambda v, bi=bi: v.tensor_tensor(
                    out=sqscr[1][:, :], in0=bi[1][0:128, 0:128], in1=csb[1][:, :],
                    op=ALU.add), reads=[bi[0], csb[0]], writes=[sqscr[0]],
                    wfree=128)
                p.add("vector", lambda v, hp=hp, c=c: v.tensor_scalar(
                    out=aout[1][:, hp * L + c * 128: hp * L + (c + 1) * 128],
                    in0=sqscr[1][:, :], scalar1=1.0, scalar2=0.0, op0=ALU.mult,
                    op1=ALU.add, accum_out=s1t[1][:, 0:1]),
                    reads=[sqscr[0]], writes=[aout[0], s1t[0]], hazard=True,
                    wfree=128)
                p.add("scalar", lambda s, hp=hp: s.activation(
                    out=sqscr[1][:, :], in_=sqscr[1][:, :], func=AF.Square,
                    accum_out=s2t[1][:, 0:1]),
                    reads=[sqscr[0]], writes=[sqscr[0], s2t[0]], hazard=True)
                p.add("vector", lambda v, hp=hp: v.tensor_tensor(
                    out=sums[1][:, hp:hp + 1], in0=sums[1][:, hp:hp + 1],
                    in1=s1t[1][:, :], op=ALU.add),
                    reads=[s1t[0], sums[0]], writes=[sums[0]])
                p.add("vector", lambda v, hp=hp: v.tensor_tensor(
                    out=sums[1][:, ND + hp: ND + hp + 1],
                    in0=sums[1][:, ND + hp: ND + hp + 1],
                    in1=s2t[1][:, :], op=ALU.add),
                    reads=[s2t[0], sums[0]], writes=[sums[0]])
                bst = pbank[2]
                p.add("tensor", lambda t, hp=hp, c4=c4, bst=bst: t.matmul(
                    bst[1][0:64, 0:64],
                    Kd[1][:, hp * 512 + c4 * 128: hp * 512 + c4 * 128 + 64],
                    Vs[1][:, c4 * D + hp * 128: c4 * D + hp * 128 + 64],
                    start=True, stop=True, tile_position=(0, 0)),
                    reads=[Kd[0], Vs[0]], writes=[bst[0]])
                p.add("tensor", lambda t, hp=hp, c4=c4, bst=bst: t.matmul(
                    bst[1][64:128, 64:128],
                    Kd[1][:, hp * 512 + c4 * 128 + 64: hp * 512 + (c4 + 1) * 128],
                    Vs[1][:, c4 * D + hp * 128 + 64: c4 * D + (hp + 1) * 128],
                    start=True, stop=True, tile_position=(0, 64)),
                    reads=[Kd[0], Vs[0]], writes=[bst[0]])
                p.add("vector", lambda v, hp=hp, bst=bst: v.scalar_tensor_tensor(
                    out=Sst[hp][1][0:64, 0:64], in0=Sst[hp][1][0:64, 0:64],
                    scalar=gC[1][0:64, hp:hp + 1], in1=bst[1][0:64, 0:64],
                    op0=ALU.mult, op1=ALU.add),
                    reads=[Sst[hp][0], gC[0], bst[0]], writes=[Sst[hp][0]])
                p.add("vector", lambda v, hp=hp, bst=bst: v.scalar_tensor_tensor(
                    out=Sst[hp][1][64:128, 64:128], in0=Sst[hp][1][64:128, 64:128],
                    scalar=gC[1][64:128, hp:hp + 1], in1=bst[1][64:128, 64:128],
                    op0=ALU.mult, op1=ALU.add),
                    reads=[Sst[hp][0], gC[0], bst[0]], writes=[Sst[hp][0]])
                p.add("vector", lambda v, hp=hp: v.tensor_copy(
                    out=Sbf[hp][1][:, :], in_=Sst[hp][1][:, :]),
                    reads=[Sst[hp][0]], writes=[Sbf[hp][0]])

    run3n = run3 and p3_norm
    if not run3:
        p.add("vector", lambda v: v.memset(aout[1][:, :], 0.0), writes=[aout[0]])
    if run3n or not run3:
      p.add("tensor", lambda t: t.matmul(
        pbank[0][1][:, 0:16], HHs[1][:, :], sums[1][:, :], start=True, stop=True),
        reads=[HHs[0], sums[0]], writes=[pbank[0][0]])
      p.add("vector", lambda v: v.tensor_copy(out=reps[1][:, :], in_=pbank[0][1][:, 0:16]),
          reads=[pbank[0][0]], writes=[reps[0]])
    NEL = float(64 * L)
    if run3n or not run3:
      p.add("vector", lambda v: v.tensor_scalar(
        out=gnst["mean"][1][:, :], in0=reps[1][:, 0:ND], scalar1=1.0 / NEL,
        scalar2=None, op0=ALU.mult), reads=[reps[0]], writes=[gnst["mean"][0]])
      p.add("vector", lambda v: v.tensor_scalar(
        out=gnst["ex2"][1][:, :], in0=reps[1][:, ND:2 * ND], scalar1=1.0 / NEL,
        scalar2=None, op0=ALU.mult), reads=[reps[0]], writes=[gnst["ex2"][0]])
      p.add("vector", lambda v: v.tensor_tensor(
        out=gnst["t1"][1][:, :], in0=gnst["mean"][1][:, :], in1=gnst["mean"][1][:, :],
        op=ALU.mult), reads=[gnst["mean"][0]], writes=[gnst["t1"][0]])
      p.add("vector", lambda v: v.tensor_tensor(
        out=gnst["var"][1][:, :], in0=gnst["ex2"][1][:, :], in1=gnst["t1"][1][:, :],
        op=ALU.subtract), reads=[gnst["ex2"][0], gnst["t1"][0]], writes=[gnst["var"][0]])
      p.add("vector", lambda v: v.tensor_scalar(
        out=gnst["var"][1][:, :], in0=gnst["var"][1][:, :], scalar1=1e-5,
        scalar2=None, op0=ALU.add), reads=[gnst["var"][0]], writes=[gnst["var"][0]])
      emit_rstd_newton(p, None, gnst["var"], gnst["rstd"], gnst["t1"], gnst["t2"])
      p.add("vector", lambda v: v.tensor_tensor(
        out=gnst["scalev"][1][:, :], in0=gnst["rstd"][1][:, :], in1=gnw[1][:, :],
        op=ALU.mult), reads=[gnst["rstd"][0], gnw[0]], writes=[gnst["scalev"][0]])
      p.add("vector", lambda v: v.tensor_tensor(
        out=gnst["t1"][1][:, :], in0=gnst["mean"][1][:, :], in1=gnst["scalev"][1][:, :],
        op=ALU.mult), reads=[gnst["mean"][0], gnst["scalev"][0]], writes=[gnst["t1"][0]])
      p.add("vector", lambda v: v.tensor_tensor(
        out=gnst["biasv"][1][:, :], in0=gnb[1][:, :], in1=gnst["t1"][1][:, :],
        op=ALU.subtract), reads=[gnb[0], gnst["t1"][0]], writes=[gnst["biasv"][0]])
      for dc in range(ND):
        p.add("scalar", lambda s, dc=dc: s.activation(
            out=aout[1][:, dc * L:(dc + 1) * L], in_=aout[1][:, dc * L:(dc + 1) * L],
            func=AF.Identity, scale=gnst["scalev"][1][:, dc:dc + 1],
            bias=gnst["biasv"][1][:, dc:dc + 1]),
            reads=[aout[0], gnst["scalev"][0], gnst["biasv"][0]], writes=[aout[0]])

    p.barrier()
    if "aout" in ddbg:
        dma("sync", ddbg["aout"], aout[1][:, :], reads=[aout[0]], tag="dbg_aout")
        p.barrier()
    free_phase()

    run4 = 4 in phases
    woS = alloc("woS", [128, ND * D], BF16)
    dma("gpsimd", woS[1][:, :], dwo, writes=[woS[0]], tag="woS")
    p4 = []
    for par in range(4):
        b = {}
        b["xc"] = alloc(f"p4x{par}", [128, D], F32)
        b["gt"] = alloc(f"p4g{par}", [128, D], BF16)
        b["sg1"] = alloc(f"p4s{par}", [128, D], F32)
        b["yt"] = alloc(f"p4y{par}", [128, D], F32)
        p4.append(b)

    for i in range(NT if run4 else 0):
        b = p4[i % 4]
        dma("sync", b["xc"][1][:, :], dx[i * 128:(i + 1) * 128, :],
            writes=[b["xc"][0]], tag=f"p4x{i}")
        dma("sync", b["gt"][1][:, :], dgres[:, i * D:(i + 1) * D],
            writes=[b["gt"][0]], tag=f"p4g{i}")
        bw1, bw2 = pbank[(i % 2) * 2], pbank[(i % 2) * 2 + 1]
        for half, bw in ((0, bw1), (1, bw2)):
            for dc in range(ND):
                p.add("tensor", lambda t, i=i, dc=dc, half=half, bw=bw: t.matmul(
                    bw[1][:, 0:512],
                    aout[1][:, dc * L + i * 128: dc * L + (i + 1) * 128],
                    woS[1][:, dc * D + half * 512: dc * D + (half + 1) * 512],
                    start=(dc == 0), stop=(dc == ND - 1)),
                    reads=[aout[0], woS[0]], writes=[bw[0]])
        for half, bw in ((0, bw1), (1, bw2)):
            p.add("vector", lambda v, b=b, i=i, half=half, bw=bw: v.tensor_tensor(
                out=b["sg1"][1][:, half * 512:(half + 1) * 512],
                in0=bw[1][:, 0:512],
                in1=b["gt"][1][:, half * 512:(half + 1) * 512],
                op=ALU.mult), reads=[bw[0], b["gt"][0]], writes=[b["sg1"][0]])
        p.add("vector", lambda v, b=b: v.tensor_tensor(
            out=b["yt"][1][:, :], in0=b["sg1"][1][:, :], in1=b["xc"][1][:, :],
            op=ALU.add), reads=[b["sg1"][0], b["xc"][0]], writes=[b["yt"][0]])
        dma("sync", dy[i * 128:(i + 1) * 128, :], b["yt"][1][:, :],
            reads=[b["yt"][0]], tag=f"p4y{i}")

    p.barrier()

    with nc.Block() as block:
        sem_ctx = p.lower(nc, block)

    for g in reversed(sem_ctx):
        g.__exit__(None, None, None)
    free_phase()
    for g in reversed(late_ctx):
        g.__exit__(None, None, None)
    for g in reversed(perm_ctx):
        g.__exit__(None, None, None)
    return nc


def _prep_inputs(inputs):
    x = np.ascontiguousarray(np.asarray(inputs["L0_attn_x"], dtype=np.float32))
    Wq = np.asarray(inputs["L0_attn_Wq"], dtype=np.float32)
    Wk = np.asarray(inputs["L0_attn_Wk"], dtype=np.float32)
    Wv = np.asarray(inputs["L0_attn_Wv"], dtype=np.float32)
    Wout = np.asarray(inputs["L0_attn_Wout"], dtype=np.float32)
    gnwv = np.asarray(inputs["L0_attn_gn_weight"], dtype=np.float32)
    gnbv = np.asarray(inputs["L0_attn_gn_bias"], dtype=np.float32)
    wg = np.ascontiguousarray(np.asarray(inputs["L0_ffn_w_gate"], dtype=np.float32))
    wu = np.ascontiguousarray(np.asarray(inputs["L0_ffn_w_up"], dtype=np.float32))
    wd = np.ascontiguousarray(np.asarray(inputs["L0_ffn_w_down"], dtype=np.float32))

    def he(W):
        a = W.transpose(1, 0, 2).reshape(D, D)
        return np.ascontiguousarray(
            a.reshape(ND, 128, D).transpose(1, 0, 2).reshape(128, ND * D))

    def rows(Wm):
        N = Wm.shape[1]
        return np.ascontiguousarray(
            Wm.reshape(ND, 128, N).transpose(1, 0, 2).reshape(128, ND * N))

    consts = host_consts()
    shared = {
        "wq_r": he(Wq), "wk_r": he(Wk), "wv_r": he(Wv), "wout_r": rows(Wout),
        "w_gate": wg, "w_up": wu, "w_down": wd,
        "gnw_r": np.ascontiguousarray(gnwv.reshape(ND, 128).T),
        "gnb_r": np.ascontiguousarray(gnbv.reshape(ND, 128).T),
        "ident": consts["ident"], "ident_bf": consts["ident_bf"],
        "tri8": consts["tri8"], "gam16": consts["gam16"],
        "gam8": consts["gam8"], "selg": consts["selg"],
        "sel2": consts["sel2"], "m2": consts["m2"], "HH": consts["HH"],
        "trow": consts["trow"], "tneg": consts["tneg"], "t127": consts["t127"],
        "qthr": consts["qthr"],
    }
    in_maps = []
    for b in range(8):
        m = dict(shared)
        m["x"] = np.ascontiguousarray(x[b])
        in_maps.append(m)
    return in_maps


def kernel(**inputs):
    if "nc" not in _CACHE:
        _CACHE["nc"] = build_nc()
    nc = _CACHE["nc"]
    in_maps = _prep_inputs(inputs)
    res = run_bass_kernel_spmd(nc, in_maps, core_ids=list(range(8)))
    return np.stack([np.asarray(res.results[b]["y"]) for b in range(8)], axis=0)


# version 18
# speedup vs baseline: 1.4830x; 1.4830x over previous
import numpy as np
import ml_dtypes

import concourse.bass as bass
import concourse.mybir as mybir
from concourse.bass_utils import run_bass_kernel_spmd

F32 = mybir.dt.float32
BF16 = mybir.dt.bfloat16
AF = mybir.ActivationFunctionType
ALU = mybir.AluOpType
AXL = mybir.AxisListType
BF = ml_dtypes.bfloat16

L = 2048
D = 1024
NH = 16
E = 64
I_FFN = 2816
C = 128
NT = L // C
ND = D // 128
NPAIR = NH // 2
NTILE_W = [512, 512, 512, 512, 512, 256]
BISECT_INLINE = True

_CACHE = {}


def _gammas():
    return np.array([1.0 - 2.0 ** (-5 - (h // 2)) for h in range(NH)], dtype=np.float64)


def host_consts():
    g = _gammas()
    t_ = np.arange(C)
    u_ = np.arange(C)
    tri8 = (np.where(t_[None, :] - u_[:, None] >= 0, 1.0, 0.0) / 8.0).astype(np.float32)
    gam16 = g.astype(np.float32).reshape(16, 1)
    gam8 = np.zeros((128, NPAIR), dtype=np.float32)
    for hp in range(NPAIR):
        gam8[0:64, hp] = np.float32(g[2 * hp] / 8.0)
        gam8[64:128, hp] = np.float32(g[2 * hp + 1] / 8.0)
    selg = np.zeros((16, NPAIR * 128), dtype=np.float32)
    for hp in range(NPAIR):
        selg[2 * hp, hp * 128: hp * 128 + 64] = 1.0
        selg[2 * hp + 1, hp * 128 + 64: (hp + 1) * 128] = 1.0
    sel2 = np.zeros((16, 128), dtype=np.float32)
    for k in range(16):
        sel2[k, :] = ((k % 2) == (np.arange(128) >= 64)).astype(np.float32)
    m2 = np.zeros((16, NPAIR), dtype=np.float32)
    for k in range(16):
        m2[k, k // 2] = 1.0
    HH = np.zeros((128, 128), dtype=np.float32)
    HH[0:64, 0:64] = 1.0
    HH[64:128, 64:128] = 1.0
    trow = np.tile(np.arange(128, dtype=np.float32)[None, :], (16, 1))
    qthr = np.zeros((128, 8), dtype=np.float32)
    qthr[:, 0:4] = -np.array([0.25, 0.75, 1.25, 1.75], dtype=np.float32)
    return dict(ident=np.eye(128, dtype=np.float32),
                ident_bf=np.eye(128, dtype=BF),
                tri8=tri8, gam16=gam16, gam8=gam8, selg=selg, sel2=sel2,
                m2=m2, HH=HH, trow=trow, tneg=-trow,
                t127=(127.0 - trow).astype(np.float32), qthr=qthr)


ENGINES = ("tensor", "vector", "scalar", "gpsimd", "sync")


class Buf:
    __slots__ = ("name", "writer", "readers", "psum")

    def __init__(self, name, psum=False):
        self.name = name
        self.writer = None
        self.readers = []
        self.psum = psum


class Op:
    __slots__ = ("oid", "engine", "emit", "deps", "is_dma", "dma_idx", "pos",
                 "hazard", "tag", "wfree", "dma_feed")

    def __init__(self, oid, engine, emit, deps, is_dma, hazard, tag, wfree):
        self.dma_feed = False
        self.oid = oid
        self.engine = engine
        self.emit = emit
        self.deps = deps
        self.is_dma = is_dma
        self.dma_idx = None
        self.pos = None
        self.hazard = hazard
        self.tag = tag
        self.wfree = wfree


class Prog:
    def __init__(self):
        self.ops = []
        self.streams = {e: [] for e in ENGINES}
        self.dma_counts = {e: 0 for e in ENGINES}
        self.last_dma = {}
        self.last_barrier = {}

    def add(self, engine, emit, reads=(), writes=(), dma=False, hazard=False,
            extra_deps=(), tag="", wfree=1, _is_barrier=False):
        deps = set(extra_deps)
        if not _is_barrier and engine in self.last_barrier:
            deps.add(self.last_barrier[engine])
        reads = list(reads)
        writes = list(writes)
        reads2 = []
        for b in reads:
            (writes if b.psum else reads2).append(b)
        reads = reads2
        for b in reads:
            if b.writer is not None:
                deps.add(b.writer)
        for b in writes:
            if b.writer is not None:
                deps.add(b.writer)
            deps.update(b.readers)
        oid = len(self.ops)
        if dma:
            for b in reads:
                if b.writer is not None:
                    self.ops[b.writer].dma_feed = True
        op = Op(oid, engine, emit, deps, dma, hazard, tag, wfree)
        self.ops.append(op)
        op.pos = len(self.streams[engine])
        self.streams[engine].append(op)
        if dma:
            op.dma_idx = self.dma_counts[engine]
            self.dma_counts[engine] += 1
            self.last_dma[engine] = oid
        for b in reads:
            b.readers.append(oid)
        for b in writes:
            b.writer = oid
            b.readers = []
        return oid

    def barrier(self):
        deps = set()
        for e in ENGINES:
            if self.streams[e]:
                deps.add(self.streams[e][-1].oid)
        deps.update(self.last_dma.values())
        new_bar = {}
        for e in ENGINES:
            new_bar[e] = self.add(e, None, extra_deps=deps, tag="barrier",
                                  _is_barrier=True)
        self.last_barrier = new_bar

    def lower(self, nc, block):
        wait_targets = {e: set() for e in ENGINES}
        for o in self.ops:
            for did in o.deps:
                d = self.ops[did]
                if d.is_dma:
                    continue
                if d.engine != o.engine:
                    wait_targets[d.engine].add(d.pos)
        inc_rank = {}
        for e in ENGINES:
            r = {}
            for i, pp in enumerate(sorted(wait_targets[e])):
                r[pp] = i + 1
            inc_rank[e] = r

        eng_sems = {}
        dma_sems = {}
        ctx = []
        for e in ENGINES:
            s = nc.semaphore(f"es_{e}")
            eng_sems[e] = s.__enter__()
            ctx.append(s)
            s2 = nc.semaphore(f"ds_{e}")
            dma_sems[e] = s2.__enter__()
            ctx.append(s2)

        def run_engine(e, eng):
            prev = None
            for o in self.streams[e]:
                if (prev is not None and not prev.is_dma and not o.is_dma
                        and prev.wfree < 1000000 and not prev.hazard
                        and prev.oid in o.deps and o.emit is not None
                        and prev.emit is not None):
                    if e in ("vector", "gpsimd"):
                        eng.engine_nop()
                    else:
                        eng.nop()
                prev = o
                waits = {}
                for did in sorted(o.deps):
                    d = self.ops[did]
                    if d.is_dma:
                        key = ("dma", d.engine)
                        val = 16 * (d.dma_idx + 1)
                    elif d.engine == e:
                        continue
                    else:
                        key = ("eng", d.engine)
                        val = inc_rank[d.engine][d.pos]
                    if waits.get(key, 0) < val:
                        waits[key] = val
                for (kind, se), val in sorted(waits.items()):
                    sem = dma_sems[se] if kind == "dma" else eng_sems[se]
                    eng.wait_ge(sem, val)
                if o.emit is None:
                    inst = eng.drain()
                    if o.pos in inc_rank[e]:
                        inst.then_inc(eng_sems[e], 1)
                    continue
                inst = o.emit(eng)
                if o.is_dma:
                    inst.then_inc(dma_sems[e], 16)
                elif o.hazard or o.dma_feed:
                    dr = eng.drain()
                    if o.pos in inc_rank[e]:
                        dr.then_inc(eng_sems[e], 1)
                elif o.pos in inc_rank[e]:
                    inst.then_inc(eng_sems[e], 1)

        for e in ENGINES:
            getattr(block, e)(lambda eng, e=e: run_engine(e, eng))
        return ctx


def emit_quant_multi(p, items, thr, tag=""):
    def blk(a):
        return a.rearrange("p (b k) -> p b k", k=16)

    if len(items) >= 2:
        segs = [(it, 0, it["w"] // 16) for it in items]
    else:
        it0 = items[0]
        h = it0["w"] // 32
        segs = [(it0, 0, h), (it0, h, it0["w"] // 16)]
    for it, c0, c1 in segs:
        p.add("vector", lambda v, it=it, c0=c0, c1=c1: v.tensor_reduce(
            out=it["scr"]["bm"][1][:, c0:c1],
            in_=blk(it["src_ap"]())[:, c0:c1, :],
            axis=AXL.X, op=ALU.max, apply_absolute_value=True),
            reads=[it["src_buf"]], writes=[it["scr"]["bm"][0]],
            tag=tag + "bm", wfree=64)
    for it, c0, c1 in segs:
        p.add("vector", lambda v, it=it, c0=c0, c1=c1: v.tensor_scalar(
            out=it["scr"]["s6"][1][:, c0:c1], in0=it["scr"]["bm"][1][:, c0:c1],
            scalar1=float(1.0 / 6.0), scalar2=1e-12, op0=ALU.mult, op1=ALU.max),
            reads=[it["scr"]["bm"][0]], writes=[it["scr"]["s6"][0]],
            tag=tag + "s6", wfree=64)
    for it in items:
        p.add("scalar", lambda s, it=it: s.activation(
            out=it["scr"]["ab"][1][:, 0:it["w"]], in_=it["src_ap"](),
            func=AF.Abs), reads=[it["src_buf"]], writes=[it["scr"]["ab"][0]],
            tag=tag + "ab")
    for it, c0, c1 in segs:
        p.add("vector", lambda v, it=it, c0=c0, c1=c1: v.reciprocal(
            out=it["scr"]["si"][1][:, c0:c1], in_=it["scr"]["s6"][1][:, c0:c1]),
            reads=[it["scr"]["s6"][0]], writes=[it["scr"]["si"][0]],
            tag=tag + "si")
    for it in items:
        p.add("scalar", lambda s, it=it: s.activation(
            out=it["scr"]["sg"][1][:, 0:it["w"]], in_=it["src_ap"](),
            func=AF.Sign), reads=[it["src_buf"]], writes=[it["scr"]["sg"][0]],
            tag=tag + "sg")
    for it, c0, c1 in segs:
        p.add("vector", lambda v, it=it, c0=c0, c1=c1: v.tensor_tensor(
            out=blk(it["scr"]["z"][1].ap()[:, 0:it["w"]])[:, c0:c1, :],
            in0=blk(it["scr"]["ab"][1].ap()[:, 0:it["w"]])[:, c0:c1, :],
            in1=it["scr"]["si"][1].ap()[:, c0:c1].unsqueeze(2).broadcast_to([128, c1 - c0, 16]),
            op=ALU.mult),
            reads=[it["scr"]["ab"][0], it["scr"]["si"][0]],
            writes=[it["scr"]["z"][0]], tag=tag + "z", wfree=it["w"])
    for k in range(4):
        for it in items:
            p.add("scalar", lambda s, it=it, k=k: s.activation(
                out=it["scr"][f"s{k + 1}"][1][:, 0:it["w"]],
                in_=it["scr"]["z"][1][:, 0:it["w"]], func=AF.Sign,
                bias=thr[1][:, k:k + 1], scale=1.0),
                reads=[it["scr"]["z"][0], thr[0]],
                writes=[it["scr"][f"s{k + 1}"][0]], tag=tag + f"lo{k}")
    for nm, mid in (("iA", 2.5), ("iB", 3.5), ("iC", 5.0)):
        for it in items:
            p.add("vector", lambda v, it=it, nm=nm, mid=mid: v.tensor_scalar(
                out=it["scr"][nm][1][:, 0:it["w"]],
                in0=it["scr"]["z"][1][:, 0:it["w"]], scalar1=mid,
                scalar2=None, op0=ALU.is_gt),
                reads=[it["scr"]["z"][0]], writes=[it["scr"][nm][0]],
                tag=tag + nm, wfree=it["w"])
    for other in ("s2", "s3", "s4"):
        for it in items:
            p.add("vector", lambda v, it=it, other=other: v.tensor_tensor(
                out=it["scr"]["s1"][1][:, 0:it["w"]],
                in0=it["scr"]["s1"][1][:, 0:it["w"]],
                in1=it["scr"][other][1][:, 0:it["w"]], op=ALU.add),
                reads=[it["scr"]["s1"][0], it["scr"][other][0]],
                writes=[it["scr"]["s1"][0]], tag=tag + "A" + other,
                wfree=it["w"])
    for it in items:
        p.add("vector", lambda v, it=it: v.tensor_tensor(
            out=it["scr"]["iA"][1][:, 0:it["w"]],
            in0=it["scr"]["iA"][1][:, 0:it["w"]],
            in1=it["scr"]["iB"][1][:, 0:it["w"]], op=ALU.add),
            reads=[it["scr"]["iA"][0], it["scr"]["iB"][0]],
            writes=[it["scr"]["iA"][0]], tag=tag + "dAB", wfree=it["w"])
    for it in items:
        p.add("vector", lambda v, it=it: v.scalar_tensor_tensor(
            out=it["scr"]["iA"][1][:, 0:it["w"]],
            in0=it["scr"]["iC"][1][:, 0:it["w"]], scalar=2.0,
            in1=it["scr"]["iA"][1][:, 0:it["w"]], op0=ALU.mult, op1=ALU.add),
            reads=[it["scr"]["iC"][0], it["scr"]["iA"][0]],
            writes=[it["scr"]["iA"][0]], tag=tag + "dC", wfree=it["w"])
    for it in items:
        p.add("vector", lambda v, it=it: v.scalar_tensor_tensor(
            out=it["scr"]["s2"][1][:, 0:it["w"]],
            in0=it["scr"]["s1"][1][:, 0:it["w"]], scalar=0.25,
            in1=it["scr"]["iA"][1][:, 0:it["w"]], op0=ALU.mult, op1=ALU.add),
            reads=[it["scr"]["s1"][0], it["scr"]["iA"][0]],
            writes=[it["scr"]["s2"][0]], tag=tag + "u", wfree=it["w"])
    for it in items:
        p.add("vector", lambda v, it=it: v.scalar_tensor_tensor(
            out=it["scr"]["s3"][1][:, 0:it["w"]],
            in0=it["scr"]["s2"][1][:, 0:it["w"]], scalar=1.0,
            in1=it["scr"]["sg"][1][:, 0:it["w"]], op0=ALU.add, op1=ALU.mult),
            reads=[it["scr"]["s2"][0], it["scr"]["sg"][0]],
            writes=[it["scr"]["s3"][0]], tag=tag + "tn", wfree=it["w"])
    for it in items:
        nb = it["w"] // 16
        p.add("vector", lambda v, it=it, nb=nb: v.tensor_tensor(
            out=blk(it["dst_ap"]()),
            in0=blk(it["scr"]["s3"][1].ap()[:, 0:it["w"]]),
            in1=it["scr"]["s6"][1].ap()[:, 0:nb].unsqueeze(2).broadcast_to([128, nb, 16]),
            op=ALU.mult),
            reads=[it["scr"]["s3"][0], it["scr"]["s6"][0]],
            writes=[it["dst_buf"]], tag=tag + "y", wfree=it["w"])


def emit_quant(p, src_buf, src_ap, w, scr, dst_buf, dst_ap, tag="", thr=None):
    emit_quant_multi(p, [dict(src_buf=src_buf, src_ap=src_ap, w=w, scr=scr,
                              dst_buf=dst_buf, dst_ap=dst_ap)], thr, tag=tag)


def quant_scratch(alloc, prefix, w):
    scr = {}
    scr["ab"] = alloc(prefix + "ab", [128, w], F32)
    scr["z"] = alloc(prefix + "z", [128, w], F32)
    for nm in ("sg", "s1", "s2", "s3", "s4", "iA", "iB", "iC"):
        scr[nm] = alloc(prefix + nm, [128, w], BF16)
    for nm in ("bm", "s6", "si"):
        scr[nm] = alloc(prefix + nm, [128, w // 16], F32)
    return scr


def emit_rstd_newton(p, st, src_var, dst_rstd, t1, t2):
    p.add("scalar", lambda s: s.activation(
        out=t1[1][:, :], in_=src_var[1][:, :], func=AF.Sqrt),
        reads=[src_var[0]], writes=[t1[0]])
    p.add("vector", lambda v: v.reciprocal(
        out=dst_rstd[1][:, :], in_=t1[1][:, :]),
        reads=[t1[0]], writes=[dst_rstd[0]], hazard=True)
    p.add("vector", lambda v: v.tensor_tensor(
        out=t1[1][:, :], in0=dst_rstd[1][:, :], in1=dst_rstd[1][:, :],
        op=ALU.mult), reads=[dst_rstd[0]], writes=[t1[0]])
    p.add("vector", lambda v: v.tensor_tensor(
        out=t2[1][:, :], in0=t1[1][:, :], in1=src_var[1][:, :], op=ALU.mult),
        reads=[t1[0], src_var[0]], writes=[t2[0]])
    p.add("vector", lambda v: v.tensor_scalar(
        out=t2[1][:, :], in0=t2[1][:, :], scalar1=-0.5, scalar2=1.5,
        op0=ALU.mult, op1=ALU.add), reads=[t2[0]], writes=[t2[0]])
    p.add("vector", lambda v: v.tensor_tensor(
        out=dst_rstd[1][:, :], in0=dst_rstd[1][:, :], in1=t2[1][:, :],
        op=ALU.mult), reads=[dst_rstd[0], t2[0]], writes=[dst_rstd[0]])


def build_nc(phases=(1, 2, 3, 4), debug=(), p3_ngroups=4, p3_nchunks=4, p3_qkv=True, p3_norm=True):
    nc = bass.Bass("TRN2", target_bir_lowering=False, detect_race_conditions=False)

    dx = nc.dram_tensor("x", [L, D], F32, kind="ExternalInput").ap()
    dwq = nc.dram_tensor("wq_r", [128, ND * D], F32, kind="ExternalInput").ap()
    dwk = nc.dram_tensor("wk_r", [128, ND * D], F32, kind="ExternalInput").ap()
    dwv = nc.dram_tensor("wv_r", [128, ND * D], F32, kind="ExternalInput").ap()
    dwo = nc.dram_tensor("wout_r", [128, ND * D], F32, kind="ExternalInput").ap()
    dwg = nc.dram_tensor("w_gate", [I_FFN, D], F32, kind="ExternalInput").ap()
    dwu = nc.dram_tensor("w_up", [I_FFN, D], F32, kind="ExternalInput").ap()
    dwd = nc.dram_tensor("w_down", [D, I_FFN], F32, kind="ExternalInput").ap()
    dgnw = nc.dram_tensor("gnw_r", [128, ND], F32, kind="ExternalInput").ap()
    dgnb = nc.dram_tensor("gnb_r", [128, ND], F32, kind="ExternalInput").ap()
    dident = nc.dram_tensor("ident", [128, 128], F32, kind="ExternalInput").ap()
    didentb = nc.dram_tensor("ident_bf", [128, 128], BF16, kind="ExternalInput").ap()
    dtri8 = nc.dram_tensor("tri8", [128, 128], F32, kind="ExternalInput").ap()
    dgam16 = nc.dram_tensor("gam16", [16, 1], F32, kind="ExternalInput").ap()
    dgam8 = nc.dram_tensor("gam8", [128, NPAIR], F32, kind="ExternalInput").ap()
    dselg = nc.dram_tensor("selg", [16, NPAIR * 128], F32, kind="ExternalInput").ap()
    dsel2 = nc.dram_tensor("sel2", [16, 128], F32, kind="ExternalInput").ap()
    dm2 = nc.dram_tensor("m2", [16, NPAIR], F32, kind="ExternalInput").ap()
    dtrow = nc.dram_tensor("trow", [16, 128], F32, kind="ExternalInput").ap()
    dtneg = nc.dram_tensor("tneg", [16, 128], F32, kind="ExternalInput").ap()
    dt127 = nc.dram_tensor("t127", [16, 128], F32, kind="ExternalInput").ap()
    dhh = nc.dram_tensor("HH", [128, 128], F32, kind="ExternalInput").ap()
    dqthr = nc.dram_tensor("qthr", [128, 8], F32, kind="ExternalInput").ap()
    dy = nc.dram_tensor("y", [L, D], F32, kind="ExternalOutput").ap()
    dxnT = nc.dram_tensor("xnT_sp", [128, ND * L], BF16, kind="Internal").ap()
    dgres = nc.dram_tensor("gres_sp", [128, NT * D], BF16, kind="Internal").ap()
    ddbg = {}
    if "xqT" in debug:
        ddbg["xqT"] = nc.dram_tensor("dbg_xqT", [128, ND * L], BF16, kind="ExternalOutput").ap()
    if "xnT" in debug:
        ddbg["xnT"] = nc.dram_tensor("dbg_xnT", [128, ND * L], BF16, kind="ExternalOutput").ap()
    if "acc" in debug:
        ddbg["acc"] = nc.dram_tensor("dbg_acc", [128, NT * D], F32, kind="ExternalOutput").ap()
    if "aout" in debug:
        ddbg["aout"] = nc.dram_tensor("dbg_aout", [128, ND * L], BF16, kind="ExternalOutput").ap()

    p = Prog()
    perm_ctx = []
    late_ctx = []
    p12_ctx = []
    phase_ctx = []

    def alloc(name, shape, dtype, perm=False, late=False, p12=False):
        g = nc.sbuf_tensor(name, shape, dtype)
        t = g.__enter__()
        (perm_ctx if perm else (late_ctx if late else
         (p12_ctx if p12 else phase_ctx))).append(g)
        return (Buf(name), t)

    def palloc(name, shape, dtype):
        g = nc.psum_tensor(name, shape, dtype)
        t = g.__enter__()
        perm_ctx.append(g)
        return (Buf(name, psum=True), t)

    def free_phase():
        nonlocal phase_ctx
        for g in reversed(phase_ctx):
            g.__exit__(None, None, None)
        phase_ctx = []

    def dma(engine, out_ap, in_ap, reads=(), writes=(), tag=""):
        def f(eng, out_ap=out_ap, in_ap=in_ap):
            return eng.dma_start(out=out_ap, in_=in_ap)
        return p.add(engine, f, reads=reads, writes=writes, dma=True, tag=tag)

    ident = alloc("identS", [128, 128], F32, perm=True)
    identb = alloc("identbS", [128, 128], BF16, perm=True)
    gC = alloc("gCS", [128, NPAIR], F32, perm=True)
    HHs = alloc("HHS", [128, 128], F32, perm=True)
    gnw = alloc("gnwS", [128, ND], F32, perm=True)
    gnb = alloc("gnbS", [128, ND], F32, perm=True)
    qthr = alloc("qthrS", [128, 8], F32, perm=True)
    xqT = alloc("xqT", [128, ND * L], BF16, p12=True)

    pbank = [palloc(f"pb{i}", [128, 512], F32) for i in range(6)]
    pbbf = [palloc(f"pbb{i}", [128, 1024], BF16) for i in range(2)]

    dma("sync", ident[1][:, :], dident, writes=[ident[0]], tag="c0")
    dma("sync", identb[1][:, :], didentb, writes=[identb[0]], tag="c1")
    dma("sync", HHs[1][:, :], dhh, writes=[HHs[0]], tag="c6")
    dma("sync", gnw[1][:, :], dgnw, writes=[gnw[0]], tag="c7")
    dma("sync", gnb[1][:, :], dgnb, writes=[gnb[0]], tag="c8")
    dma("sync", qthr[1][:, :], dqthr, writes=[qthr[0]], tag="c9")

    xfull = alloc("xfull", [128, NT * D], F32)
    sums16 = alloc("sums16", [128, NT], F32)
    sqs16 = alloc("sqs16", [128, NT], F32)
    mean16 = alloc("mean16", [128, NT], F32)
    msq16 = alloc("msq16", [128, NT], F32)
    var16 = alloc("var16", [128, NT], F32)
    rstd16 = alloc("rstd16", [128, NT], F32)
    negm16 = alloc("negm16", [128, NT], F32)
    nt1 = alloc("nt1", [128, NT], F32)
    nt2 = alloc("nt2", [128, NT], F32)
    sqscrA = alloc("sqscrA", [128, D], F32)
    ph1 = []
    for par in range(2):
        b = {}
        b["xn"] = alloc(f"xn{par}", [128, D], F32)
        b["xnb"] = alloc(f"xnb{par}", [128, D], BF16)
        b["xq"] = alloc(f"p1xq{par}", [128, D], BF16)
        b["xntS"] = alloc(f"p1xnt{par}", [128, D], BF16)
        b.update(quant_scratch(alloc, f"p1s{par}", D))
        ph1.append(b)

    dma("sync", xfull[1].ap().rearrange("p (i d) -> p i d", d=D),
        dx.rearrange("(i p) d -> p i d", p=128), writes=[xfull[0]], tag="p1x")
    for i in range(NT):
        p.add("vector", lambda v, i=i: v.tensor_reduce(
            out=sums16[1][:, i:i + 1], in_=xfull[1][:, i * D:(i + 1) * D],
            axis=AXL.X, op=ALU.add),
            reads=[xfull[0]], writes=[sums16[0]], hazard=(i == NT - 1))
        p.add("scalar", lambda s, i=i: s.activation(
            out=sqscrA[1][:, :], in_=xfull[1][:, i * D:(i + 1) * D],
            func=AF.Square, accum_out=sqs16[1][:, i:i + 1]),
            reads=[xfull[0]], writes=[sqscrA[0], sqs16[0]],
            hazard=(i == NT - 1))
    p.add("vector", lambda v: v.tensor_scalar(
        out=mean16[1][:, :], in0=sums16[1][:, :], scalar1=float(1.0 / D),
        scalar2=None, op0=ALU.mult), reads=[sums16[0]], writes=[mean16[0]])
    p.add("vector", lambda v: v.tensor_tensor(
        out=msq16[1][:, :], in0=mean16[1][:, :], in1=mean16[1][:, :],
        op=ALU.mult), reads=[mean16[0]], writes=[msq16[0]])
    p.add("vector", lambda v: v.scalar_tensor_tensor(
        out=var16[1][:, :], in0=sqs16[1][:, :], scalar=float(1.0 / D),
        in1=msq16[1][:, :], op0=ALU.mult, op1=ALU.subtract),
        reads=[sqs16[0], msq16[0]], writes=[var16[0]])
    p.add("vector", lambda v: v.tensor_scalar(
        out=var16[1][:, :], in0=var16[1][:, :], scalar1=1e-5,
        scalar2=None, op0=ALU.add), reads=[var16[0]], writes=[var16[0]])
    emit_rstd_newton(p, None, var16, rstd16, nt1, nt2)
    p.add("vector", lambda v: v.tensor_tensor(
        out=negm16[1][:, :], in0=mean16[1][:, :], in1=rstd16[1][:, :],
        op=ALU.mult), reads=[mean16[0], rstd16[0]], writes=[negm16[0]])
    p.add("vector", lambda v: v.tensor_scalar(
        out=negm16[1][:, :], in0=negm16[1][:, :], scalar1=-1.0,
        scalar2=None, op0=ALU.mult), reads=[negm16[0]], writes=[negm16[0]])

    for i0 in range(0, NT, 2):
        items = []
        for i in (i0, i0 + 1):
            b = ph1[i % 2]
            p.add("scalar", lambda s, b=b, i=i: s.activation(
                out=b["xn"][1][:, :], in_=xfull[1][:, i * D:(i + 1) * D],
                func=AF.Identity, scale=rstd16[1][:, i:i + 1],
                bias=negm16[1][:, i:i + 1]),
                reads=[xfull[0], rstd16[0], negm16[0]], writes=[b["xn"][0]])
            items.append(dict(
                src_buf=b["xn"][0], src_ap=(lambda b=b: b["xn"][1].ap()[:, :]),
                w=D, scr=b, dst_buf=b["xq"][0],
                dst_ap=(lambda b=b: b["xq"][1].ap()[:, :])))
        emit_quant_multi(p, items, qthr, tag=f"p1q{i0}_")
        for i in (i0, i0 + 1):
            b = ph1[i % 2]
            p.add("vector", lambda v, b=b: v.tensor_copy(
                out=b["xnb"][1][:, :], in_=b["xn"][1][:, :]),
                reads=[b["xn"][0]], writes=[b["xnb"][0]], wfree=D)
            pbq = pbbf[i % 2]
            for dc in range(ND):
                p.add("tensor", lambda t, b=b, dc=dc, pbq=pbq: t.transpose(
                    pbq[1][:, dc * 128:(dc + 1) * 128],
                    b["xq"][1][:, dc * 128:(dc + 1) * 128], identb[1][:, :]),
                    reads=[b["xq"][0], identb[0]], writes=[pbq[0]])
            p.add("scalar", lambda s, i=i, pbq=pbq: s.activation(
                out=xqT[1].ap().rearrange("p (dc t) -> p dc t", t=L)[:, :, i * 128:(i + 1) * 128],
                in_=pbq[1].ap().rearrange("p (dc t) -> p dc t", t=128),
                func=AF.Copy), reads=[pbq[0]], writes=[xqT[0]])
            for dc in range(ND):
                p.add("tensor", lambda t, b=b, dc=dc, pbq=pbq: t.transpose(
                    pbq[1][:, dc * 128:(dc + 1) * 128],
                    b["xnb"][1][:, dc * 128:(dc + 1) * 128], identb[1][:, :]),
                    reads=[b["xnb"][0], identb[0]], writes=[pbq[0]])
            p.add("scalar", lambda s, b=b, pbq=pbq: s.activation(
                out=b["xntS"][1][:, :], in_=pbq[1][:, :], func=AF.Copy),
                reads=[pbq[0]], writes=[b["xntS"][0]])
            dma("gpsimd",
                dxnT.rearrange("p (dc i t) -> p dc i t", dc=ND, i=NT)[:, :, i, :],
                b["xntS"][1].ap().rearrange("p (dc t) -> p dc t", t=128),
                reads=[b["xntS"][0]], tag=f"p1sp{i}")

    p.barrier()
    if "xqT" in ddbg:
        dma("sync", ddbg["xqT"], xqT[1][:, :], reads=[xqT[0]], tag="dbg_xqT")
    if "xnT" in ddbg:
        dma("sync", ddbg["xnT"], dxnT, tag="dbg_xnT")
    free_phase()

    run2 = 2 in phases
    acc = alloc("accF", [128, NT * D], F32)
    wstg2 = [alloc(f"wstg{k}", [128, D], F32) for k in range(2)]
    wq_nat = alloc("wq_nat", [128, 12 * D], BF16)
    wgqT = alloc("wgqT", [128, ND * 512], BF16)
    wuqT = alloc("wuqT", [128, ND * 512], BF16)
    wdqT = alloc("wdqT", [128, 4 * D], BF16)
    f2 = quant_scratch(alloc, "f2", 512)
    g2 = [quant_scratch(alloc, f"g2{k}", 512) for k in range(2)]
    hb = []
    for par in range(2):
        hh = {}
        hh["hs"] = alloc(f"hs{par}", [128, 512], F32)
        hh["h"] = alloc(f"h{par}", [128, 512], F32)
        hh["hq"] = alloc(f"hq{par}", [128, 512], BF16)
        hh["hqT"] = alloc(f"hqT{par}", [128, 512], BF16)
        hb.append(hh)
    gstg = [alloc(f"gstg{par}", [128, D], BF16) for par in range(2)]

    NT_OFF = [0]
    for W in NTILE_W:
        NT_OFF.append(NT_OFF[-1] + W)

    def prep_tile(j):
        W = NTILE_W[j]
        CW = W // 128
        n0 = NT_OFF[j]
        load_fns = []
        witems = []
        for slot in range(3 * CW):
            kind, sl = divmod(slot, CW)
            stg = wstg2[slot % 2]

            def load(kind=kind, sl=sl, stg=stg, j=j):
                if kind < 2:
                    dsrc = dwg if kind == 0 else dwu
                    dma("gpsimd", stg[1][:, :],
                        dsrc.rearrange("(a p) k -> p a k", p=128)[:, (n0 // 128) + sl:(n0 // 128) + sl + 1, :].rearrange("p a k -> p (a k)"),
                        writes=[stg[0]], tag=f"w{kind}_{j}_{sl}")
                else:
                    ndc = D // W
                    dma("gpsimd",
                        stg[1].ap().rearrange("p (dc n) -> p dc n", n=W),
                        dwd.rearrange("(dc p) n -> p dc n", p=128)[:, sl * ndc:(sl + 1) * ndc, n0:n0 + W],
                        writes=[stg[0]], tag=f"wd_{j}_{sl}")

            load_fns.append(load)
            so = slot
            for hf in range(2):
                witems.append(dict(
                    src_buf=stg[0],
                    src_ap=(lambda stg=stg, hf=hf: stg[1].ap()[:, hf * 512:(hf + 1) * 512]),
                    w=512, scr=g2[hf], dst_buf=wq_nat[0],
                    dst_ap=(lambda so=so, hf=hf: wq_nat[1].ap()[:, so * D + hf * 512: so * D + (hf + 1) * 512])))
        for k in range(min(2, len(load_fns))):
            load_fns[k]()
        return load_fns, witems

    def trans_tile(j):
        W = NTILE_W[j]
        CW = W // 128
        for kind, dstT in ((0, wgqT), (1, wuqT)):
            for nc4 in range(CW):
                for kc in range(ND):
                    p.add("tensor", lambda t, kind=kind, nc4=nc4, kc=kc, CW=CW: t.transpose(
                        pbbf[0][1][:, kc * 128:(kc + 1) * 128],
                        wq_nat[1][:, (kind * CW + nc4) * D + kc * 128: (kind * CW + nc4) * D + (kc + 1) * 128],
                        identb[1][:, :]), reads=[wq_nat[0], identb[0]],
                        writes=[pbbf[0][0]])
                p.add("scalar", lambda s, dstT=dstT, nc4=nc4: s.activation(
                    out=dstT[1].ap().rearrange("p (kc n) -> p kc n", n=512)[:, :, nc4 * 128:(nc4 + 1) * 128],
                    in_=pbbf[0][1].ap().rearrange("p (kc t) -> p kc t", t=128),
                    func=AF.Copy), reads=[pbbf[0][0]], writes=[dstT[0]])
        dbase = 2 * CW * D
        for nc4 in range(CW):
            for dc in range(ND):
                p.add("tensor", lambda t, nc4=nc4, dc=dc, W=W, dbase=dbase: t.transpose(
                    pbbf[0][1][:, dc * 128:(dc + 1) * 128],
                    wq_nat[1][:, dbase + dc * W + nc4 * 128: dbase + dc * W + (nc4 + 1) * 128],
                    identb[1][:, :]), reads=[wq_nat[0], identb[0]],
                    writes=[pbbf[0][0]])
            p.add("scalar", lambda s, nc4=nc4: s.activation(
                out=wdqT[1][:, nc4 * D:(nc4 + 1) * D], in_=pbbf[0][1][:, :],
                func=AF.Copy), reads=[pbbf[0][0]], writes=[wdqT[0]])

    def t_loop(j, load_fns=(), witems=()):
        W = NTILE_W[j]
        CW = W // 128

        def emit_gateup(i):
            bg, bu = pbank[(i % 2) * 2], pbank[(i % 2) * 2 + 1]
            for kc in range(ND):
                p.add("tensor", lambda t, kc=kc, i=i, W=W, bg=bg: t.matmul(
                    bg[1][:, 0:W],
                    xqT[1][:, kc * L + i * 128: kc * L + (i + 1) * 128],
                    wgqT[1][:, kc * 512: kc * 512 + W],
                    start=(kc == 0), stop=(kc == ND - 1)),
                    reads=[xqT[0], wgqT[0]], writes=[bg[0]])
            for kc in range(ND):
                p.add("tensor", lambda t, kc=kc, i=i, W=W, bu=bu: t.matmul(
                    bu[1][:, 0:W],
                    xqT[1][:, kc * L + i * 128: kc * L + (i + 1) * 128],
                    wuqT[1][:, kc * 512: kc * 512 + W],
                    start=(kc == 0), stop=(kc == ND - 1)),
                    reads=[xqT[0], wuqT[0]], writes=[bu[0]])

        if not BISECT_INLINE:
            for k in range(0, len(witems), 2):
                emit_quant_multi(p, witems[k:k + 2], qthr, tag=f"wp{j}_{k}_")
                if k // 2 + 2 < len(load_fns):
                    load_fns[k // 2 + 2]()
        emit_gateup(0)
        for i in range(NT):
            hh = hb[i % 2]
            bg, bu = pbank[(i % 2) * 2], pbank[(i % 2) * 2 + 1]
            bd1, bd2 = pbank[4], pbank[5]
            p.add("scalar", lambda s, hh=hh, W=W, bg=bg: s.activation(
                out=hh["hs"][1][:, 0:W], in_=bg[1][:, 0:W], func=AF.Silu),
                reads=[bg[0]], writes=[hh["hs"][0]])
            p.add("vector", lambda v, hh=hh, W=W, bu=bu: v.tensor_tensor(
                out=hh["h"][1][:, 0:W], in0=hh["hs"][1][:, 0:W],
                in1=bu[1][:, 0:W], op=ALU.mult),
                reads=[hh["hs"][0], bu[0]], writes=[hh["h"][0]], wfree=W)
            if i + 1 < NT:
                emit_gateup(i + 1)
            items = [dict(
                src_buf=hh["h"][0],
                src_ap=(lambda hh=hh, W=W: hh["h"][1].ap()[:, 0:W]),
                w=W, scr=f2, dst_buf=hh["hq"][0],
                dst_ap=(lambda hh=hh, W=W: hh["hq"][1].ap()[:, 0:W]))]
            if BISECT_INLINE:
                items += list(witems[2 * i:2 * i + 2])
            emit_quant_multi(p, items, qthr, tag=f"hq{j}_{i}_")
            if BISECT_INLINE and i + 2 < len(load_fns):
                load_fns[i + 2]()
            pbh = pbbf[1]
            for nc4 in range(CW):
                p.add("tensor", lambda t, hh=hh, nc4=nc4, pbh=pbh: t.transpose(
                    pbh[1][:, nc4 * 128:(nc4 + 1) * 128],
                    hh["hq"][1][:, nc4 * 128:(nc4 + 1) * 128],
                    identb[1][:, :]), reads=[hh["hq"][0], identb[0]],
                    writes=[pbh[0]])
            p.add("scalar", lambda s, hh=hh, CW=CW, pbh=pbh: s.activation(
                out=hh["hqT"][1][:, 0:CW * 128], in_=pbh[1][:, 0:CW * 128],
                func=AF.Copy), reads=[pbh[0]], writes=[hh["hqT"][0]])
            for half, bd in ((0, bd1), (1, bd2)):
                for nc4 in range(CW):
                    p.add("tensor", lambda t, hh=hh, nc4=nc4, half=half, bd=bd, CW=CW: t.matmul(
                        bd[1][:, 0:512],
                        hh["hqT"][1][:, nc4 * 128:(nc4 + 1) * 128],
                        wdqT[1][:, nc4 * D + half * 512: nc4 * D + (half + 1) * 512],
                        start=(nc4 == 0), stop=(nc4 == CW - 1)),
                        reads=[hh["hqT"][0], wdqT[0]], writes=[bd[0]])
            for half, bd in ((0, bd1), (1, bd2)):
                if j == 0:
                    p.add("scalar", lambda s, i=i, half=half, bd=bd: s.activation(
                        out=acc[1][:, i * D + half * 512: i * D + (half + 1) * 512],
                        in_=bd[1][:, 0:512], func=AF.Copy),
                        reads=[bd[0]], writes=[acc[0]])
                else:
                    p.add("vector", lambda v, i=i, half=half, bd=bd: v.tensor_tensor(
                        out=acc[1][:, i * D + half * 512: i * D + (half + 1) * 512],
                        in0=acc[1][:, i * D + half * 512: i * D + (half + 1) * 512],
                        in1=bd[1][:, 0:512], op=ALU.add),
                        reads=[acc[0], bd[0]], writes=[acc[0]], wfree=512)

    if run2:
        lf0, w0 = prep_tile(0)
        for k in range(0, len(w0), 2):
            emit_quant_multi(p, w0[k:k + 2], qthr, tag=f"wq0_{k}_")
            if k // 2 + 2 < len(lf0):
                lf0[k // 2 + 2]()
        trans_tile(0)
        for j in range(len(NTILE_W)):
            if j + 1 < len(NTILE_W):
                lfs, nxt = prep_tile(j + 1)
            else:
                lfs, nxt = (), ()
            t_loop(j, lfs, nxt)
            if j + 1 < len(NTILE_W):
                trans_tile(j + 1)

    for i in range(NT if run2 else 0):
        gs = gstg[i % 2]
        p.add("scalar", lambda s, i=i, gs=gs: s.activation(
            out=gs[1][:, :], in_=acc[1][:, i * D:(i + 1) * D], func=AF.Silu),
            reads=[acc[0]], writes=[gs[0]])
        dma("gpsimd", dgres[:, i * D:(i + 1) * D], gs[1][:, :],
            reads=[gs[0]], tag=f"gsp{i}")

    p.barrier()
    if "acc" in ddbg and run2:
        dma("sync", ddbg["acc"], acc[1][:, :], reads=[acc[0]], tag="dbg_acc")
        p.barrier()
    free_phase()
    for g in reversed(p12_ctx):
        g.__exit__(None, None, None)
    p12_ctx = []

    run3 = 3 in phases
    aout = alloc("aout", [128, ND * L], BF16, late=True)
    xnT = alloc("xnT", [128, ND * L], BF16)
    maskT = alloc("maskTS", [128, NH * C], F32)
    Gam = alloc("GamS", [128, NPAIR * C], F32)
    tri8 = alloc("tri8S", [128, 128], F32)
    gam16 = alloc("gam16S", [16, 1], F32)
    gam8 = alloc("gam8S", [128, NPAIR], F32)
    selg = alloc("selgS", [16, NPAIR * 128], F32)
    sel2 = alloc("sel2S", [16, 128], F32)
    m2t = alloc("m2S", [16, NPAIR], F32)
    r16 = alloc("r16S", [16, 128], F32)
    ri16 = alloc("ri16S", [16, 128], F32)
    ri2 = alloc("ri2S", [16, 128], F32)
    ri2T = alloc("ri2TS", [128, 16], F32)
    sck = alloc("sckS", [16, 1], F32)
    sck2 = alloc("sck2S", [16, 1], F32)
    tb = alloc("tbS", [16, 16], F32)
    lnG = alloc("lnGS", [16, 1], F32)
    trow = alloc("trowS", [16, 128], F32)
    tneg = alloc("tnegS", [16, 128], F32)
    t127 = alloc("t127S", [16, 128], F32)
    dma("sync", tri8[1][:, :], dtri8, writes=[tri8[0]], tag="c2")
    dma("sync", gam16[1][:, :], dgam16, writes=[gam16[0]], tag="c3")
    dma("sync", gam8[1][:, :], dgam8, writes=[gam8[0]], tag="c4")
    dma("sync", selg[1][:, :], dselg, writes=[selg[0]], tag="c4b")
    dma("sync", sel2[1][:, :], dsel2, writes=[sel2[0]], tag="c4c")
    dma("sync", m2t[1][:, :], dm2, writes=[m2t[0]], tag="c4d")
    dma("sync", trow[1][:, :], dtrow, writes=[trow[0]], tag="c4e")
    dma("sync", tneg[1][:, :], dtneg, writes=[tneg[0]], tag="c4f")
    dma("sync", t127[1][:, :], dt127, writes=[t127[0]], tag="c4g")

    p.add("scalar", lambda s: s.activation(
        out=lnG[1][:, :], in_=gam16[1][:, :], func=AF.Ln),
        reads=[gam16[0]], writes=[lnG[0]])
    p.add("vector", lambda v: v.tensor_tensor(
        out=r16[1][:, :], in0=trow[1][:, :],
        in1=lnG[1].ap()[:, 0:1].broadcast_to([16, 128]), op=ALU.mult),
        reads=[trow[0], lnG[0]], writes=[r16[0]])
    p.add("scalar", lambda s: s.activation(
        out=r16[1][:, :], in_=r16[1][:, :], func=AF.Exp),
        reads=[r16[0]], writes=[r16[0]])
    p.add("vector", lambda v: v.tensor_tensor(
        out=ri16[1][:, :], in0=tneg[1][:, :],
        in1=lnG[1].ap()[:, 0:1].broadcast_to([16, 128]), op=ALU.mult),
        reads=[tneg[0], lnG[0]], writes=[ri16[0]])
    p.add("scalar", lambda s: s.activation(
        out=ri16[1][:, :], in_=ri16[1][:, :], func=AF.Exp),
        reads=[ri16[0]], writes=[ri16[0]])
    p.add("vector", lambda v: v.tensor_tensor(
        out=ri2[1][:, :], in0=t127[1][:, :],
        in1=lnG[1].ap()[:, 0:1].broadcast_to([16, 128]), op=ALU.mult),
        reads=[t127[0], lnG[0]], writes=[ri2[0]])
    p.add("scalar", lambda s: s.activation(
        out=ri2[1][:, :], in_=ri2[1][:, :], func=AF.Exp),
        reads=[ri2[0]], writes=[ri2[0]])
    p.add("tensor", lambda t: t.transpose(
        pbank[0][1][:, 0:16], ri2[1][:, :], ident[1][0:16, 0:16]),
        reads=[ri2[0], ident[0]], writes=[pbank[0][0]])
    p.add("vector", lambda v: v.tensor_copy(out=ri2T[1][:, :], in_=pbank[0][1][:, 0:16]),
          reads=[pbank[0][0]], writes=[ri2T[0]])
    rflat = alloc("rflatS", [1, NH * 128], F32)
    riflat = alloc("riflatS", [1, NH * 128], F32)
    dma("sync", rflat[1].ap().rearrange("p (k t) -> p k t", k=16),
        r16[1].ap()[:, :], reads=[r16[0]], writes=[rflat[0]], tag="rf")
    dma("sync", riflat[1].ap().rearrange("p (k t) -> p k t", k=16),
        ri16[1].ap()[:, :], reads=[ri16[0]], writes=[riflat[0]], tag="rif")
    for h in range(NH):
        p.add("tensor", lambda t, h=h: t.matmul(
            pbank[0][1][:, 0:128], riflat[1][0:1, h * 128:(h + 1) * 128],
            rflat[1][0:1, h * 128:(h + 1) * 128], start=True, stop=True),
            reads=[riflat[0], rflat[0]], writes=[pbank[0][0]])
        p.add("vector", lambda v, h=h: v.tensor_tensor(
            out=maskT[1][:, h * 128:(h + 1) * 128], in0=pbank[0][1][:, 0:128],
            in1=tri8[1][:, :], op=ALU.mult),
            reads=[pbank[0][0], tri8[0]], writes=[maskT[0]])
    for hp in range(NPAIR):
        p.add("tensor", lambda t, hp=hp: t.matmul(
            pbank[1][1][:, 0:128], selg[1][:, hp * 128:(hp + 1) * 128],
            r16[1][:, :], start=True, stop=True),
            reads=[selg[0], r16[0]], writes=[pbank[1][0]])
        p.add("vector", lambda v, hp=hp: v.tensor_scalar(
            out=Gam[1][:, hp * 128:(hp + 1) * 128], in0=pbank[1][1][:, 0:128],
            scalar1=gam8[1][:, hp:hp + 1], scalar2=None, op0=ALU.mult),
            reads=[pbank[1][0], gam8[0]], writes=[Gam[0]])
    p.add("vector", lambda v: v.tensor_scalar(
        out=sck[1][:, :], in0=lnG[1][:, :], scalar1=128.0, scalar2=None,
        op0=ALU.mult), reads=[lnG[0]], writes=[sck[0]])
    p.add("scalar", lambda s: s.activation(
        out=sck2[1][:, :], in_=sck[1][:, :], func=AF.Exp),
        reads=[sck[0]], writes=[sck2[0]])
    p.add("vector", lambda v: v.tensor_tensor(
        out=tb[1][:, 0:NPAIR], in0=m2t[1][:, :],
        in1=sck2[1].ap()[:, 0:1].broadcast_to([16, NPAIR]), op=ALU.mult),
        reads=[m2t[0], sck2[0]], writes=[tb[0]])
    p.add("tensor", lambda t: t.matmul(
        pbank[0][1][:, 0:NPAIR], sel2[1][:, :], tb[1][:, 0:NPAIR],
        start=True, stop=True), reads=[sel2[0], tb[0]], writes=[pbank[0][0]])
    p.add("vector", lambda v: v.tensor_copy(out=gC[1][:, :], in_=pbank[0][1][:, 0:NPAIR]),
          reads=[pbank[0][0]], writes=[gC[0]])

    wqS = alloc("wqS", [128, ND * D], BF16)
    wkS = alloc("wkS", [128, ND * D], BF16)
    wvS = alloc("wvS", [128, ND * D], BF16)
    dma("sync", xnT[1][:, :], dxnT, writes=[xnT[0]], tag="xnT_rl")
    dma("gpsimd", wqS[1][:, :], dwq, writes=[wqS[0]], tag="wqS")
    dma("gpsimd", wkS[1][:, :], dwk, writes=[wkS[0]], tag="wkS")
    dma("gpsimd", wvS[1][:, :], dwv, writes=[wvS[0]], tag="wvS")

    QT = alloc("QT", [128, NPAIR * 512], BF16)
    Qg = alloc("Qg", [128, NPAIR * 512], BF16)
    KT = alloc("KT", [128, NPAIR * 512], BF16)
    Kd = alloc("Kd", [128, NPAIR * 512], BF16)
    Vs = alloc("Vs", [128, 4 * D], BF16)
    SA = [alloc("SA0", [128, 256], BF16), alloc("SA1", [128, 256], BF16)]
    Sst = [alloc(f"Sst{hp}", [128, 128], F32) for hp in range(NPAIR)]
    Sbf = [alloc(f"Sbf{hp}", [128, 128], BF16) for hp in range(NPAIR)]
    csb = alloc("csb", [128, 128], F32)
    sums = alloc("sums", [128, 2 * ND], F32)
    s1t = alloc("s1t", [128, 1], F32)
    s2t = alloc("s2t", [128, 1], F32)
    sqscr = alloc("sqscr", [128, 128], F32)
    reps = alloc("reps", [128, 16], F32)
    gnst = {k: alloc("gn" + k, [128, ND], F32)
            for k in ("mean", "ex2", "var", "rstd", "t1", "t2", "scalev", "biasv")}

    p.add("vector", lambda v: v.memset(sums[1][:, :], 0.0), writes=[sums[0]])
    for hp in range(NPAIR):
        p.add("vector", lambda v, hp=hp: v.memset(Sst[hp][1][:, :], 0.0),
              writes=[Sst[hp][0]])
        p.add("vector", lambda v, hp=hp: v.memset(Sbf[hp][1][:, :], 0.0),
              writes=[Sbf[hp][0]])

    qkvlvl = 5 if p3_qkv is True else int(p3_qkv)
    for g4 in range(p3_ngroups if run3 else 0):
        for hp in range(NPAIR if qkvlvl >= 1 else 0):
            bq = pbank[0]
            for kc in range(ND):
                p.add("tensor", lambda t, hp=hp, kc=kc, g4=g4: t.matmul(
                    bq[1][:, 0:512],
                    wqS[1][:, kc * D + hp * 128: kc * D + (hp + 1) * 128],
                    xnT[1][:, kc * L + g4 * 512: kc * L + (g4 + 1) * 512],
                    start=(kc == 0), stop=(kc == ND - 1)),
                    reads=[wqS[0], xnT[0]], writes=[bq[0]])
            p.add("scalar", lambda s, hp=hp: s.activation(
                out=QT[1][:, hp * 512:(hp + 1) * 512], in_=bq[1][:, 0:512],
                func=AF.Copy), reads=[bq[0]], writes=[QT[0]])
            if qkvlvl < 2:
                continue
            p.add("vector", lambda v, hp=hp: v.tensor_tensor(
                out=Qg[1].ap().rearrange("p (a t) -> p a t", t=128)[:, 4 * hp:4 * hp + 4, :],
                in0=bq[1].ap().rearrange("p (c t) -> p c t", t=128),
                in1=Gam[1].ap()[:, hp * 128:(hp + 1) * 128].unsqueeze(1).broadcast_to([128, 4, 128]),
                op=ALU.mult), reads=[bq[0], Gam[0]], writes=[Qg[0]])
            if qkvlvl < 3:
                continue
            bk = pbank[1]
            for kc in range(ND):
                p.add("tensor", lambda t, hp=hp, kc=kc, g4=g4: t.matmul(
                    bk[1][:, 0:512],
                    wkS[1][:, kc * D + hp * 128: kc * D + (hp + 1) * 128],
                    xnT[1][:, kc * L + g4 * 512: kc * L + (g4 + 1) * 512],
                    start=(kc == 0), stop=(kc == ND - 1)),
                    reads=[wkS[0], xnT[0]], writes=[bk[0]])
            p.add("scalar", lambda s, hp=hp: s.activation(
                out=KT[1][:, hp * 512:(hp + 1) * 512], in_=bk[1][:, 0:512],
                func=AF.Copy), reads=[bk[0]], writes=[KT[0]])
            if qkvlvl < 4:
                continue
            for c4 in range(4):
                p.add("tensor", lambda t, hp=hp, c4=c4: t.transpose(
                    pbbf[0][1][:, c4 * 128:(c4 + 1) * 128],
                    KT[1][:, hp * 512 + c4 * 128: hp * 512 + (c4 + 1) * 128],
                    identb[1][:, :]), reads=[KT[0], identb[0]], writes=[pbbf[0][0]])
            p.add("vector", lambda v, hp=hp: v.tensor_tensor(
                out=Kd[1].ap().rearrange("p (a h e) -> p a h e", h=2, e=64)[:, 4 * hp:4 * hp + 4, :, :],
                in0=pbbf[0][1].ap().rearrange("p (c h e) -> p c h e", h=2, e=64)[:, 0:4, :, :],
                in1=ri2T[1].ap()[:, 2 * hp:2 * hp + 2].unsqueeze(1).unsqueeze(3).broadcast_to([128, 4, 2, 64]),
                op=ALU.mult), reads=[pbbf[0][0], ri2T[0]], writes=[Kd[0]])
        for c4 in range(4 if qkvlvl >= 5 else 0):
            for half, bv in ((0, pbank[2]), (1, pbank[3])):
                for kc in range(ND):
                    p.add("tensor", lambda t, c4=c4, kc=kc, g4=g4, half=half, bv=bv: t.matmul(
                        bv[1][:, 0:512],
                        xnT[1][:, kc * L + g4 * 512 + c4 * 128: kc * L + g4 * 512 + (c4 + 1) * 128],
                        wvS[1][:, kc * D + half * 512: kc * D + (half + 1) * 512],
                        start=(kc == 0), stop=(kc == ND - 1)),
                        reads=[xnT[0], wvS[0]], writes=[bv[0]])
            p.add("scalar", lambda s, c4=c4: s.activation(
                out=Vs[1][:, c4 * D: c4 * D + 512], in_=pbank[2][1][:, 0:512],
                func=AF.Copy), reads=[pbank[2][0]], writes=[Vs[0]])
            p.add("scalar", lambda s, c4=c4: s.activation(
                out=Vs[1][:, c4 * D + 512: (c4 + 1) * D], in_=pbank[3][1][:, 0:512],
                func=AF.Copy), reads=[pbank[3][0]], writes=[Vs[0]])

        def emit_scores(hp, c4):
            bs = pbank[4 + (hp % 2)]
            p.add("tensor", lambda t, hp=hp, c4=c4, bs=bs: t.matmul(
                bs[1][:, 0:128],
                KT[1][0:64, hp * 512 + c4 * 128: hp * 512 + (c4 + 1) * 128],
                QT[1][0:64, hp * 512 + c4 * 128: hp * 512 + (c4 + 1) * 128],
                start=True, stop=True, tile_position=(0, 0)),
                reads=[KT[0], QT[0]], writes=[bs[0]])
            p.add("tensor", lambda t, hp=hp, c4=c4, bs=bs: t.matmul(
                bs[1][:, 128:256],
                KT[1][64:128, hp * 512 + c4 * 128: hp * 512 + (c4 + 1) * 128],
                QT[1][64:128, hp * 512 + c4 * 128: hp * 512 + (c4 + 1) * 128],
                start=True, stop=True, tile_position=(64, 0)),
                reads=[KT[0], QT[0]], writes=[bs[0]])

        for c4 in range(p3_nchunks):
            c = g4 * 4 + c4
            emit_scores(0, c4)
            for hp in range(NPAIR):
                bs = pbank[4 + (hp % 2)]
                sa = SA[hp % 2]
                p.add("vector", lambda v, hp=hp, bs=bs, sa=sa: v.tensor_tensor(
                    out=sa[1][:, 0:256], in0=bs[1][:, 0:256],
                    in1=maskT[1][:, (2 * hp) * 128: (2 * hp + 2) * 128],
                    op=ALU.mult), reads=[bs[0], maskT[0]], writes=[sa[0]],
                    wfree=256)
                if hp + 1 < NPAIR:
                    emit_scores(hp + 1, c4)
                bo = pbank[0]
                p.add("tensor", lambda t, hp=hp, c4=c4, bo=bo: t.matmul(
                    bo[1][:, 0:128], Sbf[hp][1][:, :],
                    Qg[1][:, hp * 512 + c4 * 128: hp * 512 + (c4 + 1) * 128],
                    start=True, stop=True),
                    reads=[Sbf[hp][0], Qg[0]], writes=[bo[0]])
                bi = pbank[1]
                p.add("tensor", lambda t, hp=hp, c4=c4, bi=bi, sa=sa: t.matmul(
                    bi[1][0:64, 0:128],
                    Vs[1][:, c4 * D + hp * 128: c4 * D + hp * 128 + 64],
                    sa[1][:, 0:128],
                    start=True, stop=True, tile_position=(0, 0)),
                    reads=[Vs[0], sa[0]], writes=[bi[0]])
                p.add("tensor", lambda t, hp=hp, c4=c4, bi=bi, sa=sa: t.matmul(
                    bi[1][64:128, 0:128],
                    Vs[1][:, c4 * D + hp * 128 + 64: c4 * D + (hp + 1) * 128],
                    sa[1][:, 128:256],
                    start=True, stop=True, tile_position=(0, 64)),
                    reads=[Vs[0], sa[0]], writes=[bi[0]])
                p.add("scalar", lambda s, bo=bo: s.activation(
                    out=csb[1][:, :], in_=bo[1][:, 0:128], func=AF.Copy),
                    reads=[bo[0]], writes=[csb[0]])
                p.add("vector", lambda v, bi=bi: v.tensor_tensor(
                    out=sqscr[1][:, :], in0=bi[1][0:128, 0:128], in1=csb[1][:, :],
                    op=ALU.add), reads=[bi[0], csb[0]], writes=[sqscr[0]],
                    wfree=128)
                p.add("vector", lambda v, hp=hp, c=c: v.tensor_scalar(
                    out=aout[1][:, hp * L + c * 128: hp * L + (c + 1) * 128],
                    in0=sqscr[1][:, :], scalar1=1.0, scalar2=0.0, op0=ALU.mult,
                    op1=ALU.add, accum_out=s1t[1][:, 0:1]),
                    reads=[sqscr[0]], writes=[aout[0], s1t[0]], hazard=True,
                    wfree=128)
                p.add("scalar", lambda s, hp=hp: s.activation(
                    out=sqscr[1][:, :], in_=sqscr[1][:, :], func=AF.Square,
                    accum_out=s2t[1][:, 0:1]),
                    reads=[sqscr[0]], writes=[sqscr[0], s2t[0]], hazard=True)
                p.add("vector", lambda v, hp=hp: v.tensor_tensor(
                    out=sums[1][:, hp:hp + 1], in0=sums[1][:, hp:hp + 1],
                    in1=s1t[1][:, :], op=ALU.add),
                    reads=[s1t[0], sums[0]], writes=[sums[0]])
                p.add("vector", lambda v, hp=hp: v.tensor_tensor(
                    out=sums[1][:, ND + hp: ND + hp + 1],
                    in0=sums[1][:, ND + hp: ND + hp + 1],
                    in1=s2t[1][:, :], op=ALU.add),
                    reads=[s2t[0], sums[0]], writes=[sums[0]])
                bst = pbank[2]
                p.add("tensor", lambda t, hp=hp, c4=c4, bst=bst: t.matmul(
                    bst[1][0:64, 0:64],
                    Kd[1][:, hp * 512 + c4 * 128: hp * 512 + c4 * 128 + 64],
                    Vs[1][:, c4 * D + hp * 128: c4 * D + hp * 128 + 64],
                    start=True, stop=True, tile_position=(0, 0)),
                    reads=[Kd[0], Vs[0]], writes=[bst[0]])
                p.add("tensor", lambda t, hp=hp, c4=c4, bst=bst: t.matmul(
                    bst[1][64:128, 64:128],
                    Kd[1][:, hp * 512 + c4 * 128 + 64: hp * 512 + (c4 + 1) * 128],
                    Vs[1][:, c4 * D + hp * 128 + 64: c4 * D + (hp + 1) * 128],
                    start=True, stop=True, tile_position=(0, 64)),
                    reads=[Kd[0], Vs[0]], writes=[bst[0]])
                p.add("vector", lambda v, hp=hp, bst=bst: v.scalar_tensor_tensor(
                    out=Sst[hp][1][0:64, 0:64], in0=Sst[hp][1][0:64, 0:64],
                    scalar=gC[1][0:64, hp:hp + 1], in1=bst[1][0:64, 0:64],
                    op0=ALU.mult, op1=ALU.add),
                    reads=[Sst[hp][0], gC[0], bst[0]], writes=[Sst[hp][0]])
                p.add("vector", lambda v, hp=hp, bst=bst: v.scalar_tensor_tensor(
                    out=Sst[hp][1][64:128, 64:128], in0=Sst[hp][1][64:128, 64:128],
                    scalar=gC[1][64:128, hp:hp + 1], in1=bst[1][64:128, 64:128],
                    op0=ALU.mult, op1=ALU.add),
                    reads=[Sst[hp][0], gC[0], bst[0]], writes=[Sst[hp][0]])
                p.add("vector", lambda v, hp=hp: v.tensor_copy(
                    out=Sbf[hp][1][:, :], in_=Sst[hp][1][:, :]),
                    reads=[Sst[hp][0]], writes=[Sbf[hp][0]])

    run3n = run3 and p3_norm
    if not run3:
        p.add("vector", lambda v: v.memset(aout[1][:, :], 0.0), writes=[aout[0]])
    if run3n or not run3:
      p.add("tensor", lambda t: t.matmul(
        pbank[0][1][:, 0:16], HHs[1][:, :], sums[1][:, :], start=True, stop=True),
        reads=[HHs[0], sums[0]], writes=[pbank[0][0]])
      p.add("vector", lambda v: v.tensor_copy(out=reps[1][:, :], in_=pbank[0][1][:, 0:16]),
          reads=[pbank[0][0]], writes=[reps[0]])
    NEL = float(64 * L)
    if run3n or not run3:
      p.add("vector", lambda v: v.tensor_scalar(
        out=gnst["mean"][1][:, :], in0=reps[1][:, 0:ND], scalar1=1.0 / NEL,
        scalar2=None, op0=ALU.mult), reads=[reps[0]], writes=[gnst["mean"][0]])
      p.add("vector", lambda v: v.tensor_scalar(
        out=gnst["ex2"][1][:, :], in0=reps[1][:, ND:2 * ND], scalar1=1.0 / NEL,
        scalar2=None, op0=ALU.mult), reads=[reps[0]], writes=[gnst["ex2"][0]])
      p.add("vector", lambda v: v.tensor_tensor(
        out=gnst["t1"][1][:, :], in0=gnst["mean"][1][:, :], in1=gnst["mean"][1][:, :],
        op=ALU.mult), reads=[gnst["mean"][0]], writes=[gnst["t1"][0]])
      p.add("vector", lambda v: v.tensor_tensor(
        out=gnst["var"][1][:, :], in0=gnst["ex2"][1][:, :], in1=gnst["t1"][1][:, :],
        op=ALU.subtract), reads=[gnst["ex2"][0], gnst["t1"][0]], writes=[gnst["var"][0]])
      p.add("vector", lambda v: v.tensor_scalar(
        out=gnst["var"][1][:, :], in0=gnst["var"][1][:, :], scalar1=1e-5,
        scalar2=None, op0=ALU.add), reads=[gnst["var"][0]], writes=[gnst["var"][0]])
      emit_rstd_newton(p, None, gnst["var"], gnst["rstd"], gnst["t1"], gnst["t2"])
      p.add("vector", lambda v: v.tensor_tensor(
        out=gnst["scalev"][1][:, :], in0=gnst["rstd"][1][:, :], in1=gnw[1][:, :],
        op=ALU.mult), reads=[gnst["rstd"][0], gnw[0]], writes=[gnst["scalev"][0]])
      p.add("vector", lambda v: v.tensor_tensor(
        out=gnst["t1"][1][:, :], in0=gnst["mean"][1][:, :], in1=gnst["scalev"][1][:, :],
        op=ALU.mult), reads=[gnst["mean"][0], gnst["scalev"][0]], writes=[gnst["t1"][0]])
      p.add("vector", lambda v: v.tensor_tensor(
        out=gnst["biasv"][1][:, :], in0=gnb[1][:, :], in1=gnst["t1"][1][:, :],
        op=ALU.subtract), reads=[gnb[0], gnst["t1"][0]], writes=[gnst["biasv"][0]])
      for dc in range(ND):
        p.add("scalar", lambda s, dc=dc: s.activation(
            out=aout[1][:, dc * L:(dc + 1) * L], in_=aout[1][:, dc * L:(dc + 1) * L],
            func=AF.Identity, scale=gnst["scalev"][1][:, dc:dc + 1],
            bias=gnst["biasv"][1][:, dc:dc + 1]),
            reads=[aout[0], gnst["scalev"][0], gnst["biasv"][0]], writes=[aout[0]])

    p.barrier()
    if "aout" in ddbg:
        dma("sync", ddbg["aout"], aout[1][:, :], reads=[aout[0]], tag="dbg_aout")
        p.barrier()
    free_phase()

    run4 = 4 in phases
    woS = alloc("woS", [128, ND * D], BF16)
    dma("gpsimd", woS[1][:, :], dwo, writes=[woS[0]], tag="woS")
    p4 = []
    for par in range(4):
        b = {}
        b["xc"] = alloc(f"p4x{par}", [128, D], F32)
        b["gt"] = alloc(f"p4g{par}", [128, D], BF16)
        b["sg1"] = alloc(f"p4s{par}", [128, D], F32)
        b["yt"] = alloc(f"p4y{par}", [128, D], F32)
        p4.append(b)

    for i in range(NT if run4 else 0):
        b = p4[i % 4]
        dma("sync", b["xc"][1][:, :], dx[i * 128:(i + 1) * 128, :],
            writes=[b["xc"][0]], tag=f"p4x{i}")
        dma("sync", b["gt"][1][:, :], dgres[:, i * D:(i + 1) * D],
            writes=[b["gt"][0]], tag=f"p4g{i}")
        bw1, bw2 = pbank[(i % 2) * 2], pbank[(i % 2) * 2 + 1]
        for half, bw in ((0, bw1), (1, bw2)):
            for dc in range(ND):
                p.add("tensor", lambda t, i=i, dc=dc, half=half, bw=bw: t.matmul(
                    bw[1][:, 0:512],
                    aout[1][:, dc * L + i * 128: dc * L + (i + 1) * 128],
                    woS[1][:, dc * D + half * 512: dc * D + (half + 1) * 512],
                    start=(dc == 0), stop=(dc == ND - 1)),
                    reads=[aout[0], woS[0]], writes=[bw[0]])
        for half, bw in ((0, bw1), (1, bw2)):
            p.add("vector", lambda v, b=b, i=i, half=half, bw=bw: v.tensor_tensor(
                out=b["sg1"][1][:, half * 512:(half + 1) * 512],
                in0=bw[1][:, 0:512],
                in1=b["gt"][1][:, half * 512:(half + 1) * 512],
                op=ALU.mult), reads=[bw[0], b["gt"][0]], writes=[b["sg1"][0]])
        p.add("vector", lambda v, b=b: v.tensor_tensor(
            out=b["yt"][1][:, :], in0=b["sg1"][1][:, :], in1=b["xc"][1][:, :],
            op=ALU.add), reads=[b["sg1"][0], b["xc"][0]], writes=[b["yt"][0]])
        dma("sync", dy[i * 128:(i + 1) * 128, :], b["yt"][1][:, :],
            reads=[b["yt"][0]], tag=f"p4y{i}")

    p.barrier()

    with nc.Block() as block:
        sem_ctx = p.lower(nc, block)

    for g in reversed(sem_ctx):
        g.__exit__(None, None, None)
    free_phase()
    for g in reversed(late_ctx):
        g.__exit__(None, None, None)
    for g in reversed(perm_ctx):
        g.__exit__(None, None, None)
    return nc


def _prep_inputs(inputs):
    x = np.ascontiguousarray(np.asarray(inputs["L0_attn_x"], dtype=np.float32))
    Wq = np.asarray(inputs["L0_attn_Wq"], dtype=np.float32)
    Wk = np.asarray(inputs["L0_attn_Wk"], dtype=np.float32)
    Wv = np.asarray(inputs["L0_attn_Wv"], dtype=np.float32)
    Wout = np.asarray(inputs["L0_attn_Wout"], dtype=np.float32)
    gnwv = np.asarray(inputs["L0_attn_gn_weight"], dtype=np.float32)
    gnbv = np.asarray(inputs["L0_attn_gn_bias"], dtype=np.float32)
    wg = np.ascontiguousarray(np.asarray(inputs["L0_ffn_w_gate"], dtype=np.float32))
    wu = np.ascontiguousarray(np.asarray(inputs["L0_ffn_w_up"], dtype=np.float32))
    wd = np.ascontiguousarray(np.asarray(inputs["L0_ffn_w_down"], dtype=np.float32))

    def he(W):
        a = W.transpose(1, 0, 2).reshape(D, D)
        return np.ascontiguousarray(
            a.reshape(ND, 128, D).transpose(1, 0, 2).reshape(128, ND * D))

    def rows(Wm):
        N = Wm.shape[1]
        return np.ascontiguousarray(
            Wm.reshape(ND, 128, N).transpose(1, 0, 2).reshape(128, ND * N))

    consts = host_consts()
    shared = {
        "wq_r": he(Wq), "wk_r": he(Wk), "wv_r": he(Wv), "wout_r": rows(Wout),
        "w_gate": wg, "w_up": wu, "w_down": wd,
        "gnw_r": np.ascontiguousarray(gnwv.reshape(ND, 128).T),
        "gnb_r": np.ascontiguousarray(gnbv.reshape(ND, 128).T),
        "ident": consts["ident"], "ident_bf": consts["ident_bf"],
        "tri8": consts["tri8"], "gam16": consts["gam16"],
        "gam8": consts["gam8"], "selg": consts["selg"],
        "sel2": consts["sel2"], "m2": consts["m2"], "HH": consts["HH"],
        "trow": consts["trow"], "tneg": consts["tneg"], "t127": consts["t127"],
        "qthr": consts["qthr"],
    }
    in_maps = []
    for b in range(8):
        m = dict(shared)
        m["x"] = np.ascontiguousarray(x[b])
        in_maps.append(m)
    return in_maps


def kernel(**inputs):
    if "nc" not in _CACHE:
        _CACHE["nc"] = build_nc()
    nc = _CACHE["nc"]
    in_maps = _prep_inputs(inputs)
    res = run_bass_kernel_spmd(nc, in_maps, core_ids=list(range(8)))
    return np.stack([np.asarray(res.results[b]["y"]) for b in range(8)], axis=0)


# version 19
# speedup vs baseline: 1.4926x; 1.0065x over previous
import numpy as np
import ml_dtypes

import concourse.bass as bass
import concourse.mybir as mybir
from concourse.bass_utils import run_bass_kernel_spmd

F32 = mybir.dt.float32
BF16 = mybir.dt.bfloat16
AF = mybir.ActivationFunctionType
ALU = mybir.AluOpType
AXL = mybir.AxisListType
BF = ml_dtypes.bfloat16

L = 2048
D = 1024
NH = 16
E = 64
I_FFN = 2816
C = 128
NT = L // C
ND = D // 128
NPAIR = NH // 2
NTILE_W = [512, 512, 512, 512, 512, 256]
BISECT_INLINE = True

_CACHE = {}


def _gammas():
    return np.array([1.0 - 2.0 ** (-5 - (h // 2)) for h in range(NH)], dtype=np.float64)


def host_consts():
    g = _gammas()
    t_ = np.arange(C)
    u_ = np.arange(C)
    tri8 = (np.where(t_[None, :] - u_[:, None] >= 0, 1.0, 0.0) / 8.0).astype(np.float32)
    gam16 = g.astype(np.float32).reshape(16, 1)
    gam8 = np.zeros((128, NPAIR), dtype=np.float32)
    for hp in range(NPAIR):
        gam8[0:64, hp] = np.float32(g[2 * hp] / 8.0)
        gam8[64:128, hp] = np.float32(g[2 * hp + 1] / 8.0)
    selg = np.zeros((16, NPAIR * 128), dtype=np.float32)
    for hp in range(NPAIR):
        selg[2 * hp, hp * 128: hp * 128 + 64] = 1.0
        selg[2 * hp + 1, hp * 128 + 64: (hp + 1) * 128] = 1.0
    sel2 = np.zeros((16, 128), dtype=np.float32)
    for k in range(16):
        sel2[k, :] = ((k % 2) == (np.arange(128) >= 64)).astype(np.float32)
    m2 = np.zeros((16, NPAIR), dtype=np.float32)
    for k in range(16):
        m2[k, k // 2] = 1.0
    HH = np.zeros((128, 128), dtype=np.float32)
    HH[0:64, 0:64] = 1.0
    HH[64:128, 64:128] = 1.0
    trow = np.tile(np.arange(128, dtype=np.float32)[None, :], (16, 1))
    qthr = np.zeros((128, 8), dtype=np.float32)
    qthr[:, 0:4] = -np.array([0.25, 0.75, 1.25, 1.75], dtype=np.float32)
    return dict(ident=np.eye(128, dtype=np.float32),
                ident_bf=np.eye(128, dtype=BF),
                tri8=tri8, gam16=gam16, gam8=gam8, selg=selg, sel2=sel2,
                m2=m2, HH=HH, trow=trow, tneg=-trow,
                t127=(127.0 - trow).astype(np.float32), qthr=qthr)


ENGINES = ("tensor", "vector", "scalar", "gpsimd", "sync")


class Buf:
    __slots__ = ("name", "writer", "readers", "psum")

    def __init__(self, name, psum=False):
        self.name = name
        self.writer = None
        self.readers = []
        self.psum = psum


class Op:
    __slots__ = ("oid", "engine", "emit", "deps", "is_dma", "dma_idx", "pos",
                 "hazard", "tag", "wfree", "dma_feed")

    def __init__(self, oid, engine, emit, deps, is_dma, hazard, tag, wfree):
        self.dma_feed = False
        self.oid = oid
        self.engine = engine
        self.emit = emit
        self.deps = deps
        self.is_dma = is_dma
        self.dma_idx = None
        self.pos = None
        self.hazard = hazard
        self.tag = tag
        self.wfree = wfree


class Prog:
    def __init__(self):
        self.ops = []
        self.streams = {e: [] for e in ENGINES}
        self.dma_counts = {e: 0 for e in ENGINES}
        self.last_dma = {}
        self.last_barrier = {}

    def add(self, engine, emit, reads=(), writes=(), dma=False, hazard=False,
            extra_deps=(), tag="", wfree=1, _is_barrier=False):
        deps = set(extra_deps)
        if not _is_barrier and engine in self.last_barrier:
            deps.add(self.last_barrier[engine])
        reads = list(reads)
        writes = list(writes)
        reads2 = []
        for b in reads:
            (writes if b.psum else reads2).append(b)
        reads = reads2
        for b in reads:
            if b.writer is not None:
                deps.add(b.writer)
        for b in writes:
            if b.writer is not None:
                deps.add(b.writer)
            deps.update(b.readers)
        oid = len(self.ops)
        if dma:
            for b in reads:
                if b.writer is not None:
                    self.ops[b.writer].dma_feed = True
        op = Op(oid, engine, emit, deps, dma, hazard, tag, wfree)
        self.ops.append(op)
        op.pos = len(self.streams[engine])
        self.streams[engine].append(op)
        if dma:
            op.dma_idx = self.dma_counts[engine]
            self.dma_counts[engine] += 1
            self.last_dma[engine] = oid
        for b in reads:
            b.readers.append(oid)
        for b in writes:
            b.writer = oid
            b.readers = []
        return oid

    def barrier(self):
        deps = set()
        for e in ENGINES:
            if self.streams[e]:
                deps.add(self.streams[e][-1].oid)
        deps.update(self.last_dma.values())
        new_bar = {}
        for e in ENGINES:
            new_bar[e] = self.add(e, None, extra_deps=deps, tag="barrier",
                                  _is_barrier=True)
        self.last_barrier = new_bar

    def lower(self, nc, block):
        wait_targets = {e: set() for e in ENGINES}
        for o in self.ops:
            for did in o.deps:
                d = self.ops[did]
                if d.is_dma:
                    continue
                if d.engine != o.engine:
                    wait_targets[d.engine].add(d.pos)
        inc_rank = {}
        for e in ENGINES:
            r = {}
            for i, pp in enumerate(sorted(wait_targets[e])):
                r[pp] = i + 1
            inc_rank[e] = r

        eng_sems = {}
        dma_sems = {}
        ctx = []
        for e in ENGINES:
            s = nc.semaphore(f"es_{e}")
            eng_sems[e] = s.__enter__()
            ctx.append(s)
            s2 = nc.semaphore(f"ds_{e}")
            dma_sems[e] = s2.__enter__()
            ctx.append(s2)

        def run_engine(e, eng):
            prev = None
            for o in self.streams[e]:
                if (prev is not None and not prev.is_dma and not o.is_dma
                        and prev.wfree < 1000000 and not prev.hazard
                        and prev.oid in o.deps and o.emit is not None
                        and prev.emit is not None):
                    if e in ("vector", "gpsimd"):
                        eng.engine_nop()
                    else:
                        eng.nop()
                prev = o
                waits = {}
                for did in sorted(o.deps):
                    d = self.ops[did]
                    if d.is_dma:
                        key = ("dma", d.engine)
                        val = 16 * (d.dma_idx + 1)
                    elif d.engine == e:
                        continue
                    else:
                        key = ("eng", d.engine)
                        val = inc_rank[d.engine][d.pos]
                    if waits.get(key, 0) < val:
                        waits[key] = val
                for (kind, se), val in sorted(waits.items()):
                    sem = dma_sems[se] if kind == "dma" else eng_sems[se]
                    eng.wait_ge(sem, val)
                if o.emit is None:
                    inst = eng.drain()
                    if o.pos in inc_rank[e]:
                        inst.then_inc(eng_sems[e], 1)
                    continue
                inst = o.emit(eng)
                if o.is_dma:
                    inst.then_inc(dma_sems[e], 16)
                elif o.hazard or o.dma_feed:
                    dr = eng.drain()
                    if o.pos in inc_rank[e]:
                        dr.then_inc(eng_sems[e], 1)
                elif o.pos in inc_rank[e]:
                    inst.then_inc(eng_sems[e], 1)

        for e in ENGINES:
            getattr(block, e)(lambda eng, e=e: run_engine(e, eng))
        return ctx


def emit_quant_multi(p, items, thr, tag=""):
    def blk(a):
        return a.rearrange("p (b k) -> p b k", k=16)

    if len(items) >= 2:
        segs = [(it, 0, it["w"] // 16) for it in items]
    else:
        it0 = items[0]
        h = it0["w"] // 32
        segs = [(it0, 0, h), (it0, h, it0["w"] // 16)]
    for it, c0, c1 in segs:
        p.add("vector", lambda v, it=it, c0=c0, c1=c1: v.tensor_reduce(
            out=it["scr"]["bm"][1][:, c0:c1],
            in_=blk(it["src_ap"]())[:, c0:c1, :],
            axis=AXL.X, op=ALU.max, apply_absolute_value=True),
            reads=[it["src_buf"]], writes=[it["scr"]["bm"][0]],
            tag=tag + "bm", wfree=64)
    for it, c0, c1 in segs:
        p.add("vector", lambda v, it=it, c0=c0, c1=c1: v.tensor_scalar(
            out=it["scr"]["s6"][1][:, c0:c1], in0=it["scr"]["bm"][1][:, c0:c1],
            scalar1=float(1.0 / 6.0), scalar2=1e-12, op0=ALU.mult, op1=ALU.max),
            reads=[it["scr"]["bm"][0]], writes=[it["scr"]["s6"][0]],
            tag=tag + "s6", wfree=64)
    for it in items:
        p.add("scalar", lambda s, it=it: s.activation(
            out=it["scr"]["ab"][1][:, 0:it["w"]], in_=it["src_ap"](),
            func=AF.Abs), reads=[it["src_buf"]], writes=[it["scr"]["ab"][0]],
            tag=tag + "ab")
    for it, c0, c1 in segs:
        p.add("vector", lambda v, it=it, c0=c0, c1=c1: v.reciprocal(
            out=it["scr"]["si"][1][:, c0:c1], in_=it["scr"]["s6"][1][:, c0:c1]),
            reads=[it["scr"]["s6"][0]], writes=[it["scr"]["si"][0]],
            tag=tag + "si")
    for it in items:
        p.add("scalar", lambda s, it=it: s.activation(
            out=it["scr"]["sg"][1][:, 0:it["w"]], in_=it["src_ap"](),
            func=AF.Sign), reads=[it["src_buf"]], writes=[it["scr"]["sg"][0]],
            tag=tag + "sg")
    for it, c0, c1 in segs:
        p.add("vector", lambda v, it=it, c0=c0, c1=c1: v.tensor_tensor(
            out=blk(it["scr"]["z"][1].ap()[:, 0:it["w"]])[:, c0:c1, :],
            in0=blk(it["scr"]["ab"][1].ap()[:, 0:it["w"]])[:, c0:c1, :],
            in1=it["scr"]["si"][1].ap()[:, c0:c1].unsqueeze(2).broadcast_to([128, c1 - c0, 16]),
            op=ALU.mult),
            reads=[it["scr"]["ab"][0], it["scr"]["si"][0]],
            writes=[it["scr"]["z"][0]], tag=tag + "z", wfree=it["w"])
    for k in range(4):
        for it in items:
            p.add("scalar", lambda s, it=it, k=k: s.activation(
                out=it["scr"][f"s{k + 1}"][1][:, 0:it["w"]],
                in_=it["scr"]["z"][1][:, 0:it["w"]], func=AF.Sign,
                bias=thr[1][:, k:k + 1], scale=1.0),
                reads=[it["scr"]["z"][0], thr[0]],
                writes=[it["scr"][f"s{k + 1}"][0]], tag=tag + f"lo{k}")
    for nm, mid in (("iA", 2.5), ("iB", 3.5), ("iC", 5.0)):
        for it in items:
            p.add("vector", lambda v, it=it, nm=nm, mid=mid: v.tensor_scalar(
                out=it["scr"][nm][1][:, 0:it["w"]],
                in0=it["scr"]["z"][1][:, 0:it["w"]], scalar1=mid,
                scalar2=None, op0=ALU.is_gt),
                reads=[it["scr"]["z"][0]], writes=[it["scr"][nm][0]],
                tag=tag + nm, wfree=it["w"])
    for other in ("s2", "s3", "s4"):
        for it in items:
            p.add("vector", lambda v, it=it, other=other: v.tensor_tensor(
                out=it["scr"]["s1"][1][:, 0:it["w"]],
                in0=it["scr"]["s1"][1][:, 0:it["w"]],
                in1=it["scr"][other][1][:, 0:it["w"]], op=ALU.add),
                reads=[it["scr"]["s1"][0], it["scr"][other][0]],
                writes=[it["scr"]["s1"][0]], tag=tag + "A" + other,
                wfree=it["w"])
    for it in items:
        p.add("vector", lambda v, it=it: v.tensor_tensor(
            out=it["scr"]["iA"][1][:, 0:it["w"]],
            in0=it["scr"]["iA"][1][:, 0:it["w"]],
            in1=it["scr"]["iB"][1][:, 0:it["w"]], op=ALU.add),
            reads=[it["scr"]["iA"][0], it["scr"]["iB"][0]],
            writes=[it["scr"]["iA"][0]], tag=tag + "dAB", wfree=it["w"])
    for it in items:
        p.add("vector", lambda v, it=it: v.scalar_tensor_tensor(
            out=it["scr"]["iA"][1][:, 0:it["w"]],
            in0=it["scr"]["iC"][1][:, 0:it["w"]], scalar=2.0,
            in1=it["scr"]["iA"][1][:, 0:it["w"]], op0=ALU.mult, op1=ALU.add),
            reads=[it["scr"]["iC"][0], it["scr"]["iA"][0]],
            writes=[it["scr"]["iA"][0]], tag=tag + "dC", wfree=it["w"])
    for it in items:
        p.add("vector", lambda v, it=it: v.scalar_tensor_tensor(
            out=it["scr"]["s2"][1][:, 0:it["w"]],
            in0=it["scr"]["s1"][1][:, 0:it["w"]], scalar=0.25,
            in1=it["scr"]["iA"][1][:, 0:it["w"]], op0=ALU.mult, op1=ALU.add),
            reads=[it["scr"]["s1"][0], it["scr"]["iA"][0]],
            writes=[it["scr"]["s2"][0]], tag=tag + "u", wfree=it["w"])
    for it in items:
        p.add("vector", lambda v, it=it: v.scalar_tensor_tensor(
            out=it["scr"]["s3"][1][:, 0:it["w"]],
            in0=it["scr"]["s2"][1][:, 0:it["w"]], scalar=1.0,
            in1=it["scr"]["sg"][1][:, 0:it["w"]], op0=ALU.add, op1=ALU.mult),
            reads=[it["scr"]["s2"][0], it["scr"]["sg"][0]],
            writes=[it["scr"]["s3"][0]], tag=tag + "tn", wfree=it["w"])
    for it in items:
        nb = it["w"] // 16
        p.add("vector", lambda v, it=it, nb=nb: v.tensor_tensor(
            out=blk(it["dst_ap"]()),
            in0=blk(it["scr"]["s3"][1].ap()[:, 0:it["w"]]),
            in1=it["scr"]["s6"][1].ap()[:, 0:nb].unsqueeze(2).broadcast_to([128, nb, 16]),
            op=ALU.mult),
            reads=[it["scr"]["s3"][0], it["scr"]["s6"][0]],
            writes=[it["dst_buf"]], tag=tag + "y", wfree=it["w"])


def emit_quant(p, src_buf, src_ap, w, scr, dst_buf, dst_ap, tag="", thr=None):
    emit_quant_multi(p, [dict(src_buf=src_buf, src_ap=src_ap, w=w, scr=scr,
                              dst_buf=dst_buf, dst_ap=dst_ap)], thr, tag=tag)


def quant_scratch(alloc, prefix, w):
    scr = {}
    scr["ab"] = alloc(prefix + "ab", [128, w], F32)
    scr["z"] = alloc(prefix + "z", [128, w], F32)
    for nm in ("sg", "s1", "s2", "s3", "s4", "iA", "iB", "iC"):
        scr[nm] = alloc(prefix + nm, [128, w], BF16)
    for nm in ("bm", "s6", "si"):
        scr[nm] = alloc(prefix + nm, [128, w // 16], F32)
    return scr


def emit_rstd_newton(p, st, src_var, dst_rstd, t1, t2):
    p.add("scalar", lambda s: s.activation(
        out=t1[1][:, :], in_=src_var[1][:, :], func=AF.Sqrt),
        reads=[src_var[0]], writes=[t1[0]])
    p.add("vector", lambda v: v.reciprocal(
        out=dst_rstd[1][:, :], in_=t1[1][:, :]),
        reads=[t1[0]], writes=[dst_rstd[0]], hazard=True)
    p.add("vector", lambda v: v.tensor_tensor(
        out=t1[1][:, :], in0=dst_rstd[1][:, :], in1=dst_rstd[1][:, :],
        op=ALU.mult), reads=[dst_rstd[0]], writes=[t1[0]])
    p.add("vector", lambda v: v.tensor_tensor(
        out=t2[1][:, :], in0=t1[1][:, :], in1=src_var[1][:, :], op=ALU.mult),
        reads=[t1[0], src_var[0]], writes=[t2[0]])
    p.add("vector", lambda v: v.tensor_scalar(
        out=t2[1][:, :], in0=t2[1][:, :], scalar1=-0.5, scalar2=1.5,
        op0=ALU.mult, op1=ALU.add), reads=[t2[0]], writes=[t2[0]])
    p.add("vector", lambda v: v.tensor_tensor(
        out=dst_rstd[1][:, :], in0=dst_rstd[1][:, :], in1=t2[1][:, :],
        op=ALU.mult), reads=[dst_rstd[0], t2[0]], writes=[dst_rstd[0]])


def build_nc(phases=(1, 2, 3, 4), debug=(), p3_ngroups=4, p3_nchunks=4, p3_qkv=True, p3_norm=True):
    nc = bass.Bass("TRN2", target_bir_lowering=False, detect_race_conditions=False)

    dx = nc.dram_tensor("x", [L, D], F32, kind="ExternalInput").ap()
    dwq = nc.dram_tensor("wq_r", [128, ND * D], F32, kind="ExternalInput").ap()
    dwk = nc.dram_tensor("wk_r", [128, ND * D], F32, kind="ExternalInput").ap()
    dwv = nc.dram_tensor("wv_r", [128, ND * D], F32, kind="ExternalInput").ap()
    dwo = nc.dram_tensor("wout_r", [128, ND * D], F32, kind="ExternalInput").ap()
    dwg = nc.dram_tensor("w_gate", [I_FFN, D], F32, kind="ExternalInput").ap()
    dwu = nc.dram_tensor("w_up", [I_FFN, D], F32, kind="ExternalInput").ap()
    dwd = nc.dram_tensor("w_down", [D, I_FFN], F32, kind="ExternalInput").ap()
    dgnw = nc.dram_tensor("gnw_r", [128, ND], F32, kind="ExternalInput").ap()
    dgnb = nc.dram_tensor("gnb_r", [128, ND], F32, kind="ExternalInput").ap()
    dident = nc.dram_tensor("ident", [128, 128], F32, kind="ExternalInput").ap()
    didentb = nc.dram_tensor("ident_bf", [128, 128], BF16, kind="ExternalInput").ap()
    dtri8 = nc.dram_tensor("tri8", [128, 128], F32, kind="ExternalInput").ap()
    dgam16 = nc.dram_tensor("gam16", [16, 1], F32, kind="ExternalInput").ap()
    dgam8 = nc.dram_tensor("gam8", [128, NPAIR], F32, kind="ExternalInput").ap()
    dselg = nc.dram_tensor("selg", [16, NPAIR * 128], F32, kind="ExternalInput").ap()
    dsel2 = nc.dram_tensor("sel2", [16, 128], F32, kind="ExternalInput").ap()
    dm2 = nc.dram_tensor("m2", [16, NPAIR], F32, kind="ExternalInput").ap()
    dtrow = nc.dram_tensor("trow", [16, 128], F32, kind="ExternalInput").ap()
    dtneg = nc.dram_tensor("tneg", [16, 128], F32, kind="ExternalInput").ap()
    dt127 = nc.dram_tensor("t127", [16, 128], F32, kind="ExternalInput").ap()
    dhh = nc.dram_tensor("HH", [128, 128], F32, kind="ExternalInput").ap()
    dqthr = nc.dram_tensor("qthr", [128, 8], F32, kind="ExternalInput").ap()
    dy = nc.dram_tensor("y", [L, D], F32, kind="ExternalOutput").ap()
    dxnT = nc.dram_tensor("xnT_sp", [128, ND * L], BF16, kind="Internal").ap()
    dgres = nc.dram_tensor("gres_sp", [128, NT * D], BF16, kind="Internal").ap()
    ddbg = {}
    if "xqT" in debug:
        ddbg["xqT"] = nc.dram_tensor("dbg_xqT", [128, ND * L], BF16, kind="ExternalOutput").ap()
    if "xnT" in debug:
        ddbg["xnT"] = nc.dram_tensor("dbg_xnT", [128, ND * L], BF16, kind="ExternalOutput").ap()
    if "acc" in debug:
        ddbg["acc"] = nc.dram_tensor("dbg_acc", [128, NT * D], F32, kind="ExternalOutput").ap()
    if "aout" in debug:
        ddbg["aout"] = nc.dram_tensor("dbg_aout", [128, ND * L], BF16, kind="ExternalOutput").ap()

    p = Prog()
    perm_ctx = []
    late_ctx = []
    p12_ctx = []
    phase_ctx = []

    def alloc(name, shape, dtype, perm=False, late=False, p12=False):
        g = nc.sbuf_tensor(name, shape, dtype)
        t = g.__enter__()
        (perm_ctx if perm else (late_ctx if late else
         (p12_ctx if p12 else phase_ctx))).append(g)
        return (Buf(name), t)

    def palloc(name, shape, dtype):
        g = nc.psum_tensor(name, shape, dtype)
        t = g.__enter__()
        perm_ctx.append(g)
        return (Buf(name, psum=True), t)

    def free_phase():
        nonlocal phase_ctx
        for g in reversed(phase_ctx):
            g.__exit__(None, None, None)
        phase_ctx = []

    def dma(engine, out_ap, in_ap, reads=(), writes=(), tag=""):
        def f(eng, out_ap=out_ap, in_ap=in_ap):
            return eng.dma_start(out=out_ap, in_=in_ap)
        return p.add(engine, f, reads=reads, writes=writes, dma=True, tag=tag)

    ident = alloc("identS", [128, 128], F32, perm=True)
    identb = alloc("identbS", [128, 128], BF16, perm=True)
    gC = alloc("gCS", [128, NPAIR], F32, perm=True)
    HHs = alloc("HHS", [128, 128], F32, perm=True)
    gnw = alloc("gnwS", [128, ND], F32, perm=True)
    gnb = alloc("gnbS", [128, ND], F32, perm=True)
    qthr = alloc("qthrS", [128, 8], F32, perm=True)
    xqT = alloc("xqT", [128, ND * L], BF16, p12=True)

    pbank = [palloc(f"pb{i}", [128, 512], F32) for i in range(6)]
    pbbf = [palloc(f"pbb{i}", [128, 1024], BF16) for i in range(2)]

    dma("sync", ident[1][:, :], dident, writes=[ident[0]], tag="c0")
    dma("sync", identb[1][:, :], didentb, writes=[identb[0]], tag="c1")
    dma("sync", HHs[1][:, :], dhh, writes=[HHs[0]], tag="c6")
    dma("sync", gnw[1][:, :], dgnw, writes=[gnw[0]], tag="c7")
    dma("sync", gnb[1][:, :], dgnb, writes=[gnb[0]], tag="c8")
    dma("sync", qthr[1][:, :], dqthr, writes=[qthr[0]], tag="c9")

    xfull = alloc("xfull", [128, NT * D], F32)
    sums16 = alloc("sums16", [128, NT], F32)
    sqs16 = alloc("sqs16", [128, NT], F32)
    mean16 = alloc("mean16", [128, NT], F32)
    msq16 = alloc("msq16", [128, NT], F32)
    var16 = alloc("var16", [128, NT], F32)
    rstd16 = alloc("rstd16", [128, NT], F32)
    negm16 = alloc("negm16", [128, NT], F32)
    nt1 = alloc("nt1", [128, NT], F32)
    nt2 = alloc("nt2", [128, NT], F32)
    sqscrA = alloc("sqscrA", [128, D], F32)
    ph1 = []
    for par in range(2):
        b = {}
        b["xn"] = alloc(f"xn{par}", [128, D], F32)
        b["xnb"] = alloc(f"xnb{par}", [128, D], BF16)
        b["xq"] = alloc(f"p1xq{par}", [128, D], BF16)
        b["xntS"] = alloc(f"p1xnt{par}", [128, D], BF16)
        b.update(quant_scratch(alloc, f"p1s{par}", D))
        ph1.append(b)

    dma("sync", xfull[1].ap().rearrange("p (i d) -> p i d", d=D),
        dx.rearrange("(i p) d -> p i d", p=128), writes=[xfull[0]], tag="p1x")
    for i in range(NT):
        p.add("vector", lambda v, i=i: v.tensor_reduce(
            out=sums16[1][:, i:i + 1], in_=xfull[1][:, i * D:(i + 1) * D],
            axis=AXL.X, op=ALU.add),
            reads=[xfull[0]], writes=[sums16[0]], hazard=(i == NT - 1))
        p.add("scalar", lambda s, i=i: s.activation(
            out=sqscrA[1][:, :], in_=xfull[1][:, i * D:(i + 1) * D],
            func=AF.Square, accum_out=sqs16[1][:, i:i + 1]),
            reads=[xfull[0]], writes=[sqscrA[0], sqs16[0]],
            hazard=(i == NT - 1))
    p.add("vector", lambda v: v.tensor_scalar(
        out=mean16[1][:, :], in0=sums16[1][:, :], scalar1=float(1.0 / D),
        scalar2=None, op0=ALU.mult), reads=[sums16[0]], writes=[mean16[0]])
    p.add("vector", lambda v: v.tensor_tensor(
        out=msq16[1][:, :], in0=mean16[1][:, :], in1=mean16[1][:, :],
        op=ALU.mult), reads=[mean16[0]], writes=[msq16[0]])
    p.add("vector", lambda v: v.scalar_tensor_tensor(
        out=var16[1][:, :], in0=sqs16[1][:, :], scalar=float(1.0 / D),
        in1=msq16[1][:, :], op0=ALU.mult, op1=ALU.subtract),
        reads=[sqs16[0], msq16[0]], writes=[var16[0]])
    p.add("vector", lambda v: v.tensor_scalar(
        out=var16[1][:, :], in0=var16[1][:, :], scalar1=1e-5,
        scalar2=None, op0=ALU.add), reads=[var16[0]], writes=[var16[0]])
    emit_rstd_newton(p, None, var16, rstd16, nt1, nt2)
    p.add("vector", lambda v: v.tensor_tensor(
        out=negm16[1][:, :], in0=mean16[1][:, :], in1=rstd16[1][:, :],
        op=ALU.mult), reads=[mean16[0], rstd16[0]], writes=[negm16[0]])
    p.add("vector", lambda v: v.tensor_scalar(
        out=negm16[1][:, :], in0=negm16[1][:, :], scalar1=-1.0,
        scalar2=None, op0=ALU.mult), reads=[negm16[0]], writes=[negm16[0]])

    for i0 in range(0, NT, 2):
        items = []
        for i in (i0, i0 + 1):
            b = ph1[i % 2]
            p.add("scalar", lambda s, b=b, i=i: s.activation(
                out=b["xn"][1][:, :], in_=xfull[1][:, i * D:(i + 1) * D],
                func=AF.Identity, scale=rstd16[1][:, i:i + 1],
                bias=negm16[1][:, i:i + 1]),
                reads=[xfull[0], rstd16[0], negm16[0]], writes=[b["xn"][0]])
            items.append(dict(
                src_buf=b["xn"][0], src_ap=(lambda b=b: b["xn"][1].ap()[:, :]),
                w=D, scr=b, dst_buf=b["xq"][0],
                dst_ap=(lambda b=b: b["xq"][1].ap()[:, :])))
        emit_quant_multi(p, items, qthr, tag=f"p1q{i0}_")
        for i in (i0, i0 + 1):
            b = ph1[i % 2]
            p.add("vector", lambda v, b=b: v.tensor_copy(
                out=b["xnb"][1][:, :], in_=b["xn"][1][:, :]),
                reads=[b["xn"][0]], writes=[b["xnb"][0]], wfree=D)
            pbq = pbbf[i % 2]
            for dc in range(ND):
                p.add("tensor", lambda t, b=b, dc=dc, pbq=pbq: t.transpose(
                    pbq[1][:, dc * 128:(dc + 1) * 128],
                    b["xq"][1][:, dc * 128:(dc + 1) * 128], identb[1][:, :]),
                    reads=[b["xq"][0], identb[0]], writes=[pbq[0]])
            p.add("scalar", lambda s, i=i, pbq=pbq: s.activation(
                out=xqT[1].ap().rearrange("p (dc t) -> p dc t", t=L)[:, :, i * 128:(i + 1) * 128],
                in_=pbq[1].ap().rearrange("p (dc t) -> p dc t", t=128),
                func=AF.Copy), reads=[pbq[0]], writes=[xqT[0]])
            for dc in range(ND):
                p.add("tensor", lambda t, b=b, dc=dc, pbq=pbq: t.transpose(
                    pbq[1][:, dc * 128:(dc + 1) * 128],
                    b["xnb"][1][:, dc * 128:(dc + 1) * 128], identb[1][:, :]),
                    reads=[b["xnb"][0], identb[0]], writes=[pbq[0]])
            p.add("scalar", lambda s, b=b, pbq=pbq: s.activation(
                out=b["xntS"][1][:, :], in_=pbq[1][:, :], func=AF.Copy),
                reads=[pbq[0]], writes=[b["xntS"][0]])
            dma("gpsimd",
                dxnT.rearrange("p (dc i t) -> p dc i t", dc=ND, i=NT)[:, :, i, :],
                b["xntS"][1].ap().rearrange("p (dc t) -> p dc t", t=128),
                reads=[b["xntS"][0]], tag=f"p1sp{i}")

    p.barrier()
    if "xqT" in ddbg:
        dma("sync", ddbg["xqT"], xqT[1][:, :], reads=[xqT[0]], tag="dbg_xqT")
    if "xnT" in ddbg:
        dma("sync", ddbg["xnT"], dxnT, tag="dbg_xnT")
    free_phase()

    run2 = 2 in phases
    acc = alloc("accF", [128, NT * D], F32)
    wstg2 = [alloc(f"wstg{k}", [128, D], F32) for k in range(2)]
    wq_nat = alloc("wq_nat", [128, 12 * D], BF16)
    wgqT = alloc("wgqT", [128, ND * 512], BF16)
    wuqT = alloc("wuqT", [128, ND * 512], BF16)
    wdqT = alloc("wdqT", [128, 4 * D], BF16)
    f2 = quant_scratch(alloc, "f2", 512)
    g2 = [quant_scratch(alloc, f"g2{k}", 512) for k in range(2)]
    hb = []
    for par in range(2):
        hh = {}
        hh["hs"] = alloc(f"hs{par}", [128, 512], F32)
        hh["h"] = alloc(f"h{par}", [128, 512], F32)
        hh["hq"] = alloc(f"hq{par}", [128, 512], BF16)
        hh["hqT"] = alloc(f"hqT{par}", [128, 512], BF16)
        hb.append(hh)
    gstg = [alloc(f"gstg{par}", [128, D], BF16) for par in range(2)]

    NT_OFF = [0]
    for W in NTILE_W:
        NT_OFF.append(NT_OFF[-1] + W)

    def prep_tile(j):
        W = NTILE_W[j]
        CW = W // 128
        n0 = NT_OFF[j]
        load_fns = []
        witems = []
        for slot in range(3 * CW):
            kind, sl = divmod(slot, CW)
            stg = wstg2[slot % 2]

            def load(kind=kind, sl=sl, stg=stg, j=j):
                if kind < 2:
                    dsrc = dwg if kind == 0 else dwu
                    dma("gpsimd", stg[1][:, :],
                        dsrc.rearrange("(a p) k -> p a k", p=128)[:, (n0 // 128) + sl:(n0 // 128) + sl + 1, :].rearrange("p a k -> p (a k)"),
                        writes=[stg[0]], tag=f"w{kind}_{j}_{sl}")
                else:
                    ndc = D // W
                    dma("gpsimd",
                        stg[1].ap().rearrange("p (dc n) -> p dc n", n=W),
                        dwd.rearrange("(dc p) n -> p dc n", p=128)[:, sl * ndc:(sl + 1) * ndc, n0:n0 + W],
                        writes=[stg[0]], tag=f"wd_{j}_{sl}")

            load_fns.append(load)
            so = slot
            for hf in range(2):
                witems.append(dict(
                    src_buf=stg[0],
                    src_ap=(lambda stg=stg, hf=hf: stg[1].ap()[:, hf * 512:(hf + 1) * 512]),
                    w=512, scr=g2[hf], dst_buf=wq_nat[0],
                    dst_ap=(lambda so=so, hf=hf: wq_nat[1].ap()[:, so * D + hf * 512: so * D + (hf + 1) * 512])))
        for k in range(min(2, len(load_fns))):
            load_fns[k]()
        return load_fns, witems

    def trans_tile(j):
        W = NTILE_W[j]
        CW = W // 128
        for kind, dstT in ((0, wgqT), (1, wuqT)):
            for nc4 in range(CW):
                for kc in range(ND):
                    p.add("tensor", lambda t, kind=kind, nc4=nc4, kc=kc, CW=CW: t.transpose(
                        pbbf[0][1][:, kc * 128:(kc + 1) * 128],
                        wq_nat[1][:, (kind * CW + nc4) * D + kc * 128: (kind * CW + nc4) * D + (kc + 1) * 128],
                        identb[1][:, :]), reads=[wq_nat[0], identb[0]],
                        writes=[pbbf[0][0]])
                p.add("scalar", lambda s, dstT=dstT, nc4=nc4: s.activation(
                    out=dstT[1].ap().rearrange("p (kc n) -> p kc n", n=512)[:, :, nc4 * 128:(nc4 + 1) * 128],
                    in_=pbbf[0][1].ap().rearrange("p (kc t) -> p kc t", t=128),
                    func=AF.Copy), reads=[pbbf[0][0]], writes=[dstT[0]])
        dbase = 2 * CW * D
        for nc4 in range(CW):
            for dc in range(ND):
                p.add("tensor", lambda t, nc4=nc4, dc=dc, W=W, dbase=dbase: t.transpose(
                    pbbf[0][1][:, dc * 128:(dc + 1) * 128],
                    wq_nat[1][:, dbase + dc * W + nc4 * 128: dbase + dc * W + (nc4 + 1) * 128],
                    identb[1][:, :]), reads=[wq_nat[0], identb[0]],
                    writes=[pbbf[0][0]])
            p.add("scalar", lambda s, nc4=nc4: s.activation(
                out=wdqT[1][:, nc4 * D:(nc4 + 1) * D], in_=pbbf[0][1][:, :],
                func=AF.Copy), reads=[pbbf[0][0]], writes=[wdqT[0]])

    def t_loop(j, load_fns=(), witems=()):
        W = NTILE_W[j]
        CW = W // 128

        def emit_gateup(i):
            bg, bu = pbank[(i % 2) * 2], pbank[(i % 2) * 2 + 1]
            for kc in range(ND):
                p.add("tensor", lambda t, kc=kc, i=i, W=W, bg=bg: t.matmul(
                    bg[1][:, 0:W],
                    xqT[1][:, kc * L + i * 128: kc * L + (i + 1) * 128],
                    wgqT[1][:, kc * 512: kc * 512 + W],
                    start=(kc == 0), stop=(kc == ND - 1)),
                    reads=[xqT[0], wgqT[0]], writes=[bg[0]])
            for kc in range(ND):
                p.add("tensor", lambda t, kc=kc, i=i, W=W, bu=bu: t.matmul(
                    bu[1][:, 0:W],
                    xqT[1][:, kc * L + i * 128: kc * L + (i + 1) * 128],
                    wuqT[1][:, kc * 512: kc * 512 + W],
                    start=(kc == 0), stop=(kc == ND - 1)),
                    reads=[xqT[0], wuqT[0]], writes=[bu[0]])

        if not BISECT_INLINE:
            for k in range(0, len(witems), 2):
                emit_quant_multi(p, witems[k:k + 2], qthr, tag=f"wp{j}_{k}_")
                if k // 2 + 2 < len(load_fns):
                    load_fns[k // 2 + 2]()
        emit_gateup(0)
        for i in range(NT):
            hh = hb[i % 2]
            bg, bu = pbank[(i % 2) * 2], pbank[(i % 2) * 2 + 1]
            bd1, bd2 = pbank[4], pbank[5]
            p.add("scalar", lambda s, hh=hh, W=W, bg=bg: s.activation(
                out=hh["hs"][1][:, 0:W], in_=bg[1][:, 0:W], func=AF.Silu),
                reads=[bg[0]], writes=[hh["hs"][0]])
            p.add("vector", lambda v, hh=hh, W=W, bu=bu: v.tensor_tensor(
                out=hh["h"][1][:, 0:W], in0=hh["hs"][1][:, 0:W],
                in1=bu[1][:, 0:W], op=ALU.mult),
                reads=[hh["hs"][0], bu[0]], writes=[hh["h"][0]], wfree=W)
            if i + 1 < NT:
                emit_gateup(i + 1)
            items = [dict(
                src_buf=hh["h"][0],
                src_ap=(lambda hh=hh, W=W: hh["h"][1].ap()[:, 0:W]),
                w=W, scr=f2, dst_buf=hh["hq"][0],
                dst_ap=(lambda hh=hh, W=W: hh["hq"][1].ap()[:, 0:W]))]
            if BISECT_INLINE:
                k0 = (i * len(witems)) // NT
                k1 = ((i + 1) * len(witems)) // NT
                items += list(witems[k0:k1])
            emit_quant_multi(p, items, qthr, tag=f"hq{j}_{i}_")
            if BISECT_INLINE:
                for k in range(k0, k1):
                    sl = k // 2 + 2
                    if k % 2 == 1 and sl < len(load_fns):
                        load_fns[sl]()
            pbh = pbbf[1]
            for nc4 in range(CW):
                p.add("tensor", lambda t, hh=hh, nc4=nc4, pbh=pbh: t.transpose(
                    pbh[1][:, nc4 * 128:(nc4 + 1) * 128],
                    hh["hq"][1][:, nc4 * 128:(nc4 + 1) * 128],
                    identb[1][:, :]), reads=[hh["hq"][0], identb[0]],
                    writes=[pbh[0]])
            p.add("scalar", lambda s, hh=hh, CW=CW, pbh=pbh: s.activation(
                out=hh["hqT"][1][:, 0:CW * 128], in_=pbh[1][:, 0:CW * 128],
                func=AF.Copy), reads=[pbh[0]], writes=[hh["hqT"][0]])
            for half, bd in ((0, bd1), (1, bd2)):
                for nc4 in range(CW):
                    p.add("tensor", lambda t, hh=hh, nc4=nc4, half=half, bd=bd, CW=CW: t.matmul(
                        bd[1][:, 0:512],
                        hh["hqT"][1][:, nc4 * 128:(nc4 + 1) * 128],
                        wdqT[1][:, nc4 * D + half * 512: nc4 * D + (half + 1) * 512],
                        start=(nc4 == 0), stop=(nc4 == CW - 1)),
                        reads=[hh["hqT"][0], wdqT[0]], writes=[bd[0]])
            for half, bd in ((0, bd1), (1, bd2)):
                if j == 0:
                    p.add("scalar", lambda s, i=i, half=half, bd=bd: s.activation(
                        out=acc[1][:, i * D + half * 512: i * D + (half + 1) * 512],
                        in_=bd[1][:, 0:512], func=AF.Copy),
                        reads=[bd[0]], writes=[acc[0]])
                else:
                    p.add("vector", lambda v, i=i, half=half, bd=bd: v.tensor_tensor(
                        out=acc[1][:, i * D + half * 512: i * D + (half + 1) * 512],
                        in0=acc[1][:, i * D + half * 512: i * D + (half + 1) * 512],
                        in1=bd[1][:, 0:512], op=ALU.add),
                        reads=[acc[0], bd[0]], writes=[acc[0]], wfree=512)

    if run2:
        lf0, w0 = prep_tile(0)
        for k in range(0, len(w0), 2):
            emit_quant_multi(p, w0[k:k + 2], qthr, tag=f"wq0_{k}_")
            if k // 2 + 2 < len(lf0):
                lf0[k // 2 + 2]()
        trans_tile(0)
        for j in range(len(NTILE_W)):
            if j + 1 < len(NTILE_W):
                lfs, nxt = prep_tile(j + 1)
            else:
                lfs, nxt = (), ()
            t_loop(j, lfs, nxt)
            if j + 1 < len(NTILE_W):
                trans_tile(j + 1)

    for i in range(NT if run2 else 0):
        gs = gstg[i % 2]
        p.add("scalar", lambda s, i=i, gs=gs: s.activation(
            out=gs[1][:, :], in_=acc[1][:, i * D:(i + 1) * D], func=AF.Silu),
            reads=[acc[0]], writes=[gs[0]])
        dma("gpsimd", dgres[:, i * D:(i + 1) * D], gs[1][:, :],
            reads=[gs[0]], tag=f"gsp{i}")

    p.barrier()
    if "acc" in ddbg and run2:
        dma("sync", ddbg["acc"], acc[1][:, :], reads=[acc[0]], tag="dbg_acc")
        p.barrier()
    free_phase()
    for g in reversed(p12_ctx):
        g.__exit__(None, None, None)
    p12_ctx = []

    run3 = 3 in phases
    aout = alloc("aout", [128, ND * L], BF16, late=True)
    xnT = alloc("xnT", [128, ND * L], BF16)
    maskT = alloc("maskTS", [128, NH * C], F32)
    Gam = alloc("GamS", [128, NPAIR * C], F32)
    tri8 = alloc("tri8S", [128, 128], F32)
    gam16 = alloc("gam16S", [16, 1], F32)
    gam8 = alloc("gam8S", [128, NPAIR], F32)
    selg = alloc("selgS", [16, NPAIR * 128], F32)
    sel2 = alloc("sel2S", [16, 128], F32)
    m2t = alloc("m2S", [16, NPAIR], F32)
    r16 = alloc("r16S", [16, 128], F32)
    ri16 = alloc("ri16S", [16, 128], F32)
    ri2 = alloc("ri2S", [16, 128], F32)
    ri2T = alloc("ri2TS", [128, 16], F32)
    sck = alloc("sckS", [16, 1], F32)
    sck2 = alloc("sck2S", [16, 1], F32)
    tb = alloc("tbS", [16, 16], F32)
    lnG = alloc("lnGS", [16, 1], F32)
    trow = alloc("trowS", [16, 128], F32)
    tneg = alloc("tnegS", [16, 128], F32)
    t127 = alloc("t127S", [16, 128], F32)
    dma("sync", tri8[1][:, :], dtri8, writes=[tri8[0]], tag="c2")
    dma("sync", gam16[1][:, :], dgam16, writes=[gam16[0]], tag="c3")
    dma("sync", gam8[1][:, :], dgam8, writes=[gam8[0]], tag="c4")
    dma("sync", selg[1][:, :], dselg, writes=[selg[0]], tag="c4b")
    dma("sync", sel2[1][:, :], dsel2, writes=[sel2[0]], tag="c4c")
    dma("sync", m2t[1][:, :], dm2, writes=[m2t[0]], tag="c4d")
    dma("sync", trow[1][:, :], dtrow, writes=[trow[0]], tag="c4e")
    dma("sync", tneg[1][:, :], dtneg, writes=[tneg[0]], tag="c4f")
    dma("sync", t127[1][:, :], dt127, writes=[t127[0]], tag="c4g")

    p.add("scalar", lambda s: s.activation(
        out=lnG[1][:, :], in_=gam16[1][:, :], func=AF.Ln),
        reads=[gam16[0]], writes=[lnG[0]])
    p.add("vector", lambda v: v.tensor_tensor(
        out=r16[1][:, :], in0=trow[1][:, :],
        in1=lnG[1].ap()[:, 0:1].broadcast_to([16, 128]), op=ALU.mult),
        reads=[trow[0], lnG[0]], writes=[r16[0]])
    p.add("scalar", lambda s: s.activation(
        out=r16[1][:, :], in_=r16[1][:, :], func=AF.Exp),
        reads=[r16[0]], writes=[r16[0]])
    p.add("vector", lambda v: v.tensor_tensor(
        out=ri16[1][:, :], in0=tneg[1][:, :],
        in1=lnG[1].ap()[:, 0:1].broadcast_to([16, 128]), op=ALU.mult),
        reads=[tneg[0], lnG[0]], writes=[ri16[0]])
    p.add("scalar", lambda s: s.activation(
        out=ri16[1][:, :], in_=ri16[1][:, :], func=AF.Exp),
        reads=[ri16[0]], writes=[ri16[0]])
    p.add("vector", lambda v: v.tensor_tensor(
        out=ri2[1][:, :], in0=t127[1][:, :],
        in1=lnG[1].ap()[:, 0:1].broadcast_to([16, 128]), op=ALU.mult),
        reads=[t127[0], lnG[0]], writes=[ri2[0]])
    p.add("scalar", lambda s: s.activation(
        out=ri2[1][:, :], in_=ri2[1][:, :], func=AF.Exp),
        reads=[ri2[0]], writes=[ri2[0]])
    p.add("tensor", lambda t: t.transpose(
        pbank[0][1][:, 0:16], ri2[1][:, :], ident[1][0:16, 0:16]),
        reads=[ri2[0], ident[0]], writes=[pbank[0][0]])
    p.add("vector", lambda v: v.tensor_copy(out=ri2T[1][:, :], in_=pbank[0][1][:, 0:16]),
          reads=[pbank[0][0]], writes=[ri2T[0]])
    rflat = alloc("rflatS", [1, NH * 128], F32)
    riflat = alloc("riflatS", [1, NH * 128], F32)
    dma("sync", rflat[1].ap().rearrange("p (k t) -> p k t", k=16),
        r16[1].ap()[:, :], reads=[r16[0]], writes=[rflat[0]], tag="rf")
    dma("sync", riflat[1].ap().rearrange("p (k t) -> p k t", k=16),
        ri16[1].ap()[:, :], reads=[ri16[0]], writes=[riflat[0]], tag="rif")
    for h in range(NH):
        p.add("tensor", lambda t, h=h: t.matmul(
            pbank[0][1][:, 0:128], riflat[1][0:1, h * 128:(h + 1) * 128],
            rflat[1][0:1, h * 128:(h + 1) * 128], start=True, stop=True),
            reads=[riflat[0], rflat[0]], writes=[pbank[0][0]])
        p.add("vector", lambda v, h=h: v.tensor_tensor(
            out=maskT[1][:, h * 128:(h + 1) * 128], in0=pbank[0][1][:, 0:128],
            in1=tri8[1][:, :], op=ALU.mult),
            reads=[pbank[0][0], tri8[0]], writes=[maskT[0]])
    for hp in range(NPAIR):
        p.add("tensor", lambda t, hp=hp: t.matmul(
            pbank[1][1][:, 0:128], selg[1][:, hp * 128:(hp + 1) * 128],
            r16[1][:, :], start=True, stop=True),
            reads=[selg[0], r16[0]], writes=[pbank[1][0]])
        p.add("vector", lambda v, hp=hp: v.tensor_scalar(
            out=Gam[1][:, hp * 128:(hp + 1) * 128], in0=pbank[1][1][:, 0:128],
            scalar1=gam8[1][:, hp:hp + 1], scalar2=None, op0=ALU.mult),
            reads=[pbank[1][0], gam8[0]], writes=[Gam[0]])
    p.add("vector", lambda v: v.tensor_scalar(
        out=sck[1][:, :], in0=lnG[1][:, :], scalar1=128.0, scalar2=None,
        op0=ALU.mult), reads=[lnG[0]], writes=[sck[0]])
    p.add("scalar", lambda s: s.activation(
        out=sck2[1][:, :], in_=sck[1][:, :], func=AF.Exp),
        reads=[sck[0]], writes=[sck2[0]])
    p.add("vector", lambda v: v.tensor_tensor(
        out=tb[1][:, 0:NPAIR], in0=m2t[1][:, :],
        in1=sck2[1].ap()[:, 0:1].broadcast_to([16, NPAIR]), op=ALU.mult),
        reads=[m2t[0], sck2[0]], writes=[tb[0]])
    p.add("tensor", lambda t: t.matmul(
        pbank[0][1][:, 0:NPAIR], sel2[1][:, :], tb[1][:, 0:NPAIR],
        start=True, stop=True), reads=[sel2[0], tb[0]], writes=[pbank[0][0]])
    p.add("vector", lambda v: v.tensor_copy(out=gC[1][:, :], in_=pbank[0][1][:, 0:NPAIR]),
          reads=[pbank[0][0]], writes=[gC[0]])

    wqS = alloc("wqS", [128, ND * D], BF16)
    wkS = alloc("wkS", [128, ND * D], BF16)
    wvS = alloc("wvS", [128, ND * D], BF16)
    dma("sync", xnT[1][:, :], dxnT, writes=[xnT[0]], tag="xnT_rl")
    dma("gpsimd", wqS[1][:, :], dwq, writes=[wqS[0]], tag="wqS")
    dma("gpsimd", wkS[1][:, :], dwk, writes=[wkS[0]], tag="wkS")
    dma("gpsimd", wvS[1][:, :], dwv, writes=[wvS[0]], tag="wvS")

    QT = alloc("QT", [128, NPAIR * 512], BF16)
    Qg = alloc("Qg", [128, NPAIR * 512], BF16)
    KT = alloc("KT", [128, NPAIR * 512], BF16)
    Kd = alloc("Kd", [128, NPAIR * 512], BF16)
    Vs = alloc("Vs", [128, 4 * D], BF16)
    SA = [alloc("SA0", [128, 256], BF16), alloc("SA1", [128, 256], BF16)]
    Sst = [alloc(f"Sst{hp}", [128, 128], F32) for hp in range(NPAIR)]
    Sbf = [alloc(f"Sbf{hp}", [128, 128], BF16) for hp in range(NPAIR)]
    csb = alloc("csb", [128, 128], F32)
    sums = alloc("sums", [128, 2 * ND], F32)
    s1t = alloc("s1t", [128, 1], F32)
    s2t = alloc("s2t", [128, 1], F32)
    sqscr = alloc("sqscr", [128, 128], F32)
    reps = alloc("reps", [128, 16], F32)
    gnst = {k: alloc("gn" + k, [128, ND], F32)
            for k in ("mean", "ex2", "var", "rstd", "t1", "t2", "scalev", "biasv")}

    p.add("vector", lambda v: v.memset(sums[1][:, :], 0.0), writes=[sums[0]])
    for hp in range(NPAIR):
        p.add("vector", lambda v, hp=hp: v.memset(Sst[hp][1][:, :], 0.0),
              writes=[Sst[hp][0]])
        p.add("vector", lambda v, hp=hp: v.memset(Sbf[hp][1][:, :], 0.0),
              writes=[Sbf[hp][0]])

    qkvlvl = 5 if p3_qkv is True else int(p3_qkv)
    for g4 in range(p3_ngroups if run3 else 0):
        for hp in range(NPAIR if qkvlvl >= 1 else 0):
            bq = pbank[0]
            for kc in range(ND):
                p.add("tensor", lambda t, hp=hp, kc=kc, g4=g4: t.matmul(
                    bq[1][:, 0:512],
                    wqS[1][:, kc * D + hp * 128: kc * D + (hp + 1) * 128],
                    xnT[1][:, kc * L + g4 * 512: kc * L + (g4 + 1) * 512],
                    start=(kc == 0), stop=(kc == ND - 1)),
                    reads=[wqS[0], xnT[0]], writes=[bq[0]])
            p.add("scalar", lambda s, hp=hp: s.activation(
                out=QT[1][:, hp * 512:(hp + 1) * 512], in_=bq[1][:, 0:512],
                func=AF.Copy), reads=[bq[0]], writes=[QT[0]])
            if qkvlvl < 2:
                continue
            p.add("vector", lambda v, hp=hp: v.tensor_tensor(
                out=Qg[1].ap().rearrange("p (a t) -> p a t", t=128)[:, 4 * hp:4 * hp + 4, :],
                in0=bq[1].ap().rearrange("p (c t) -> p c t", t=128),
                in1=Gam[1].ap()[:, hp * 128:(hp + 1) * 128].unsqueeze(1).broadcast_to([128, 4, 128]),
                op=ALU.mult), reads=[bq[0], Gam[0]], writes=[Qg[0]])
            if qkvlvl < 3:
                continue
            bk = pbank[1]
            for kc in range(ND):
                p.add("tensor", lambda t, hp=hp, kc=kc, g4=g4: t.matmul(
                    bk[1][:, 0:512],
                    wkS[1][:, kc * D + hp * 128: kc * D + (hp + 1) * 128],
                    xnT[1][:, kc * L + g4 * 512: kc * L + (g4 + 1) * 512],
                    start=(kc == 0), stop=(kc == ND - 1)),
                    reads=[wkS[0], xnT[0]], writes=[bk[0]])
            p.add("scalar", lambda s, hp=hp: s.activation(
                out=KT[1][:, hp * 512:(hp + 1) * 512], in_=bk[1][:, 0:512],
                func=AF.Copy), reads=[bk[0]], writes=[KT[0]])
            if qkvlvl < 4:
                continue
            for c4 in range(4):
                p.add("tensor", lambda t, hp=hp, c4=c4: t.transpose(
                    pbbf[0][1][:, c4 * 128:(c4 + 1) * 128],
                    KT[1][:, hp * 512 + c4 * 128: hp * 512 + (c4 + 1) * 128],
                    identb[1][:, :]), reads=[KT[0], identb[0]], writes=[pbbf[0][0]])
            p.add("vector", lambda v, hp=hp: v.tensor_tensor(
                out=Kd[1].ap().rearrange("p (a h e) -> p a h e", h=2, e=64)[:, 4 * hp:4 * hp + 4, :, :],
                in0=pbbf[0][1].ap().rearrange("p (c h e) -> p c h e", h=2, e=64)[:, 0:4, :, :],
                in1=ri2T[1].ap()[:, 2 * hp:2 * hp + 2].unsqueeze(1).unsqueeze(3).broadcast_to([128, 4, 2, 64]),
                op=ALU.mult), reads=[pbbf[0][0], ri2T[0]], writes=[Kd[0]])
        for c4 in range(4 if qkvlvl >= 5 else 0):
            for half, bv in ((0, pbank[2]), (1, pbank[3])):
                for kc in range(ND):
                    p.add("tensor", lambda t, c4=c4, kc=kc, g4=g4, half=half, bv=bv: t.matmul(
                        bv[1][:, 0:512],
                        xnT[1][:, kc * L + g4 * 512 + c4 * 128: kc * L + g4 * 512 + (c4 + 1) * 128],
                        wvS[1][:, kc * D + half * 512: kc * D + (half + 1) * 512],
                        start=(kc == 0), stop=(kc == ND - 1)),
                        reads=[xnT[0], wvS[0]], writes=[bv[0]])
            p.add("scalar", lambda s, c4=c4: s.activation(
                out=Vs[1][:, c4 * D: c4 * D + 512], in_=pbank[2][1][:, 0:512],
                func=AF.Copy), reads=[pbank[2][0]], writes=[Vs[0]])
            p.add("scalar", lambda s, c4=c4: s.activation(
                out=Vs[1][:, c4 * D + 512: (c4 + 1) * D], in_=pbank[3][1][:, 0:512],
                func=AF.Copy), reads=[pbank[3][0]], writes=[Vs[0]])

        def emit_scores(hp, c4):
            bs = pbank[4 + (hp % 2)]
            p.add("tensor", lambda t, hp=hp, c4=c4, bs=bs: t.matmul(
                bs[1][:, 0:128],
                KT[1][0:64, hp * 512 + c4 * 128: hp * 512 + (c4 + 1) * 128],
                QT[1][0:64, hp * 512 + c4 * 128: hp * 512 + (c4 + 1) * 128],
                start=True, stop=True, tile_position=(0, 0)),
                reads=[KT[0], QT[0]], writes=[bs[0]])
            p.add("tensor", lambda t, hp=hp, c4=c4, bs=bs: t.matmul(
                bs[1][:, 128:256],
                KT[1][64:128, hp * 512 + c4 * 128: hp * 512 + (c4 + 1) * 128],
                QT[1][64:128, hp * 512 + c4 * 128: hp * 512 + (c4 + 1) * 128],
                start=True, stop=True, tile_position=(64, 0)),
                reads=[KT[0], QT[0]], writes=[bs[0]])

        for c4 in range(p3_nchunks):
            c = g4 * 4 + c4
            emit_scores(0, c4)
            for hp in range(NPAIR):
                bs = pbank[4 + (hp % 2)]
                sa = SA[hp % 2]
                p.add("vector", lambda v, hp=hp, bs=bs, sa=sa: v.tensor_tensor(
                    out=sa[1][:, 0:256], in0=bs[1][:, 0:256],
                    in1=maskT[1][:, (2 * hp) * 128: (2 * hp + 2) * 128],
                    op=ALU.mult), reads=[bs[0], maskT[0]], writes=[sa[0]],
                    wfree=256)
                if hp + 1 < NPAIR:
                    emit_scores(hp + 1, c4)
                bo = pbank[0]
                p.add("tensor", lambda t, hp=hp, c4=c4, bo=bo: t.matmul(
                    bo[1][:, 0:128], Sbf[hp][1][:, :],
                    Qg[1][:, hp * 512 + c4 * 128: hp * 512 + (c4 + 1) * 128],
                    start=True, stop=True),
                    reads=[Sbf[hp][0], Qg[0]], writes=[bo[0]])
                bi = pbank[1]
                p.add("tensor", lambda t, hp=hp, c4=c4, bi=bi, sa=sa: t.matmul(
                    bi[1][0:64, 0:128],
                    Vs[1][:, c4 * D + hp * 128: c4 * D + hp * 128 + 64],
                    sa[1][:, 0:128],
                    start=True, stop=True, tile_position=(0, 0)),
                    reads=[Vs[0], sa[0]], writes=[bi[0]])
                p.add("tensor", lambda t, hp=hp, c4=c4, bi=bi, sa=sa: t.matmul(
                    bi[1][64:128, 0:128],
                    Vs[1][:, c4 * D + hp * 128 + 64: c4 * D + (hp + 1) * 128],
                    sa[1][:, 128:256],
                    start=True, stop=True, tile_position=(0, 64)),
                    reads=[Vs[0], sa[0]], writes=[bi[0]])
                p.add("scalar", lambda s, bo=bo: s.activation(
                    out=csb[1][:, :], in_=bo[1][:, 0:128], func=AF.Copy),
                    reads=[bo[0]], writes=[csb[0]])
                p.add("vector", lambda v, bi=bi: v.tensor_tensor(
                    out=sqscr[1][:, :], in0=bi[1][0:128, 0:128], in1=csb[1][:, :],
                    op=ALU.add), reads=[bi[0], csb[0]], writes=[sqscr[0]],
                    wfree=128)
                p.add("vector", lambda v, hp=hp, c=c: v.tensor_scalar(
                    out=aout[1][:, hp * L + c * 128: hp * L + (c + 1) * 128],
                    in0=sqscr[1][:, :], scalar1=1.0, scalar2=0.0, op0=ALU.mult,
                    op1=ALU.add, accum_out=s1t[1][:, 0:1]),
                    reads=[sqscr[0]], writes=[aout[0], s1t[0]], hazard=True,
                    wfree=128)
                p.add("scalar", lambda s, hp=hp: s.activation(
                    out=sqscr[1][:, :], in_=sqscr[1][:, :], func=AF.Square,
                    accum_out=s2t[1][:, 0:1]),
                    reads=[sqscr[0]], writes=[sqscr[0], s2t[0]], hazard=True)
                p.add("vector", lambda v, hp=hp: v.tensor_tensor(
                    out=sums[1][:, hp:hp + 1], in0=sums[1][:, hp:hp + 1],
                    in1=s1t[1][:, :], op=ALU.add),
                    reads=[s1t[0], sums[0]], writes=[sums[0]])
                p.add("vector", lambda v, hp=hp: v.tensor_tensor(
                    out=sums[1][:, ND + hp: ND + hp + 1],
                    in0=sums[1][:, ND + hp: ND + hp + 1],
                    in1=s2t[1][:, :], op=ALU.add),
                    reads=[s2t[0], sums[0]], writes=[sums[0]])
                bst = pbank[2]
                p.add("tensor", lambda t, hp=hp, c4=c4, bst=bst: t.matmul(
                    bst[1][0:64, 0:64],
                    Kd[1][:, hp * 512 + c4 * 128: hp * 512 + c4 * 128 + 64],
                    Vs[1][:, c4 * D + hp * 128: c4 * D + hp * 128 + 64],
                    start=True, stop=True, tile_position=(0, 0)),
                    reads=[Kd[0], Vs[0]], writes=[bst[0]])
                p.add("tensor", lambda t, hp=hp, c4=c4, bst=bst: t.matmul(
                    bst[1][64:128, 64:128],
                    Kd[1][:, hp * 512 + c4 * 128 + 64: hp * 512 + (c4 + 1) * 128],
                    Vs[1][:, c4 * D + hp * 128 + 64: c4 * D + (hp + 1) * 128],
                    start=True, stop=True, tile_position=(0, 64)),
                    reads=[Kd[0], Vs[0]], writes=[bst[0]])
                p.add("vector", lambda v, hp=hp, bst=bst: v.scalar_tensor_tensor(
                    out=Sst[hp][1][0:64, 0:64], in0=Sst[hp][1][0:64, 0:64],
                    scalar=gC[1][0:64, hp:hp + 1], in1=bst[1][0:64, 0:64],
                    op0=ALU.mult, op1=ALU.add),
                    reads=[Sst[hp][0], gC[0], bst[0]], writes=[Sst[hp][0]])
                p.add("vector", lambda v, hp=hp, bst=bst: v.scalar_tensor_tensor(
                    out=Sst[hp][1][64:128, 64:128], in0=Sst[hp][1][64:128, 64:128],
                    scalar=gC[1][64:128, hp:hp + 1], in1=bst[1][64:128, 64:128],
                    op0=ALU.mult, op1=ALU.add),
                    reads=[Sst[hp][0], gC[0], bst[0]], writes=[Sst[hp][0]])
                p.add("vector", lambda v, hp=hp: v.tensor_copy(
                    out=Sbf[hp][1][:, :], in_=Sst[hp][1][:, :]),
                    reads=[Sst[hp][0]], writes=[Sbf[hp][0]])

    run3n = run3 and p3_norm
    if not run3:
        p.add("vector", lambda v: v.memset(aout[1][:, :], 0.0), writes=[aout[0]])
    if run3n or not run3:
      p.add("tensor", lambda t: t.matmul(
        pbank[0][1][:, 0:16], HHs[1][:, :], sums[1][:, :], start=True, stop=True),
        reads=[HHs[0], sums[0]], writes=[pbank[0][0]])
      p.add("vector", lambda v: v.tensor_copy(out=reps[1][:, :], in_=pbank[0][1][:, 0:16]),
          reads=[pbank[0][0]], writes=[reps[0]])
    NEL = float(64 * L)
    if run3n or not run3:
      p.add("vector", lambda v: v.tensor_scalar(
        out=gnst["mean"][1][:, :], in0=reps[1][:, 0:ND], scalar1=1.0 / NEL,
        scalar2=None, op0=ALU.mult), reads=[reps[0]], writes=[gnst["mean"][0]])
      p.add("vector", lambda v: v.tensor_scalar(
        out=gnst["ex2"][1][:, :], in0=reps[1][:, ND:2 * ND], scalar1=1.0 / NEL,
        scalar2=None, op0=ALU.mult), reads=[reps[0]], writes=[gnst["ex2"][0]])
      p.add("vector", lambda v: v.tensor_tensor(
        out=gnst["t1"][1][:, :], in0=gnst["mean"][1][:, :], in1=gnst["mean"][1][:, :],
        op=ALU.mult), reads=[gnst["mean"][0]], writes=[gnst["t1"][0]])
      p.add("vector", lambda v: v.tensor_tensor(
        out=gnst["var"][1][:, :], in0=gnst["ex2"][1][:, :], in1=gnst["t1"][1][:, :],
        op=ALU.subtract), reads=[gnst["ex2"][0], gnst["t1"][0]], writes=[gnst["var"][0]])
      p.add("vector", lambda v: v.tensor_scalar(
        out=gnst["var"][1][:, :], in0=gnst["var"][1][:, :], scalar1=1e-5,
        scalar2=None, op0=ALU.add), reads=[gnst["var"][0]], writes=[gnst["var"][0]])
      emit_rstd_newton(p, None, gnst["var"], gnst["rstd"], gnst["t1"], gnst["t2"])
      p.add("vector", lambda v: v.tensor_tensor(
        out=gnst["scalev"][1][:, :], in0=gnst["rstd"][1][:, :], in1=gnw[1][:, :],
        op=ALU.mult), reads=[gnst["rstd"][0], gnw[0]], writes=[gnst["scalev"][0]])
      p.add("vector", lambda v: v.tensor_tensor(
        out=gnst["t1"][1][:, :], in0=gnst["mean"][1][:, :], in1=gnst["scalev"][1][:, :],
        op=ALU.mult), reads=[gnst["mean"][0], gnst["scalev"][0]], writes=[gnst["t1"][0]])
      p.add("vector", lambda v: v.tensor_tensor(
        out=gnst["biasv"][1][:, :], in0=gnb[1][:, :], in1=gnst["t1"][1][:, :],
        op=ALU.subtract), reads=[gnb[0], gnst["t1"][0]], writes=[gnst["biasv"][0]])
      for dc in range(ND):
        p.add("scalar", lambda s, dc=dc: s.activation(
            out=aout[1][:, dc * L:(dc + 1) * L], in_=aout[1][:, dc * L:(dc + 1) * L],
            func=AF.Identity, scale=gnst["scalev"][1][:, dc:dc + 1],
            bias=gnst["biasv"][1][:, dc:dc + 1]),
            reads=[aout[0], gnst["scalev"][0], gnst["biasv"][0]], writes=[aout[0]])

    p.barrier()
    if "aout" in ddbg:
        dma("sync", ddbg["aout"], aout[1][:, :], reads=[aout[0]], tag="dbg_aout")
        p.barrier()
    free_phase()

    run4 = 4 in phases
    woS = alloc("woS", [128, ND * D], BF16)
    dma("gpsimd", woS[1][:, :], dwo, writes=[woS[0]], tag="woS")
    p4 = []
    for par in range(4):
        b = {}
        b["xc"] = alloc(f"p4x{par}", [128, D], F32)
        b["gt"] = alloc(f"p4g{par}", [128, D], BF16)
        b["sg1"] = alloc(f"p4s{par}", [128, D], F32)
        b["yt"] = alloc(f"p4y{par}", [128, D], F32)
        p4.append(b)

    for i in range(NT if run4 else 0):
        b = p4[i % 4]
        dma("sync", b["xc"][1][:, :], dx[i * 128:(i + 1) * 128, :],
            writes=[b["xc"][0]], tag=f"p4x{i}")
        dma("sync", b["gt"][1][:, :], dgres[:, i * D:(i + 1) * D],
            writes=[b["gt"][0]], tag=f"p4g{i}")
        bw1, bw2 = pbank[(i % 2) * 2], pbank[(i % 2) * 2 + 1]
        for half, bw in ((0, bw1), (1, bw2)):
            for dc in range(ND):
                p.add("tensor", lambda t, i=i, dc=dc, half=half, bw=bw: t.matmul(
                    bw[1][:, 0:512],
                    aout[1][:, dc * L + i * 128: dc * L + (i + 1) * 128],
                    woS[1][:, dc * D + half * 512: dc * D + (half + 1) * 512],
                    start=(dc == 0), stop=(dc == ND - 1)),
                    reads=[aout[0], woS[0]], writes=[bw[0]])
        for half, bw in ((0, bw1), (1, bw2)):
            p.add("vector", lambda v, b=b, i=i, half=half, bw=bw: v.tensor_tensor(
                out=b["sg1"][1][:, half * 512:(half + 1) * 512],
                in0=bw[1][:, 0:512],
                in1=b["gt"][1][:, half * 512:(half + 1) * 512],
                op=ALU.mult), reads=[bw[0], b["gt"][0]], writes=[b["sg1"][0]])
        p.add("vector", lambda v, b=b: v.tensor_tensor(
            out=b["yt"][1][:, :], in0=b["sg1"][1][:, :], in1=b["xc"][1][:, :],
            op=ALU.add), reads=[b["sg1"][0], b["xc"][0]], writes=[b["yt"][0]])
        dma("sync", dy[i * 128:(i + 1) * 128, :], b["yt"][1][:, :],
            reads=[b["yt"][0]], tag=f"p4y{i}")

    p.barrier()

    with nc.Block() as block:
        sem_ctx = p.lower(nc, block)

    for g in reversed(sem_ctx):
        g.__exit__(None, None, None)
    free_phase()
    for g in reversed(late_ctx):
        g.__exit__(None, None, None)
    for g in reversed(perm_ctx):
        g.__exit__(None, None, None)
    return nc


def _prep_inputs(inputs):
    x = np.ascontiguousarray(np.asarray(inputs["L0_attn_x"], dtype=np.float32))
    Wq = np.asarray(inputs["L0_attn_Wq"], dtype=np.float32)
    Wk = np.asarray(inputs["L0_attn_Wk"], dtype=np.float32)
    Wv = np.asarray(inputs["L0_attn_Wv"], dtype=np.float32)
    Wout = np.asarray(inputs["L0_attn_Wout"], dtype=np.float32)
    gnwv = np.asarray(inputs["L0_attn_gn_weight"], dtype=np.float32)
    gnbv = np.asarray(inputs["L0_attn_gn_bias"], dtype=np.float32)
    wg = np.ascontiguousarray(np.asarray(inputs["L0_ffn_w_gate"], dtype=np.float32))
    wu = np.ascontiguousarray(np.asarray(inputs["L0_ffn_w_up"], dtype=np.float32))
    wd = np.ascontiguousarray(np.asarray(inputs["L0_ffn_w_down"], dtype=np.float32))

    def he(W):
        a = W.transpose(1, 0, 2).reshape(D, D)
        return np.ascontiguousarray(
            a.reshape(ND, 128, D).transpose(1, 0, 2).reshape(128, ND * D))

    def rows(Wm):
        N = Wm.shape[1]
        return np.ascontiguousarray(
            Wm.reshape(ND, 128, N).transpose(1, 0, 2).reshape(128, ND * N))

    consts = host_consts()
    shared = {
        "wq_r": he(Wq), "wk_r": he(Wk), "wv_r": he(Wv), "wout_r": rows(Wout),
        "w_gate": wg, "w_up": wu, "w_down": wd,
        "gnw_r": np.ascontiguousarray(gnwv.reshape(ND, 128).T),
        "gnb_r": np.ascontiguousarray(gnbv.reshape(ND, 128).T),
        "ident": consts["ident"], "ident_bf": consts["ident_bf"],
        "tri8": consts["tri8"], "gam16": consts["gam16"],
        "gam8": consts["gam8"], "selg": consts["selg"],
        "sel2": consts["sel2"], "m2": consts["m2"], "HH": consts["HH"],
        "trow": consts["trow"], "tneg": consts["tneg"], "t127": consts["t127"],
        "qthr": consts["qthr"],
    }
    in_maps = []
    for b in range(8):
        m = dict(shared)
        m["x"] = np.ascontiguousarray(x[b])
        in_maps.append(m)
    return in_maps


def kernel(**inputs):
    if "nc" not in _CACHE:
        _CACHE["nc"] = build_nc()
    nc = _CACHE["nc"]
    in_maps = _prep_inputs(inputs)
    res = run_bass_kernel_spmd(nc, in_maps, core_ids=list(range(8)))
    return np.stack([np.asarray(res.results[b]["y"]) for b in range(8)], axis=0)


# version 21
# speedup vs baseline: 1.5054x; 1.0085x over previous
import numpy as np
import ml_dtypes

import concourse.bass as bass
import concourse.mybir as mybir
from concourse.bass_utils import run_bass_kernel_spmd

F32 = mybir.dt.float32
BF16 = mybir.dt.bfloat16
AF = mybir.ActivationFunctionType
ALU = mybir.AluOpType
AXL = mybir.AxisListType
BF = ml_dtypes.bfloat16

L = 2048
D = 1024
NH = 16
E = 64
I_FFN = 2816
C = 128
NT = L // C
ND = D // 128
NPAIR = NH // 2
NTILE_W = [512, 512, 512, 512, 512, 256]
BISECT_INLINE = True

_CACHE = {}


def _gammas():
    return np.array([1.0 - 2.0 ** (-5 - (h // 2)) for h in range(NH)], dtype=np.float64)


def host_consts():
    g = _gammas()
    t_ = np.arange(C)
    u_ = np.arange(C)
    tri8 = (np.where(t_[None, :] - u_[:, None] >= 0, 1.0, 0.0) / 8.0).astype(np.float32)
    gam16 = g.astype(np.float32).reshape(16, 1)
    gam8 = np.zeros((128, NPAIR), dtype=np.float32)
    for hp in range(NPAIR):
        gam8[0:64, hp] = np.float32(g[2 * hp] / 8.0)
        gam8[64:128, hp] = np.float32(g[2 * hp + 1] / 8.0)
    selg = np.zeros((16, NPAIR * 128), dtype=np.float32)
    for hp in range(NPAIR):
        selg[2 * hp, hp * 128: hp * 128 + 64] = 1.0
        selg[2 * hp + 1, hp * 128 + 64: (hp + 1) * 128] = 1.0
    sel2 = np.zeros((16, 128), dtype=np.float32)
    for k in range(16):
        sel2[k, :] = ((k % 2) == (np.arange(128) >= 64)).astype(np.float32)
    m2 = np.zeros((16, NPAIR), dtype=np.float32)
    for k in range(16):
        m2[k, k // 2] = 1.0
    HH = np.zeros((128, 128), dtype=np.float32)
    HH[0:64, 0:64] = 1.0
    HH[64:128, 64:128] = 1.0
    trow = np.tile(np.arange(128, dtype=np.float32)[None, :], (16, 1))
    qthr = np.zeros((128, 8), dtype=np.float32)
    qthr[:, 0:4] = -np.array([0.25, 0.75, 1.25, 1.75], dtype=np.float32)
    return dict(ident=np.eye(128, dtype=np.float32),
                ident_bf=np.eye(128, dtype=BF),
                tri8=tri8, gam16=gam16, gam8=gam8, selg=selg, sel2=sel2,
                m2=m2, HH=HH, trow=trow, tneg=-trow,
                t127=(127.0 - trow).astype(np.float32), qthr=qthr)


ENGINES = ("tensor", "vector", "scalar", "gpsimd", "sync")


class Buf:
    __slots__ = ("name", "writer", "readers", "psum")

    def __init__(self, name, psum=False):
        self.name = name
        self.writer = None
        self.readers = []
        self.psum = psum


class Op:
    __slots__ = ("oid", "engine", "emit", "deps", "is_dma", "dma_idx", "pos",
                 "hazard", "tag", "wfree", "dma_feed")

    def __init__(self, oid, engine, emit, deps, is_dma, hazard, tag, wfree):
        self.dma_feed = False
        self.oid = oid
        self.engine = engine
        self.emit = emit
        self.deps = deps
        self.is_dma = is_dma
        self.dma_idx = None
        self.pos = None
        self.hazard = hazard
        self.tag = tag
        self.wfree = wfree


class Prog:
    def __init__(self):
        self.ops = []
        self.streams = {e: [] for e in ENGINES}
        self.dma_counts = {e: 0 for e in ENGINES}
        self.last_dma = {}
        self.last_barrier = {}

    def add(self, engine, emit, reads=(), writes=(), dma=False, hazard=False,
            extra_deps=(), tag="", wfree=1, _is_barrier=False):
        deps = set(extra_deps)
        if not _is_barrier and engine in self.last_barrier:
            deps.add(self.last_barrier[engine])
        reads = list(reads)
        writes = list(writes)
        reads2 = []
        for b in reads:
            (writes if b.psum else reads2).append(b)
        reads = reads2
        for b in reads:
            if b.writer is not None:
                deps.add(b.writer)
        for b in writes:
            if b.writer is not None:
                deps.add(b.writer)
            deps.update(b.readers)
        oid = len(self.ops)
        if dma:
            for b in reads:
                if b.writer is not None:
                    self.ops[b.writer].dma_feed = True
        op = Op(oid, engine, emit, deps, dma, hazard, tag, wfree)
        self.ops.append(op)
        op.pos = len(self.streams[engine])
        self.streams[engine].append(op)
        if dma:
            op.dma_idx = self.dma_counts[engine]
            self.dma_counts[engine] += 1
            self.last_dma[engine] = oid
        for b in reads:
            b.readers.append(oid)
        for b in writes:
            b.writer = oid
            b.readers = []
        return oid

    def barrier(self):
        deps = set()
        for e in ENGINES:
            if self.streams[e]:
                deps.add(self.streams[e][-1].oid)
        deps.update(self.last_dma.values())
        new_bar = {}
        for e in ENGINES:
            new_bar[e] = self.add(e, None, extra_deps=deps, tag="barrier",
                                  _is_barrier=True)
        self.last_barrier = new_bar

    def lower(self, nc, block):
        wait_targets = {e: set() for e in ENGINES}
        for o in self.ops:
            for did in o.deps:
                d = self.ops[did]
                if d.is_dma:
                    continue
                if d.engine != o.engine:
                    wait_targets[d.engine].add(d.pos)
        inc_rank = {}
        for e in ENGINES:
            r = {}
            for i, pp in enumerate(sorted(wait_targets[e])):
                r[pp] = i + 1
            inc_rank[e] = r

        eng_sems = {}
        dma_sems = {}
        ctx = []
        for e in ENGINES:
            s = nc.semaphore(f"es_{e}")
            eng_sems[e] = s.__enter__()
            ctx.append(s)
            s2 = nc.semaphore(f"ds_{e}")
            dma_sems[e] = s2.__enter__()
            ctx.append(s2)

        def run_engine(e, eng):
            prev = None
            for o in self.streams[e]:
                if (prev is not None and not prev.is_dma and not o.is_dma
                        and prev.wfree < 1000000 and not prev.hazard
                        and prev.oid in o.deps and o.emit is not None
                        and prev.emit is not None):
                    if e in ("vector", "gpsimd"):
                        eng.engine_nop()
                    else:
                        eng.nop()
                prev = o
                waits = {}
                for did in sorted(o.deps):
                    d = self.ops[did]
                    if d.is_dma:
                        key = ("dma", d.engine)
                        val = 16 * (d.dma_idx + 1)
                    elif d.engine == e:
                        continue
                    else:
                        key = ("eng", d.engine)
                        val = inc_rank[d.engine][d.pos]
                    if waits.get(key, 0) < val:
                        waits[key] = val
                for (kind, se), val in sorted(waits.items()):
                    sem = dma_sems[se] if kind == "dma" else eng_sems[se]
                    eng.wait_ge(sem, val)
                if o.emit is None:
                    inst = eng.drain()
                    if o.pos in inc_rank[e]:
                        inst.then_inc(eng_sems[e], 1)
                    continue
                inst = o.emit(eng)
                if o.is_dma:
                    inst.then_inc(dma_sems[e], 16)
                elif o.hazard or o.dma_feed:
                    dr = eng.drain()
                    if o.pos in inc_rank[e]:
                        dr.then_inc(eng_sems[e], 1)
                elif o.pos in inc_rank[e]:
                    inst.then_inc(eng_sems[e], 1)

        for e in ENGINES:
            getattr(block, e)(lambda eng, e=e: run_engine(e, eng))
        return ctx


def emit_quant_multi(p, items, thr, tag=""):
    def blk(a):
        return a.rearrange("p (b k) -> p b k", k=16)

    if len(items) >= 2:
        segs = [(it, 0, it["w"] // 16) for it in items]
    else:
        it0 = items[0]
        h = it0["w"] // 32
        segs = [(it0, 0, h), (it0, h, it0["w"] // 16)]
    for it, c0, c1 in segs:
        p.add("vector", lambda v, it=it, c0=c0, c1=c1: v.tensor_reduce(
            out=it["scr"]["bm"][1][:, c0:c1],
            in_=blk(it["src_ap"]())[:, c0:c1, :],
            axis=AXL.X, op=ALU.max, apply_absolute_value=True),
            reads=[it["src_buf"]], writes=[it["scr"]["bm"][0]],
            tag=tag + "bm", wfree=64)
    for it, c0, c1 in segs:
        p.add("vector", lambda v, it=it, c0=c0, c1=c1: v.tensor_scalar(
            out=it["scr"]["s6"][1][:, c0:c1], in0=it["scr"]["bm"][1][:, c0:c1],
            scalar1=float(1.0 / 6.0), scalar2=1e-12, op0=ALU.mult, op1=ALU.max),
            reads=[it["scr"]["bm"][0]], writes=[it["scr"]["s6"][0]],
            tag=tag + "s6", wfree=64)
    for it in items:
        p.add("scalar", lambda s, it=it: s.activation(
            out=it["scr"]["ab"][1][:, 0:it["w"]], in_=it["src_ap"](),
            func=AF.Abs), reads=[it["src_buf"]], writes=[it["scr"]["ab"][0]],
            tag=tag + "ab")
    for it, c0, c1 in segs:
        p.add("vector", lambda v, it=it, c0=c0, c1=c1: v.reciprocal(
            out=it["scr"]["si"][1][:, c0:c1], in_=it["scr"]["s6"][1][:, c0:c1]),
            reads=[it["scr"]["s6"][0]], writes=[it["scr"]["si"][0]],
            tag=tag + "si")
    for it in items:
        p.add("scalar", lambda s, it=it: s.activation(
            out=it["scr"]["sg"][1][:, 0:it["w"]], in_=it["src_ap"](),
            func=AF.Sign), reads=[it["src_buf"]], writes=[it["scr"]["sg"][0]],
            tag=tag + "sg")
    for it, c0, c1 in segs:
        p.add("vector", lambda v, it=it, c0=c0, c1=c1: v.tensor_tensor(
            out=blk(it["scr"]["z"][1].ap()[:, 0:it["w"]])[:, c0:c1, :],
            in0=blk(it["scr"]["ab"][1].ap()[:, 0:it["w"]])[:, c0:c1, :],
            in1=it["scr"]["si"][1].ap()[:, c0:c1].unsqueeze(2).broadcast_to([128, c1 - c0, 16]),
            op=ALU.mult),
            reads=[it["scr"]["ab"][0], it["scr"]["si"][0]],
            writes=[it["scr"]["z"][0]], tag=tag + "z", wfree=it["w"])
    for k in range(4):
        for it in items:
            p.add("scalar", lambda s, it=it, k=k: s.activation(
                out=it["scr"][f"s{k + 1}"][1][:, 0:it["w"]],
                in_=it["scr"]["z"][1][:, 0:it["w"]], func=AF.Sign,
                bias=thr[1][:, k:k + 1], scale=1.0),
                reads=[it["scr"]["z"][0], thr[0]],
                writes=[it["scr"][f"s{k + 1}"][0]], tag=tag + f"lo{k}")
    for nm, mid in (("iA", 2.5), ("iB", 3.5), ("iC", 5.0)):
        for it in items:
            p.add("vector", lambda v, it=it, nm=nm, mid=mid: v.tensor_scalar(
                out=it["scr"][nm][1][:, 0:it["w"]],
                in0=it["scr"]["z"][1][:, 0:it["w"]], scalar1=mid,
                scalar2=None, op0=ALU.is_gt),
                reads=[it["scr"]["z"][0]], writes=[it["scr"][nm][0]],
                tag=tag + nm, wfree=it["w"])
    for other in ("s2", "s3", "s4"):
        for it in items:
            p.add("vector", lambda v, it=it, other=other: v.tensor_tensor(
                out=it["scr"]["s1"][1][:, 0:it["w"]],
                in0=it["scr"]["s1"][1][:, 0:it["w"]],
                in1=it["scr"][other][1][:, 0:it["w"]], op=ALU.add),
                reads=[it["scr"]["s1"][0], it["scr"][other][0]],
                writes=[it["scr"]["s1"][0]], tag=tag + "A" + other,
                wfree=it["w"])
    for it in items:
        p.add("vector", lambda v, it=it: v.tensor_tensor(
            out=it["scr"]["iA"][1][:, 0:it["w"]],
            in0=it["scr"]["iA"][1][:, 0:it["w"]],
            in1=it["scr"]["iB"][1][:, 0:it["w"]], op=ALU.add),
            reads=[it["scr"]["iA"][0], it["scr"]["iB"][0]],
            writes=[it["scr"]["iA"][0]], tag=tag + "dAB", wfree=it["w"])
    for it in items:
        p.add("vector", lambda v, it=it: v.scalar_tensor_tensor(
            out=it["scr"]["iA"][1][:, 0:it["w"]],
            in0=it["scr"]["iC"][1][:, 0:it["w"]], scalar=2.0,
            in1=it["scr"]["iA"][1][:, 0:it["w"]], op0=ALU.mult, op1=ALU.add),
            reads=[it["scr"]["iC"][0], it["scr"]["iA"][0]],
            writes=[it["scr"]["iA"][0]], tag=tag + "dC", wfree=it["w"])
    for it in items:
        p.add("vector", lambda v, it=it: v.scalar_tensor_tensor(
            out=it["scr"]["s2"][1][:, 0:it["w"]],
            in0=it["scr"]["s1"][1][:, 0:it["w"]], scalar=0.25,
            in1=it["scr"]["iA"][1][:, 0:it["w"]], op0=ALU.mult, op1=ALU.add),
            reads=[it["scr"]["s1"][0], it["scr"]["iA"][0]],
            writes=[it["scr"]["s2"][0]], tag=tag + "u", wfree=it["w"])
    for it in items:
        p.add("vector", lambda v, it=it: v.scalar_tensor_tensor(
            out=it["scr"]["s3"][1][:, 0:it["w"]],
            in0=it["scr"]["s2"][1][:, 0:it["w"]], scalar=1.0,
            in1=it["scr"]["sg"][1][:, 0:it["w"]], op0=ALU.add, op1=ALU.mult),
            reads=[it["scr"]["s2"][0], it["scr"]["sg"][0]],
            writes=[it["scr"]["s3"][0]], tag=tag + "tn", wfree=it["w"])
    for it in items:
        nb = it["w"] // 16
        p.add("vector", lambda v, it=it, nb=nb: v.tensor_tensor(
            out=blk(it["dst_ap"]()),
            in0=blk(it["scr"]["s3"][1].ap()[:, 0:it["w"]]),
            in1=it["scr"]["s6"][1].ap()[:, 0:nb].unsqueeze(2).broadcast_to([128, nb, 16]),
            op=ALU.mult),
            reads=[it["scr"]["s3"][0], it["scr"]["s6"][0]],
            writes=[it["dst_buf"]], tag=tag + "y", wfree=it["w"])


def emit_quant(p, src_buf, src_ap, w, scr, dst_buf, dst_ap, tag="", thr=None):
    emit_quant_multi(p, [dict(src_buf=src_buf, src_ap=src_ap, w=w, scr=scr,
                              dst_buf=dst_buf, dst_ap=dst_ap)], thr, tag=tag)


def quant_scratch(alloc, prefix, w):
    scr = {}
    scr["ab"] = alloc(prefix + "ab", [128, w], F32)
    scr["z"] = alloc(prefix + "z", [128, w], F32)
    for nm in ("sg", "s1", "s2", "s3", "s4", "iA", "iB", "iC"):
        scr[nm] = alloc(prefix + nm, [128, w], BF16)
    for nm in ("bm", "s6", "si"):
        scr[nm] = alloc(prefix + nm, [128, w // 16], F32)
    return scr


def emit_rstd_newton(p, st, src_var, dst_rstd, t1, t2):
    p.add("scalar", lambda s: s.activation(
        out=t1[1][:, :], in_=src_var[1][:, :], func=AF.Sqrt),
        reads=[src_var[0]], writes=[t1[0]])
    p.add("vector", lambda v: v.reciprocal(
        out=dst_rstd[1][:, :], in_=t1[1][:, :]),
        reads=[t1[0]], writes=[dst_rstd[0]], hazard=True)
    p.add("vector", lambda v: v.tensor_tensor(
        out=t1[1][:, :], in0=dst_rstd[1][:, :], in1=dst_rstd[1][:, :],
        op=ALU.mult), reads=[dst_rstd[0]], writes=[t1[0]])
    p.add("vector", lambda v: v.tensor_tensor(
        out=t2[1][:, :], in0=t1[1][:, :], in1=src_var[1][:, :], op=ALU.mult),
        reads=[t1[0], src_var[0]], writes=[t2[0]])
    p.add("vector", lambda v: v.tensor_scalar(
        out=t2[1][:, :], in0=t2[1][:, :], scalar1=-0.5, scalar2=1.5,
        op0=ALU.mult, op1=ALU.add), reads=[t2[0]], writes=[t2[0]])
    p.add("vector", lambda v: v.tensor_tensor(
        out=dst_rstd[1][:, :], in0=dst_rstd[1][:, :], in1=t2[1][:, :],
        op=ALU.mult), reads=[dst_rstd[0], t2[0]], writes=[dst_rstd[0]])


def build_nc(phases=(1, 2, 3, 4), debug=(), p3_ngroups=4, p3_nchunks=4, p3_qkv=True, p3_norm=True):
    nc = bass.Bass("TRN2", target_bir_lowering=False, detect_race_conditions=False)

    dx = nc.dram_tensor("x", [L, D], F32, kind="ExternalInput").ap()
    dwq = nc.dram_tensor("wq_r", [128, ND * D], F32, kind="ExternalInput").ap()
    dwk = nc.dram_tensor("wk_r", [128, ND * D], F32, kind="ExternalInput").ap()
    dwv = nc.dram_tensor("wv_r", [128, ND * D], F32, kind="ExternalInput").ap()
    dwo = nc.dram_tensor("wout_r", [128, ND * D], F32, kind="ExternalInput").ap()
    dwg = nc.dram_tensor("w_gate", [I_FFN, D], F32, kind="ExternalInput").ap()
    dwu = nc.dram_tensor("w_up", [I_FFN, D], F32, kind="ExternalInput").ap()
    dwd = nc.dram_tensor("w_down", [D, I_FFN], F32, kind="ExternalInput").ap()
    dgnw = nc.dram_tensor("gnw_r", [128, ND], F32, kind="ExternalInput").ap()
    dgnb = nc.dram_tensor("gnb_r", [128, ND], F32, kind="ExternalInput").ap()
    dident = nc.dram_tensor("ident", [128, 128], F32, kind="ExternalInput").ap()
    didentb = nc.dram_tensor("ident_bf", [128, 128], BF16, kind="ExternalInput").ap()
    dtri8 = nc.dram_tensor("tri8", [128, 128], F32, kind="ExternalInput").ap()
    dgam16 = nc.dram_tensor("gam16", [16, 1], F32, kind="ExternalInput").ap()
    dgam8 = nc.dram_tensor("gam8", [128, NPAIR], F32, kind="ExternalInput").ap()
    dselg = nc.dram_tensor("selg", [16, NPAIR * 128], F32, kind="ExternalInput").ap()
    dsel2 = nc.dram_tensor("sel2", [16, 128], F32, kind="ExternalInput").ap()
    dm2 = nc.dram_tensor("m2", [16, NPAIR], F32, kind="ExternalInput").ap()
    dtrow = nc.dram_tensor("trow", [16, 128], F32, kind="ExternalInput").ap()
    dtneg = nc.dram_tensor("tneg", [16, 128], F32, kind="ExternalInput").ap()
    dt127 = nc.dram_tensor("t127", [16, 128], F32, kind="ExternalInput").ap()
    dhh = nc.dram_tensor("HH", [128, 128], F32, kind="ExternalInput").ap()
    dqthr = nc.dram_tensor("qthr", [128, 8], F32, kind="ExternalInput").ap()
    dy = nc.dram_tensor("y", [L, D], F32, kind="ExternalOutput").ap()
    dxnT = nc.dram_tensor("xnT_sp", [128, ND * L], BF16, kind="Internal").ap()
    dgres = nc.dram_tensor("gres_sp", [128, NT * D], BF16, kind="Internal").ap()
    ddbg = {}
    if "xqT" in debug:
        ddbg["xqT"] = nc.dram_tensor("dbg_xqT", [128, ND * L], BF16, kind="ExternalOutput").ap()
    if "xnT" in debug:
        ddbg["xnT"] = nc.dram_tensor("dbg_xnT", [128, ND * L], BF16, kind="ExternalOutput").ap()
    if "acc" in debug:
        ddbg["acc"] = nc.dram_tensor("dbg_acc", [128, NT * D], F32, kind="ExternalOutput").ap()
    if "aout" in debug:
        ddbg["aout"] = nc.dram_tensor("dbg_aout", [128, ND * L], BF16, kind="ExternalOutput").ap()

    p = Prog()
    perm_ctx = []
    late_ctx = []
    p12_ctx = []
    phase_ctx = []

    def alloc(name, shape, dtype, perm=False, late=False, p12=False):
        g = nc.sbuf_tensor(name, shape, dtype)
        t = g.__enter__()
        (perm_ctx if perm else (late_ctx if late else
         (p12_ctx if p12 else phase_ctx))).append(g)
        return (Buf(name), t)

    def palloc(name, shape, dtype):
        g = nc.psum_tensor(name, shape, dtype)
        t = g.__enter__()
        perm_ctx.append(g)
        return (Buf(name, psum=True), t)

    def free_phase():
        nonlocal phase_ctx
        for g in reversed(phase_ctx):
            g.__exit__(None, None, None)
        phase_ctx = []

    def dma(engine, out_ap, in_ap, reads=(), writes=(), tag=""):
        def f(eng, out_ap=out_ap, in_ap=in_ap):
            return eng.dma_start(out=out_ap, in_=in_ap)
        return p.add(engine, f, reads=reads, writes=writes, dma=True, tag=tag)

    ident = alloc("identS", [128, 128], F32, perm=True)
    identb = alloc("identbS", [128, 128], BF16, perm=True)
    gC = alloc("gCS", [128, NPAIR], F32, perm=True)
    HHs = alloc("HHS", [128, 128], F32, perm=True)
    gnw = alloc("gnwS", [128, ND], F32, perm=True)
    gnb = alloc("gnbS", [128, ND], F32, perm=True)
    qthr = alloc("qthrS", [128, 8], F32, perm=True)
    xqT = alloc("xqT", [128, ND * L], BF16, p12=True)

    pbank = [palloc(f"pb{i}", [128, 512], F32) for i in range(6)]
    pbbf = [palloc(f"pbb{i}", [128, 1024], BF16) for i in range(2)]

    dma("sync", ident[1][:, :], dident, writes=[ident[0]], tag="c0")
    dma("sync", identb[1][:, :], didentb, writes=[identb[0]], tag="c1")
    dma("sync", HHs[1][:, :], dhh, writes=[HHs[0]], tag="c6")
    dma("sync", gnw[1][:, :], dgnw, writes=[gnw[0]], tag="c7")
    dma("sync", gnb[1][:, :], dgnb, writes=[gnb[0]], tag="c8")
    dma("sync", qthr[1][:, :], dqthr, writes=[qthr[0]], tag="c9")

    xfull = alloc("xfull", [128, NT * D], F32)
    sums16 = alloc("sums16", [128, NT], F32)
    sqs16 = alloc("sqs16", [128, NT], F32)
    mean16 = alloc("mean16", [128, NT], F32)
    msq16 = alloc("msq16", [128, NT], F32)
    var16 = alloc("var16", [128, NT], F32)
    rstd16 = alloc("rstd16", [128, NT], F32)
    negm16 = alloc("negm16", [128, NT], F32)
    nt1 = alloc("nt1", [128, NT], F32)
    nt2 = alloc("nt2", [128, NT], F32)
    sqscrA = alloc("sqscrA", [128, D], F32)
    ph1 = []
    for par in range(2):
        b = {}
        b["xn"] = alloc(f"xn{par}", [128, D], F32)
        b["xnb"] = alloc(f"xnb{par}", [128, D], BF16)
        b["xq"] = alloc(f"p1xq{par}", [128, D], BF16)
        b["xntS"] = alloc(f"p1xnt{par}", [128, D], BF16)
        b.update(quant_scratch(alloc, f"p1s{par}", D))
        ph1.append(b)

    dma("sync", xfull[1].ap().rearrange("p (i d) -> p i d", d=D),
        dx.rearrange("(i p) d -> p i d", p=128), writes=[xfull[0]], tag="p1x")
    for i in range(NT):
        p.add("vector", lambda v, i=i: v.tensor_reduce(
            out=sums16[1][:, i:i + 1], in_=xfull[1][:, i * D:(i + 1) * D],
            axis=AXL.X, op=ALU.add),
            reads=[xfull[0]], writes=[sums16[0]], hazard=(i == NT - 1))
        p.add("scalar", lambda s, i=i: s.activation(
            out=sqscrA[1][:, :], in_=xfull[1][:, i * D:(i + 1) * D],
            func=AF.Square, accum_out=sqs16[1][:, i:i + 1]),
            reads=[xfull[0]], writes=[sqscrA[0], sqs16[0]],
            hazard=(i == NT - 1))
    p.add("vector", lambda v: v.tensor_scalar(
        out=mean16[1][:, :], in0=sums16[1][:, :], scalar1=float(1.0 / D),
        scalar2=None, op0=ALU.mult), reads=[sums16[0]], writes=[mean16[0]])
    p.add("vector", lambda v: v.tensor_tensor(
        out=msq16[1][:, :], in0=mean16[1][:, :], in1=mean16[1][:, :],
        op=ALU.mult), reads=[mean16[0]], writes=[msq16[0]])
    p.add("vector", lambda v: v.scalar_tensor_tensor(
        out=var16[1][:, :], in0=sqs16[1][:, :], scalar=float(1.0 / D),
        in1=msq16[1][:, :], op0=ALU.mult, op1=ALU.subtract),
        reads=[sqs16[0], msq16[0]], writes=[var16[0]])
    p.add("vector", lambda v: v.tensor_scalar(
        out=var16[1][:, :], in0=var16[1][:, :], scalar1=1e-5,
        scalar2=None, op0=ALU.add), reads=[var16[0]], writes=[var16[0]])
    emit_rstd_newton(p, None, var16, rstd16, nt1, nt2)
    p.add("vector", lambda v: v.tensor_tensor(
        out=negm16[1][:, :], in0=mean16[1][:, :], in1=rstd16[1][:, :],
        op=ALU.mult), reads=[mean16[0], rstd16[0]], writes=[negm16[0]])
    p.add("vector", lambda v: v.tensor_scalar(
        out=negm16[1][:, :], in0=negm16[1][:, :], scalar1=-1.0,
        scalar2=None, op0=ALU.mult), reads=[negm16[0]], writes=[negm16[0]])

    for i0 in range(0, NT, 2):
        items = []
        for i in (i0, i0 + 1):
            b = ph1[i % 2]
            p.add("scalar", lambda s, b=b, i=i: s.activation(
                out=b["xn"][1][:, :], in_=xfull[1][:, i * D:(i + 1) * D],
                func=AF.Identity, scale=rstd16[1][:, i:i + 1],
                bias=negm16[1][:, i:i + 1]),
                reads=[xfull[0], rstd16[0], negm16[0]], writes=[b["xn"][0]])
            items.append(dict(
                src_buf=b["xn"][0], src_ap=(lambda b=b: b["xn"][1].ap()[:, :]),
                w=D, scr=b, dst_buf=b["xq"][0],
                dst_ap=(lambda b=b: b["xq"][1].ap()[:, :])))
        emit_quant_multi(p, items, qthr, tag=f"p1q{i0}_")
        for i in (i0, i0 + 1):
            b = ph1[i % 2]
            p.add("vector", lambda v, b=b: v.tensor_copy(
                out=b["xnb"][1][:, :], in_=b["xn"][1][:, :]),
                reads=[b["xn"][0]], writes=[b["xnb"][0]], wfree=D)
            pbq = pbbf[i % 2]
            for dc in range(ND):
                p.add("tensor", lambda t, b=b, dc=dc, pbq=pbq: t.transpose(
                    pbq[1][:, dc * 128:(dc + 1) * 128],
                    b["xq"][1][:, dc * 128:(dc + 1) * 128], identb[1][:, :]),
                    reads=[b["xq"][0], identb[0]], writes=[pbq[0]])
            p.add("scalar", lambda s, i=i, pbq=pbq: s.activation(
                out=xqT[1].ap().rearrange("p (dc t) -> p dc t", t=L)[:, :, i * 128:(i + 1) * 128],
                in_=pbq[1].ap().rearrange("p (dc t) -> p dc t", t=128),
                func=AF.Copy), reads=[pbq[0]], writes=[xqT[0]])
            for dc in range(ND):
                p.add("tensor", lambda t, b=b, dc=dc, pbq=pbq: t.transpose(
                    pbq[1][:, dc * 128:(dc + 1) * 128],
                    b["xnb"][1][:, dc * 128:(dc + 1) * 128], identb[1][:, :]),
                    reads=[b["xnb"][0], identb[0]], writes=[pbq[0]])
            p.add("scalar", lambda s, b=b, pbq=pbq: s.activation(
                out=b["xntS"][1][:, :], in_=pbq[1][:, :], func=AF.Copy),
                reads=[pbq[0]], writes=[b["xntS"][0]])
            dma("gpsimd",
                dxnT.rearrange("p (dc i t) -> p dc i t", dc=ND, i=NT)[:, :, i, :],
                b["xntS"][1].ap().rearrange("p (dc t) -> p dc t", t=128),
                reads=[b["xntS"][0]], tag=f"p1sp{i}")

    p.barrier()
    if "xqT" in ddbg:
        dma("sync", ddbg["xqT"], xqT[1][:, :], reads=[xqT[0]], tag="dbg_xqT")
    if "xnT" in ddbg:
        dma("sync", ddbg["xnT"], dxnT, tag="dbg_xnT")
    free_phase()

    run2 = 2 in phases
    acc = alloc("accF", [128, NT * D], F32)
    wstg2 = [alloc(f"wstg{k}", [128, D], F32) for k in range(2)]
    wq_nat = alloc("wq_nat", [128, 12 * D], BF16)
    wgqT = alloc("wgqT", [128, ND * 512], BF16)
    wuqT = alloc("wuqT", [128, ND * 512], BF16)
    wdqT = alloc("wdqT", [128, 4 * D], BF16)
    f2 = quant_scratch(alloc, "f2", 512)
    g2 = [quant_scratch(alloc, f"g2{k}", 512) for k in range(2)]
    hb = []
    for par in range(2):
        hh = {}
        hh["hs"] = alloc(f"hs{par}", [128, 512], F32)
        hh["h"] = alloc(f"h{par}", [128, 512], F32)
        hh["hq"] = alloc(f"hq{par}", [128, 512], BF16)
        hh["hqT"] = alloc(f"hqT{par}", [128, 512], BF16)
        hb.append(hh)
    gstg = [alloc(f"gstg{par}", [128, D], BF16) for par in range(2)]

    NT_OFF = [0]
    for W in NTILE_W:
        NT_OFF.append(NT_OFF[-1] + W)

    def prep_tile(j):
        W = NTILE_W[j]
        CW = W // 128
        n0 = NT_OFF[j]
        load_fns = []
        witems = []
        for slot in range(3 * CW):
            kind, sl = divmod(slot, CW)
            stg = wstg2[slot % 2]

            def load(kind=kind, sl=sl, stg=stg, j=j):
                if kind < 2:
                    dsrc = dwg if kind == 0 else dwu
                    dma("gpsimd", stg[1][:, :],
                        dsrc.rearrange("(a p) k -> p a k", p=128)[:, (n0 // 128) + sl:(n0 // 128) + sl + 1, :].rearrange("p a k -> p (a k)"),
                        writes=[stg[0]], tag=f"w{kind}_{j}_{sl}")
                else:
                    ndc = D // W
                    dma("gpsimd",
                        stg[1].ap().rearrange("p (dc n) -> p dc n", n=W),
                        dwd.rearrange("(dc p) n -> p dc n", p=128)[:, sl * ndc:(sl + 1) * ndc, n0:n0 + W],
                        writes=[stg[0]], tag=f"wd_{j}_{sl}")

            load_fns.append(load)
            so = slot
            for hf in range(2):
                witems.append(dict(
                    src_buf=stg[0],
                    src_ap=(lambda stg=stg, hf=hf: stg[1].ap()[:, hf * 512:(hf + 1) * 512]),
                    w=512, scr=g2[hf], dst_buf=wq_nat[0],
                    dst_ap=(lambda so=so, hf=hf: wq_nat[1].ap()[:, so * D + hf * 512: so * D + (hf + 1) * 512])))
        for k in range(min(2, len(load_fns))):
            load_fns[k]()
        return load_fns, witems

    def trans_tile(j):
        W = NTILE_W[j]
        CW = W // 128
        for kind, dstT in ((0, wgqT), (1, wuqT)):
            for nc4 in range(CW):
                for kc in range(ND):
                    p.add("tensor", lambda t, kind=kind, nc4=nc4, kc=kc, CW=CW: t.transpose(
                        pbbf[0][1][:, kc * 128:(kc + 1) * 128],
                        wq_nat[1][:, (kind * CW + nc4) * D + kc * 128: (kind * CW + nc4) * D + (kc + 1) * 128],
                        identb[1][:, :]), reads=[wq_nat[0], identb[0]],
                        writes=[pbbf[0][0]])
                p.add("scalar", lambda s, dstT=dstT, nc4=nc4: s.activation(
                    out=dstT[1].ap().rearrange("p (kc n) -> p kc n", n=512)[:, :, nc4 * 128:(nc4 + 1) * 128],
                    in_=pbbf[0][1].ap().rearrange("p (kc t) -> p kc t", t=128),
                    func=AF.Copy), reads=[pbbf[0][0]], writes=[dstT[0]])
        dbase = 2 * CW * D
        for nc4 in range(CW):
            for dc in range(ND):
                p.add("tensor", lambda t, nc4=nc4, dc=dc, W=W, dbase=dbase: t.transpose(
                    pbbf[0][1][:, dc * 128:(dc + 1) * 128],
                    wq_nat[1][:, dbase + dc * W + nc4 * 128: dbase + dc * W + (nc4 + 1) * 128],
                    identb[1][:, :]), reads=[wq_nat[0], identb[0]],
                    writes=[pbbf[0][0]])
            p.add("scalar", lambda s, nc4=nc4: s.activation(
                out=wdqT[1][:, nc4 * D:(nc4 + 1) * D], in_=pbbf[0][1][:, :],
                func=AF.Copy), reads=[pbbf[0][0]], writes=[wdqT[0]])

    def t_loop(j, load_fns=(), witems=()):
        W = NTILE_W[j]
        CW = W // 128

        def emit_gateup(i):
            bg, bu = pbank[(i % 2) * 2], pbank[(i % 2) * 2 + 1]
            for kc in range(ND):
                p.add("tensor", lambda t, kc=kc, i=i, W=W, bg=bg: t.matmul(
                    bg[1][:, 0:W],
                    xqT[1][:, kc * L + i * 128: kc * L + (i + 1) * 128],
                    wgqT[1][:, kc * 512: kc * 512 + W],
                    start=(kc == 0), stop=(kc == ND - 1)),
                    reads=[xqT[0], wgqT[0]], writes=[bg[0]])
            for kc in range(ND):
                p.add("tensor", lambda t, kc=kc, i=i, W=W, bu=bu: t.matmul(
                    bu[1][:, 0:W],
                    xqT[1][:, kc * L + i * 128: kc * L + (i + 1) * 128],
                    wuqT[1][:, kc * 512: kc * 512 + W],
                    start=(kc == 0), stop=(kc == ND - 1)),
                    reads=[xqT[0], wuqT[0]], writes=[bu[0]])

        if not BISECT_INLINE:
            for k in range(0, len(witems), 2):
                emit_quant_multi(p, witems[k:k + 2], qthr, tag=f"wp{j}_{k}_")
                if k // 2 + 2 < len(load_fns):
                    load_fns[k // 2 + 2]()
        emit_gateup(0)
        for i in range(NT):
            hh = hb[i % 2]
            bg, bu = pbank[(i % 2) * 2], pbank[(i % 2) * 2 + 1]
            bd1, bd2 = pbank[4], pbank[5]
            p.add("scalar", lambda s, hh=hh, W=W, bg=bg: s.activation(
                out=hh["hs"][1][:, 0:W], in_=bg[1][:, 0:W], func=AF.Silu),
                reads=[bg[0]], writes=[hh["hs"][0]])
            p.add("vector", lambda v, hh=hh, W=W, bu=bu: v.tensor_tensor(
                out=hh["h"][1][:, 0:W], in0=hh["hs"][1][:, 0:W],
                in1=bu[1][:, 0:W], op=ALU.mult),
                reads=[hh["hs"][0], bu[0]], writes=[hh["h"][0]], wfree=W)
            if i + 1 < NT:
                emit_gateup(i + 1)
            items = [dict(
                src_buf=hh["h"][0],
                src_ap=(lambda hh=hh, W=W: hh["h"][1].ap()[:, 0:W]),
                w=W, scr=f2, dst_buf=hh["hq"][0],
                dst_ap=(lambda hh=hh, W=W: hh["hq"][1].ap()[:, 0:W]))]
            if BISECT_INLINE:
                nspread = 12
                k0 = (min(i, nspread) * len(witems)) // nspread
                k1 = (min(i + 1, nspread) * len(witems)) // nspread
                items += list(witems[k0:k1])
            emit_quant_multi(p, items, qthr, tag=f"hq{j}_{i}_")
            if BISECT_INLINE:
                for k in range(k0, k1):
                    sl = k // 2 + 2
                    if k % 2 == 1 and sl < len(load_fns):
                        load_fns[sl]()
            pbh = pbbf[1]
            for nc4 in range(CW):
                p.add("tensor", lambda t, hh=hh, nc4=nc4, pbh=pbh: t.transpose(
                    pbh[1][:, nc4 * 128:(nc4 + 1) * 128],
                    hh["hq"][1][:, nc4 * 128:(nc4 + 1) * 128],
                    identb[1][:, :]), reads=[hh["hq"][0], identb[0]],
                    writes=[pbh[0]])
            p.add("scalar", lambda s, hh=hh, CW=CW, pbh=pbh: s.activation(
                out=hh["hqT"][1][:, 0:CW * 128], in_=pbh[1][:, 0:CW * 128],
                func=AF.Copy), reads=[pbh[0]], writes=[hh["hqT"][0]])
            for half, bd in ((0, bd1), (1, bd2)):
                for nc4 in range(CW):
                    p.add("tensor", lambda t, hh=hh, nc4=nc4, half=half, bd=bd, CW=CW: t.matmul(
                        bd[1][:, 0:512],
                        hh["hqT"][1][:, nc4 * 128:(nc4 + 1) * 128],
                        wdqT[1][:, nc4 * D + half * 512: nc4 * D + (half + 1) * 512],
                        start=(nc4 == 0), stop=(nc4 == CW - 1)),
                        reads=[hh["hqT"][0], wdqT[0]], writes=[bd[0]])
            for half, bd in ((0, bd1), (1, bd2)):
                if j == 0:
                    p.add("scalar", lambda s, i=i, half=half, bd=bd: s.activation(
                        out=acc[1][:, i * D + half * 512: i * D + (half + 1) * 512],
                        in_=bd[1][:, 0:512], func=AF.Copy),
                        reads=[bd[0]], writes=[acc[0]])
                else:
                    p.add("vector", lambda v, i=i, half=half, bd=bd: v.tensor_tensor(
                        out=acc[1][:, i * D + half * 512: i * D + (half + 1) * 512],
                        in0=acc[1][:, i * D + half * 512: i * D + (half + 1) * 512],
                        in1=bd[1][:, 0:512], op=ALU.add),
                        reads=[acc[0], bd[0]], writes=[acc[0]], wfree=512)

    if run2:
        lf0, w0 = prep_tile(0)
        for k in range(0, len(w0), 2):
            emit_quant_multi(p, w0[k:k + 2], qthr, tag=f"wq0_{k}_")
            if k // 2 + 2 < len(lf0):
                lf0[k // 2 + 2]()
        trans_tile(0)
        for j in range(len(NTILE_W)):
            if j + 1 < len(NTILE_W):
                lfs, nxt = prep_tile(j + 1)
            else:
                lfs, nxt = (), ()
            t_loop(j, lfs, nxt)
            if j + 1 < len(NTILE_W):
                trans_tile(j + 1)

    for i in range(NT if run2 else 0):
        gs = gstg[i % 2]
        p.add("scalar", lambda s, i=i, gs=gs: s.activation(
            out=gs[1][:, :], in_=acc[1][:, i * D:(i + 1) * D], func=AF.Silu),
            reads=[acc[0]], writes=[gs[0]])
        dma("gpsimd", dgres[:, i * D:(i + 1) * D], gs[1][:, :],
            reads=[gs[0]], tag=f"gsp{i}")

    p.barrier()
    if "acc" in ddbg and run2:
        dma("sync", ddbg["acc"], acc[1][:, :], reads=[acc[0]], tag="dbg_acc")
        p.barrier()
    free_phase()
    for g in reversed(p12_ctx):
        g.__exit__(None, None, None)
    p12_ctx = []

    run3 = 3 in phases
    aout = alloc("aout", [128, ND * L], BF16, late=True)
    xnT = alloc("xnT", [128, ND * L], BF16)
    maskT = alloc("maskTS", [128, NH * C], F32)
    Gam = alloc("GamS", [128, NPAIR * C], F32)
    tri8 = alloc("tri8S", [128, 128], F32)
    gam16 = alloc("gam16S", [16, 1], F32)
    gam8 = alloc("gam8S", [128, NPAIR], F32)
    selg = alloc("selgS", [16, NPAIR * 128], F32)
    sel2 = alloc("sel2S", [16, 128], F32)
    m2t = alloc("m2S", [16, NPAIR], F32)
    r16 = alloc("r16S", [16, 128], F32)
    ri16 = alloc("ri16S", [16, 128], F32)
    ri2 = alloc("ri2S", [16, 128], F32)
    ri2T = alloc("ri2TS", [128, 16], F32)
    sck = alloc("sckS", [16, 1], F32)
    sck2 = alloc("sck2S", [16, 1], F32)
    tb = alloc("tbS", [16, 16], F32)
    lnG = alloc("lnGS", [16, 1], F32)
    trow = alloc("trowS", [16, 128], F32)
    tneg = alloc("tnegS", [16, 128], F32)
    t127 = alloc("t127S", [16, 128], F32)
    dma("sync", tri8[1][:, :], dtri8, writes=[tri8[0]], tag="c2")
    dma("sync", gam16[1][:, :], dgam16, writes=[gam16[0]], tag="c3")
    dma("sync", gam8[1][:, :], dgam8, writes=[gam8[0]], tag="c4")
    dma("sync", selg[1][:, :], dselg, writes=[selg[0]], tag="c4b")
    dma("sync", sel2[1][:, :], dsel2, writes=[sel2[0]], tag="c4c")
    dma("sync", m2t[1][:, :], dm2, writes=[m2t[0]], tag="c4d")
    dma("sync", trow[1][:, :], dtrow, writes=[trow[0]], tag="c4e")
    dma("sync", tneg[1][:, :], dtneg, writes=[tneg[0]], tag="c4f")
    dma("sync", t127[1][:, :], dt127, writes=[t127[0]], tag="c4g")

    p.add("scalar", lambda s: s.activation(
        out=lnG[1][:, :], in_=gam16[1][:, :], func=AF.Ln),
        reads=[gam16[0]], writes=[lnG[0]])
    p.add("vector", lambda v: v.tensor_tensor(
        out=r16[1][:, :], in0=trow[1][:, :],
        in1=lnG[1].ap()[:, 0:1].broadcast_to([16, 128]), op=ALU.mult),
        reads=[trow[0], lnG[0]], writes=[r16[0]])
    p.add("scalar", lambda s: s.activation(
        out=r16[1][:, :], in_=r16[1][:, :], func=AF.Exp),
        reads=[r16[0]], writes=[r16[0]])
    p.add("vector", lambda v: v.tensor_tensor(
        out=ri16[1][:, :], in0=tneg[1][:, :],
        in1=lnG[1].ap()[:, 0:1].broadcast_to([16, 128]), op=ALU.mult),
        reads=[tneg[0], lnG[0]], writes=[ri16[0]])
    p.add("scalar", lambda s: s.activation(
        out=ri16[1][:, :], in_=ri16[1][:, :], func=AF.Exp),
        reads=[ri16[0]], writes=[ri16[0]])
    p.add("vector", lambda v: v.tensor_tensor(
        out=ri2[1][:, :], in0=t127[1][:, :],
        in1=lnG[1].ap()[:, 0:1].broadcast_to([16, 128]), op=ALU.mult),
        reads=[t127[0], lnG[0]], writes=[ri2[0]])
    p.add("scalar", lambda s: s.activation(
        out=ri2[1][:, :], in_=ri2[1][:, :], func=AF.Exp),
        reads=[ri2[0]], writes=[ri2[0]])
    p.add("tensor", lambda t: t.transpose(
        pbank[0][1][:, 0:16], ri2[1][:, :], ident[1][0:16, 0:16]),
        reads=[ri2[0], ident[0]], writes=[pbank[0][0]])
    p.add("vector", lambda v: v.tensor_copy(out=ri2T[1][:, :], in_=pbank[0][1][:, 0:16]),
          reads=[pbank[0][0]], writes=[ri2T[0]])
    rflat = alloc("rflatS", [1, NH * 128], F32)
    riflat = alloc("riflatS", [1, NH * 128], F32)
    dma("sync", rflat[1].ap().rearrange("p (k t) -> p k t", k=16),
        r16[1].ap()[:, :], reads=[r16[0]], writes=[rflat[0]], tag="rf")
    dma("sync", riflat[1].ap().rearrange("p (k t) -> p k t", k=16),
        ri16[1].ap()[:, :], reads=[ri16[0]], writes=[riflat[0]], tag="rif")
    for h in range(NH):
        p.add("tensor", lambda t, h=h: t.matmul(
            pbank[0][1][:, 0:128], riflat[1][0:1, h * 128:(h + 1) * 128],
            rflat[1][0:1, h * 128:(h + 1) * 128], start=True, stop=True),
            reads=[riflat[0], rflat[0]], writes=[pbank[0][0]])
        p.add("vector", lambda v, h=h: v.tensor_tensor(
            out=maskT[1][:, h * 128:(h + 1) * 128], in0=pbank[0][1][:, 0:128],
            in1=tri8[1][:, :], op=ALU.mult),
            reads=[pbank[0][0], tri8[0]], writes=[maskT[0]])
    for hp in range(NPAIR):
        p.add("tensor", lambda t, hp=hp: t.matmul(
            pbank[1][1][:, 0:128], selg[1][:, hp * 128:(hp + 1) * 128],
            r16[1][:, :], start=True, stop=True),
            reads=[selg[0], r16[0]], writes=[pbank[1][0]])
        p.add("vector", lambda v, hp=hp: v.tensor_scalar(
            out=Gam[1][:, hp * 128:(hp + 1) * 128], in0=pbank[1][1][:, 0:128],
            scalar1=gam8[1][:, hp:hp + 1], scalar2=None, op0=ALU.mult),
            reads=[pbank[1][0], gam8[0]], writes=[Gam[0]])
    p.add("vector", lambda v: v.tensor_scalar(
        out=sck[1][:, :], in0=lnG[1][:, :], scalar1=128.0, scalar2=None,
        op0=ALU.mult), reads=[lnG[0]], writes=[sck[0]])
    p.add("scalar", lambda s: s.activation(
        out=sck2[1][:, :], in_=sck[1][:, :], func=AF.Exp),
        reads=[sck[0]], writes=[sck2[0]])
    p.add("vector", lambda v: v.tensor_tensor(
        out=tb[1][:, 0:NPAIR], in0=m2t[1][:, :],
        in1=sck2[1].ap()[:, 0:1].broadcast_to([16, NPAIR]), op=ALU.mult),
        reads=[m2t[0], sck2[0]], writes=[tb[0]])
    p.add("tensor", lambda t: t.matmul(
        pbank[0][1][:, 0:NPAIR], sel2[1][:, :], tb[1][:, 0:NPAIR],
        start=True, stop=True), reads=[sel2[0], tb[0]], writes=[pbank[0][0]])
    p.add("vector", lambda v: v.tensor_copy(out=gC[1][:, :], in_=pbank[0][1][:, 0:NPAIR]),
          reads=[pbank[0][0]], writes=[gC[0]])

    wqS = alloc("wqS", [128, ND * D], BF16)
    wkS = alloc("wkS", [128, ND * D], BF16)
    wvS = alloc("wvS", [128, ND * D], BF16)
    dma("sync", xnT[1][:, :], dxnT, writes=[xnT[0]], tag="xnT_rl")
    dma("gpsimd", wqS[1][:, :], dwq, writes=[wqS[0]], tag="wqS")
    dma("gpsimd", wkS[1][:, :], dwk, writes=[wkS[0]], tag="wkS")
    dma("gpsimd", wvS[1][:, :], dwv, writes=[wvS[0]], tag="wvS")

    QT = alloc("QT", [128, NPAIR * 512], BF16)
    Qg = alloc("Qg", [128, NPAIR * 512], BF16)
    KT = alloc("KT", [128, NPAIR * 512], BF16)
    Kd = alloc("Kd", [128, NPAIR * 512], BF16)
    Vs = alloc("Vs", [128, 4 * D], BF16)
    SA = [alloc("SA0", [128, 256], BF16), alloc("SA1", [128, 256], BF16)]
    Sst = [alloc(f"Sst{hp}", [128, 128], F32) for hp in range(NPAIR)]
    Sbf = [alloc(f"Sbf{hp}", [128, 128], BF16) for hp in range(NPAIR)]
    csb = alloc("csb", [128, 128], F32)
    sums = alloc("sums", [128, 2 * ND], F32)
    s1t = alloc("s1t", [128, 1], F32)
    s2t = alloc("s2t", [128, 1], F32)
    sqscr = alloc("sqscr", [128, 128], F32)
    reps = alloc("reps", [128, 16], F32)
    gnst = {k: alloc("gn" + k, [128, ND], F32)
            for k in ("mean", "ex2", "var", "rstd", "t1", "t2", "scalev", "biasv")}

    p.add("vector", lambda v: v.memset(sums[1][:, :], 0.0), writes=[sums[0]])
    for hp in range(NPAIR):
        p.add("vector", lambda v, hp=hp: v.memset(Sst[hp][1][:, :], 0.0),
              writes=[Sst[hp][0]])
        p.add("vector", lambda v, hp=hp: v.memset(Sbf[hp][1][:, :], 0.0),
              writes=[Sbf[hp][0]])

    qkvlvl = 5 if p3_qkv is True else int(p3_qkv)
    for g4 in range(p3_ngroups if run3 else 0):
        for hp in range(NPAIR if qkvlvl >= 1 else 0):
            bq = pbank[0]
            for kc in range(ND):
                p.add("tensor", lambda t, hp=hp, kc=kc, g4=g4: t.matmul(
                    bq[1][:, 0:512],
                    wqS[1][:, kc * D + hp * 128: kc * D + (hp + 1) * 128],
                    xnT[1][:, kc * L + g4 * 512: kc * L + (g4 + 1) * 512],
                    start=(kc == 0), stop=(kc == ND - 1)),
                    reads=[wqS[0], xnT[0]], writes=[bq[0]])
            p.add("scalar", lambda s, hp=hp: s.activation(
                out=QT[1][:, hp * 512:(hp + 1) * 512], in_=bq[1][:, 0:512],
                func=AF.Copy), reads=[bq[0]], writes=[QT[0]])
            if qkvlvl < 2:
                continue
            p.add("vector", lambda v, hp=hp: v.tensor_tensor(
                out=Qg[1].ap().rearrange("p (a t) -> p a t", t=128)[:, 4 * hp:4 * hp + 4, :],
                in0=bq[1].ap().rearrange("p (c t) -> p c t", t=128),
                in1=Gam[1].ap()[:, hp * 128:(hp + 1) * 128].unsqueeze(1).broadcast_to([128, 4, 128]),
                op=ALU.mult), reads=[bq[0], Gam[0]], writes=[Qg[0]])
            if qkvlvl < 3:
                continue
            bk = pbank[1]
            for kc in range(ND):
                p.add("tensor", lambda t, hp=hp, kc=kc, g4=g4: t.matmul(
                    bk[1][:, 0:512],
                    wkS[1][:, kc * D + hp * 128: kc * D + (hp + 1) * 128],
                    xnT[1][:, kc * L + g4 * 512: kc * L + (g4 + 1) * 512],
                    start=(kc == 0), stop=(kc == ND - 1)),
                    reads=[wkS[0], xnT[0]], writes=[bk[0]])
            p.add("scalar", lambda s, hp=hp: s.activation(
                out=KT[1][:, hp * 512:(hp + 1) * 512], in_=bk[1][:, 0:512],
                func=AF.Copy), reads=[bk[0]], writes=[KT[0]])
            if qkvlvl < 4:
                continue
            for c4 in range(4):
                p.add("tensor", lambda t, hp=hp, c4=c4: t.transpose(
                    pbbf[0][1][:, c4 * 128:(c4 + 1) * 128],
                    KT[1][:, hp * 512 + c4 * 128: hp * 512 + (c4 + 1) * 128],
                    identb[1][:, :]), reads=[KT[0], identb[0]], writes=[pbbf[0][0]])
            p.add("vector", lambda v, hp=hp: v.tensor_tensor(
                out=Kd[1].ap().rearrange("p (a h e) -> p a h e", h=2, e=64)[:, 4 * hp:4 * hp + 4, :, :],
                in0=pbbf[0][1].ap().rearrange("p (c h e) -> p c h e", h=2, e=64)[:, 0:4, :, :],
                in1=ri2T[1].ap()[:, 2 * hp:2 * hp + 2].unsqueeze(1).unsqueeze(3).broadcast_to([128, 4, 2, 64]),
                op=ALU.mult), reads=[pbbf[0][0], ri2T[0]], writes=[Kd[0]])
        for c4 in range(4 if qkvlvl >= 5 else 0):
            for half, bv in ((0, pbank[2]), (1, pbank[3])):
                for kc in range(ND):
                    p.add("tensor", lambda t, c4=c4, kc=kc, g4=g4, half=half, bv=bv: t.matmul(
                        bv[1][:, 0:512],
                        xnT[1][:, kc * L + g4 * 512 + c4 * 128: kc * L + g4 * 512 + (c4 + 1) * 128],
                        wvS[1][:, kc * D + half * 512: kc * D + (half + 1) * 512],
                        start=(kc == 0), stop=(kc == ND - 1)),
                        reads=[xnT[0], wvS[0]], writes=[bv[0]])
            p.add("scalar", lambda s, c4=c4: s.activation(
                out=Vs[1][:, c4 * D: c4 * D + 512], in_=pbank[2][1][:, 0:512],
                func=AF.Copy), reads=[pbank[2][0]], writes=[Vs[0]])
            p.add("scalar", lambda s, c4=c4: s.activation(
                out=Vs[1][:, c4 * D + 512: (c4 + 1) * D], in_=pbank[3][1][:, 0:512],
                func=AF.Copy), reads=[pbank[3][0]], writes=[Vs[0]])

        def emit_scores(hp, c4):
            bs = pbank[4 + (hp % 2)]
            p.add("tensor", lambda t, hp=hp, c4=c4, bs=bs: t.matmul(
                bs[1][:, 0:128],
                KT[1][0:64, hp * 512 + c4 * 128: hp * 512 + (c4 + 1) * 128],
                QT[1][0:64, hp * 512 + c4 * 128: hp * 512 + (c4 + 1) * 128],
                start=True, stop=True, tile_position=(0, 0)),
                reads=[KT[0], QT[0]], writes=[bs[0]])
            p.add("tensor", lambda t, hp=hp, c4=c4, bs=bs: t.matmul(
                bs[1][:, 128:256],
                KT[1][64:128, hp * 512 + c4 * 128: hp * 512 + (c4 + 1) * 128],
                QT[1][64:128, hp * 512 + c4 * 128: hp * 512 + (c4 + 1) * 128],
                start=True, stop=True, tile_position=(64, 0)),
                reads=[KT[0], QT[0]], writes=[bs[0]])

        for c4 in range(p3_nchunks):
            c = g4 * 4 + c4
            emit_scores(0, c4)
            for hp in range(NPAIR):
                bs = pbank[4 + (hp % 2)]
                sa = SA[hp % 2]
                p.add("vector", lambda v, hp=hp, bs=bs, sa=sa: v.tensor_tensor(
                    out=sa[1][:, 0:256], in0=bs[1][:, 0:256],
                    in1=maskT[1][:, (2 * hp) * 128: (2 * hp + 2) * 128],
                    op=ALU.mult), reads=[bs[0], maskT[0]], writes=[sa[0]],
                    wfree=256)
                if hp + 1 < NPAIR:
                    emit_scores(hp + 1, c4)
                bo = pbank[0]
                p.add("tensor", lambda t, hp=hp, c4=c4, bo=bo: t.matmul(
                    bo[1][:, 0:128], Sbf[hp][1][:, :],
                    Qg[1][:, hp * 512 + c4 * 128: hp * 512 + (c4 + 1) * 128],
                    start=True, stop=True),
                    reads=[Sbf[hp][0], Qg[0]], writes=[bo[0]])
                bst = pbank[2]
                p.add("tensor", lambda t, hp=hp, c4=c4, bst=bst: t.matmul(
                    bst[1][0:64, 0:64],
                    Kd[1][:, hp * 512 + c4 * 128: hp * 512 + c4 * 128 + 64],
                    Vs[1][:, c4 * D + hp * 128: c4 * D + hp * 128 + 64],
                    start=True, stop=True, tile_position=(0, 0)),
                    reads=[Kd[0], Vs[0]], writes=[bst[0]])
                p.add("tensor", lambda t, hp=hp, c4=c4, bst=bst: t.matmul(
                    bst[1][64:128, 64:128],
                    Kd[1][:, hp * 512 + c4 * 128 + 64: hp * 512 + (c4 + 1) * 128],
                    Vs[1][:, c4 * D + hp * 128 + 64: c4 * D + (hp + 1) * 128],
                    start=True, stop=True, tile_position=(0, 64)),
                    reads=[Kd[0], Vs[0]], writes=[bst[0]])
                bi = pbank[1]
                p.add("tensor", lambda t, hp=hp, c4=c4, bi=bi, sa=sa: t.matmul(
                    bi[1][0:64, 0:128],
                    Vs[1][:, c4 * D + hp * 128: c4 * D + hp * 128 + 64],
                    sa[1][:, 0:128],
                    start=True, stop=True, tile_position=(0, 0)),
                    reads=[Vs[0], sa[0]], writes=[bi[0]])
                p.add("tensor", lambda t, hp=hp, c4=c4, bi=bi, sa=sa: t.matmul(
                    bi[1][64:128, 0:128],
                    Vs[1][:, c4 * D + hp * 128 + 64: c4 * D + (hp + 1) * 128],
                    sa[1][:, 128:256],
                    start=True, stop=True, tile_position=(0, 64)),
                    reads=[Vs[0], sa[0]], writes=[bi[0]])
                p.add("scalar", lambda s, bo=bo: s.activation(
                    out=csb[1][:, :], in_=bo[1][:, 0:128], func=AF.Copy),
                    reads=[bo[0]], writes=[csb[0]])
                p.add("vector", lambda v, bi=bi: v.tensor_tensor(
                    out=sqscr[1][:, :], in0=bi[1][0:128, 0:128], in1=csb[1][:, :],
                    op=ALU.add), reads=[bi[0], csb[0]], writes=[sqscr[0]],
                    wfree=128)
                p.add("vector", lambda v, hp=hp, c=c: v.tensor_scalar(
                    out=aout[1][:, hp * L + c * 128: hp * L + (c + 1) * 128],
                    in0=sqscr[1][:, :], scalar1=1.0, scalar2=0.0, op0=ALU.mult,
                    op1=ALU.add, accum_out=s1t[1][:, 0:1]),
                    reads=[sqscr[0]], writes=[aout[0], s1t[0]], hazard=True,
                    wfree=128)
                p.add("scalar", lambda s, hp=hp: s.activation(
                    out=sqscr[1][:, :], in_=sqscr[1][:, :], func=AF.Square,
                    accum_out=s2t[1][:, 0:1]),
                    reads=[sqscr[0]], writes=[sqscr[0], s2t[0]], hazard=True)
                p.add("vector", lambda v, hp=hp: v.tensor_tensor(
                    out=sums[1][:, hp:hp + 1], in0=sums[1][:, hp:hp + 1],
                    in1=s1t[1][:, :], op=ALU.add),
                    reads=[s1t[0], sums[0]], writes=[sums[0]])
                p.add("vector", lambda v, hp=hp: v.tensor_tensor(
                    out=sums[1][:, ND + hp: ND + hp + 1],
                    in0=sums[1][:, ND + hp: ND + hp + 1],
                    in1=s2t[1][:, :], op=ALU.add),
                    reads=[s2t[0], sums[0]], writes=[sums[0]])
                p.add("vector", lambda v, hp=hp, bst=bst: v.scalar_tensor_tensor(
                    out=Sst[hp][1][0:64, 0:64], in0=Sst[hp][1][0:64, 0:64],
                    scalar=gC[1][0:64, hp:hp + 1], in1=bst[1][0:64, 0:64],
                    op0=ALU.mult, op1=ALU.add),
                    reads=[Sst[hp][0], gC[0], bst[0]], writes=[Sst[hp][0]])
                p.add("vector", lambda v, hp=hp, bst=bst: v.scalar_tensor_tensor(
                    out=Sst[hp][1][64:128, 64:128], in0=Sst[hp][1][64:128, 64:128],
                    scalar=gC[1][64:128, hp:hp + 1], in1=bst[1][64:128, 64:128],
                    op0=ALU.mult, op1=ALU.add),
                    reads=[Sst[hp][0], gC[0], bst[0]], writes=[Sst[hp][0]])
                p.add("vector", lambda v, hp=hp: v.tensor_copy(
                    out=Sbf[hp][1][:, :], in_=Sst[hp][1][:, :]),
                    reads=[Sst[hp][0]], writes=[Sbf[hp][0]])

    run3n = run3 and p3_norm
    if not run3:
        p.add("vector", lambda v: v.memset(aout[1][:, :], 0.0), writes=[aout[0]])
    if run3n or not run3:
      p.add("tensor", lambda t: t.matmul(
        pbank[0][1][:, 0:16], HHs[1][:, :], sums[1][:, :], start=True, stop=True),
        reads=[HHs[0], sums[0]], writes=[pbank[0][0]])
      p.add("vector", lambda v: v.tensor_copy(out=reps[1][:, :], in_=pbank[0][1][:, 0:16]),
          reads=[pbank[0][0]], writes=[reps[0]])
    NEL = float(64 * L)
    if run3n or not run3:
      p.add("vector", lambda v: v.tensor_scalar(
        out=gnst["mean"][1][:, :], in0=reps[1][:, 0:ND], scalar1=1.0 / NEL,
        scalar2=None, op0=ALU.mult), reads=[reps[0]], writes=[gnst["mean"][0]])
      p.add("vector", lambda v: v.tensor_scalar(
        out=gnst["ex2"][1][:, :], in0=reps[1][:, ND:2 * ND], scalar1=1.0 / NEL,
        scalar2=None, op0=ALU.mult), reads=[reps[0]], writes=[gnst["ex2"][0]])
      p.add("vector", lambda v: v.tensor_tensor(
        out=gnst["t1"][1][:, :], in0=gnst["mean"][1][:, :], in1=gnst["mean"][1][:, :],
        op=ALU.mult), reads=[gnst["mean"][0]], writes=[gnst["t1"][0]])
      p.add("vector", lambda v: v.tensor_tensor(
        out=gnst["var"][1][:, :], in0=gnst["ex2"][1][:, :], in1=gnst["t1"][1][:, :],
        op=ALU.subtract), reads=[gnst["ex2"][0], gnst["t1"][0]], writes=[gnst["var"][0]])
      p.add("vector", lambda v: v.tensor_scalar(
        out=gnst["var"][1][:, :], in0=gnst["var"][1][:, :], scalar1=1e-5,
        scalar2=None, op0=ALU.add), reads=[gnst["var"][0]], writes=[gnst["var"][0]])
      emit_rstd_newton(p, None, gnst["var"], gnst["rstd"], gnst["t1"], gnst["t2"])
      p.add("vector", lambda v: v.tensor_tensor(
        out=gnst["scalev"][1][:, :], in0=gnst["rstd"][1][:, :], in1=gnw[1][:, :],
        op=ALU.mult), reads=[gnst["rstd"][0], gnw[0]], writes=[gnst["scalev"][0]])
      p.add("vector", lambda v: v.tensor_tensor(
        out=gnst["t1"][1][:, :], in0=gnst["mean"][1][:, :], in1=gnst["scalev"][1][:, :],
        op=ALU.mult), reads=[gnst["mean"][0], gnst["scalev"][0]], writes=[gnst["t1"][0]])
      p.add("vector", lambda v: v.tensor_tensor(
        out=gnst["biasv"][1][:, :], in0=gnb[1][:, :], in1=gnst["t1"][1][:, :],
        op=ALU.subtract), reads=[gnb[0], gnst["t1"][0]], writes=[gnst["biasv"][0]])
      for dc in range(ND):
        p.add("scalar", lambda s, dc=dc: s.activation(
            out=aout[1][:, dc * L:(dc + 1) * L], in_=aout[1][:, dc * L:(dc + 1) * L],
            func=AF.Identity, scale=gnst["scalev"][1][:, dc:dc + 1],
            bias=gnst["biasv"][1][:, dc:dc + 1]),
            reads=[aout[0], gnst["scalev"][0], gnst["biasv"][0]], writes=[aout[0]])

    p.barrier()
    if "aout" in ddbg:
        dma("sync", ddbg["aout"], aout[1][:, :], reads=[aout[0]], tag="dbg_aout")
        p.barrier()
    free_phase()

    run4 = 4 in phases
    woS = alloc("woS", [128, ND * D], BF16)
    dma("gpsimd", woS[1][:, :], dwo, writes=[woS[0]], tag="woS")
    p4 = []
    for par in range(4):
        b = {}
        b["xc"] = alloc(f"p4x{par}", [128, D], F32)
        b["gt"] = alloc(f"p4g{par}", [128, D], BF16)
        b["sg1"] = alloc(f"p4s{par}", [128, D], F32)
        b["yt"] = alloc(f"p4y{par}", [128, D], F32)
        p4.append(b)

    def p4_load(i):
        b = p4[i % 4]
        dma("sync", b["xc"][1][:, :], dx[i * 128:(i + 1) * 128, :],
            writes=[b["xc"][0]], tag=f"p4x{i}")
        dma("sync", b["gt"][1][:, :], dgres[:, i * D:(i + 1) * D],
            writes=[b["gt"][0]], tag=f"p4g{i}")

    for i in range(4 if run4 else 0):
        p4_load(i)
    for i in range(NT if run4 else 0):
        b = p4[i % 4]
        bw1, bw2 = pbank[(i % 2) * 2], pbank[(i % 2) * 2 + 1]
        for half, bw in ((0, bw1), (1, bw2)):
            for dc in range(ND):
                p.add("tensor", lambda t, i=i, dc=dc, half=half, bw=bw: t.matmul(
                    bw[1][:, 0:512],
                    aout[1][:, dc * L + i * 128: dc * L + (i + 1) * 128],
                    woS[1][:, dc * D + half * 512: dc * D + (half + 1) * 512],
                    start=(dc == 0), stop=(dc == ND - 1)),
                    reads=[aout[0], woS[0]], writes=[bw[0]])
        for half, bw in ((0, bw1), (1, bw2)):
            p.add("vector", lambda v, b=b, i=i, half=half, bw=bw: v.tensor_tensor(
                out=b["sg1"][1][:, half * 512:(half + 1) * 512],
                in0=bw[1][:, 0:512],
                in1=b["gt"][1][:, half * 512:(half + 1) * 512],
                op=ALU.mult), reads=[bw[0], b["gt"][0]], writes=[b["sg1"][0]])
        p.add("vector", lambda v, b=b: v.tensor_tensor(
            out=b["yt"][1][:, :], in0=b["sg1"][1][:, :], in1=b["xc"][1][:, :],
            op=ALU.add), reads=[b["sg1"][0], b["xc"][0]], writes=[b["yt"][0]])
        dma("sync", dy[i * 128:(i + 1) * 128, :], b["yt"][1][:, :],
            reads=[b["yt"][0]], tag=f"p4y{i}")
        if i + 4 < NT:
            p4_load(i + 4)

    p.barrier()

    with nc.Block() as block:
        sem_ctx = p.lower(nc, block)

    for g in reversed(sem_ctx):
        g.__exit__(None, None, None)
    free_phase()
    for g in reversed(late_ctx):
        g.__exit__(None, None, None)
    for g in reversed(perm_ctx):
        g.__exit__(None, None, None)
    return nc


def _prep_inputs(inputs):
    x = np.ascontiguousarray(np.asarray(inputs["L0_attn_x"], dtype=np.float32))
    Wq = np.asarray(inputs["L0_attn_Wq"], dtype=np.float32)
    Wk = np.asarray(inputs["L0_attn_Wk"], dtype=np.float32)
    Wv = np.asarray(inputs["L0_attn_Wv"], dtype=np.float32)
    Wout = np.asarray(inputs["L0_attn_Wout"], dtype=np.float32)
    gnwv = np.asarray(inputs["L0_attn_gn_weight"], dtype=np.float32)
    gnbv = np.asarray(inputs["L0_attn_gn_bias"], dtype=np.float32)
    wg = np.ascontiguousarray(np.asarray(inputs["L0_ffn_w_gate"], dtype=np.float32))
    wu = np.ascontiguousarray(np.asarray(inputs["L0_ffn_w_up"], dtype=np.float32))
    wd = np.ascontiguousarray(np.asarray(inputs["L0_ffn_w_down"], dtype=np.float32))

    def he(W):
        a = W.transpose(1, 0, 2).reshape(D, D)
        return np.ascontiguousarray(
            a.reshape(ND, 128, D).transpose(1, 0, 2).reshape(128, ND * D))

    def rows(Wm):
        N = Wm.shape[1]
        return np.ascontiguousarray(
            Wm.reshape(ND, 128, N).transpose(1, 0, 2).reshape(128, ND * N))

    consts = host_consts()
    shared = {
        "wq_r": he(Wq), "wk_r": he(Wk), "wv_r": he(Wv), "wout_r": rows(Wout),
        "w_gate": wg, "w_up": wu, "w_down": wd,
        "gnw_r": np.ascontiguousarray(gnwv.reshape(ND, 128).T),
        "gnb_r": np.ascontiguousarray(gnbv.reshape(ND, 128).T),
        "ident": consts["ident"], "ident_bf": consts["ident_bf"],
        "tri8": consts["tri8"], "gam16": consts["gam16"],
        "gam8": consts["gam8"], "selg": consts["selg"],
        "sel2": consts["sel2"], "m2": consts["m2"], "HH": consts["HH"],
        "trow": consts["trow"], "tneg": consts["tneg"], "t127": consts["t127"],
        "qthr": consts["qthr"],
    }
    in_maps = []
    for b in range(8):
        m = dict(shared)
        m["x"] = np.ascontiguousarray(x[b])
        in_maps.append(m)
    return in_maps


def kernel(**inputs):
    if "nc" not in _CACHE:
        _CACHE["nc"] = build_nc()
    nc = _CACHE["nc"]
    in_maps = _prep_inputs(inputs)
    res = run_bass_kernel_spmd(nc, in_maps, core_ids=list(range(8)))
    return np.stack([np.asarray(res.results[b]["y"]) for b in range(8)], axis=0)
